# version 08
# baseline (speedup 1.0000x reference)
import numpy as np
import concourse.bass as bass
import concourse.mybir as mybir
from concourse.bass_utils import run_bass_kernel_spmd

F32 = mybir.dt.float32
BF16 = mybir.dt.bfloat16
I8 = mybir.dt.int8
AF = mybir.ActivationFunctionType
ALU = mybir.AluOpType
AX = mybir.AxisListType

NCORES = 8
T = 8192
C = 2048
H = 16
D = 128
TPC = T // NCORES
NCH = TPC // 128
INTER = 8192
EPS = 1e-5
RD = float(D) ** -0.5

_DEBUG_TAPS = []

KBf = 256


ENGS = ("pe", "dv", "ac", "sy", "gp")


class _Op:
    __slots__ = ("method", "args", "kwargs", "waits", "incs")

    def __init__(self, method, args, kwargs):
        self.method = method
        self.args = args
        self.kwargs = kwargs
        self.waits = {}
        self.incs = []


class _OpRef:
    def __init__(self, s, op):
        self.s = s
        self.op = op

    def after(self, *keys):
        for k in keys:
            if k is None:
                continue
            if isinstance(k, list):
                self.after(*k)
                continue
            sem, v = self.s.tick[k]
            if self.op.waits.get(sem, -1) < v:
                self.op.waits[sem] = v
        return self

    def inc(self, sem, key=None, dma=False):
        step = 16 if dma else 1
        self.s.cnt[sem] = self.s.cnt.get(sem, 0) + step
        v = self.s.cnt[sem]
        self.op.incs.append((sem, step))
        if key is not None:
            keys = key if isinstance(key, list) else [key]
            for k in keys:
                assert k not in self.s.tick, f"dup key {k}"
                self.s.tick[k] = (sem, v)
        return self


class _EngProxy:
    def __init__(self, s, eng):
        self._s = s
        self._e = eng

    def __getattr__(self, m):
        def call(*a, **kw):
            op = _Op(m, a, kw)
            self._s.q[self._e].append(op)
            return _OpRef(self._s, op)
        return call


class Sched:
    def __init__(self):
        self.q = {e: [] for e in ENGS}
        self.cnt = {}
        self.tick = {}
        self.nkey = 0
        for e in ENGS:
            setattr(self, e, _EngProxy(self, e))

    def key(self, tag=""):
        self.nkey += 1
        return (tag, self.nkey)

    def dma(self, eng, slot, out, in_, key=None, after=(), **kw):
        r = getattr(self, eng).dma_start(out=out, in_=in_, **kw)
        r.after(*after)
        r.inc("D" + slot, key=key, dma=True)
        return r


class Buf:

    def __init__(self):
        self.writer = None
        self.readers = []

    def wr(self, key):
        aft = list(self.readers) if self.readers else (
            [self.writer] if self.writer is not None else [])
        self.writer = key
        self.readers = []
        return aft

    def rd(self, key):
        self.readers.append(key)
        return [self.writer] if self.writer is not None else []


def _gammas():
    return np.array([1.0 - 2.0 ** (-5.0 - h) for h in range(H)], dtype=np.float64)


def _host_shared(wqkv, wg, wp, wgate, wup, wdown, gnw, gnb, sg, su, sd):
    g = _gammas()
    sh = {}
    sh["wqkv_img"] = np.ascontiguousarray(
        np.asarray(wqkv, np.float32).reshape(16, 128, 12, 512)
        .transpose(2, 1, 0, 3).reshape(12, 128, 8192))
    sh["wg_img"] = np.ascontiguousarray(
        np.asarray(wg, np.float32).reshape(16, 128, 4, 512)
        .transpose(2, 1, 0, 3).reshape(4, 128, 8192))
    sh["wp_img"] = np.ascontiguousarray(
        np.asarray(wp, np.float32).reshape(16, 128, 4, 512)
        .transpose(2, 1, 0, 3).reshape(4, 128, 8192))
    sh["wgate_img"] = np.ascontiguousarray(
        np.asarray(wgate).reshape(16, 128, 64, 128)
        .transpose(2, 1, 0, 3).reshape(64, 128, 2048))
    sh["wup_img"] = np.ascontiguousarray(
        np.asarray(wup).reshape(16, 128, 64, 128)
        .transpose(2, 1, 0, 3).reshape(64, 128, 2048))
    sh["wdown_img"] = np.ascontiguousarray(
        np.asarray(wdown).reshape(4, 16, 128, 4, 512)
        .transpose(0, 3, 2, 1, 4).reshape(16, 128, 8192))
    j = np.arange(128, dtype=np.float64)
    maskT = np.zeros((128, H * 128), dtype=np.float64)
    for h in range(H):
        e = j[None, :] - j[:, None]
        maskT[:, h * 128:(h + 1) * 128] = np.where(e >= 0, g[h] ** np.maximum(e, 0), 0.0)
    sh["maskT"] = maskT.astype(np.float32)
    qd = np.concatenate([g[h] ** (j + 1.0) for h in range(H)])
    sh["qdecb"] = np.broadcast_to(qd.astype(np.float32)[None, :], (128, H * 128)).copy()
    gh = np.zeros((128, H * 128), dtype=np.float64)
    for h in range(H):
        gh[:, h * 128:(h + 1) * 128] = (g[h] ** (127.0 - j))[:, None]
    sh["ghat"] = gh.astype(np.float32)
    pdec = np.zeros((NCH, 128, H * 128), dtype=np.float32)
    for c8 in range(NCH):
        row = np.concatenate([np.full(128, g[h] ** (128.0 * c8)) for h in range(H)])
        pdec[c8] = row.astype(np.float32)[None, :]
    sh["pdec"] = pdec
    P = np.zeros((128, 128), dtype=np.float32)
    ii = np.arange(0, 128, 2)
    P[ii + 1, ii] = -1.0
    P[ii, ii + 1] = 1.0
    sh["pmat"] = P
    sh["gnw_b"] = np.broadcast_to(np.asarray(gnw, np.float32).reshape(1, C), (128, C)).copy()
    sh["gnb_b"] = np.broadcast_to(np.asarray(gnb, np.float32).reshape(1, C), (128, C)).copy()
    sh["sdown_b"] = np.broadcast_to(np.asarray(sd, np.float32).reshape(1, C), (128, C)).copy()
    sh["sg_c"] = np.ascontiguousarray(np.asarray(sg, np.float32).reshape(64, 128).T)
    sh["su_c"] = np.ascontiguousarray(np.asarray(su, np.float32).reshape(64, 128).T)
    return sh


def _host_xpos():
    i = np.arange(C // 2, dtype=np.float64)
    zeta = (2.0 * i + 0.4 * C) / (1.4 * C)
    invf = 1.0 / (10000.0 ** (2.0 * i / C))
    t = np.arange(T, dtype=np.float64)
    scale = zeta[None, :] ** (t[:, None] / 512.0)
    ang = t[:, None] * invf[None, :]
    cos = np.repeat(np.cos(ang) * scale, 2, axis=1).astype(np.float32)
    sin = np.repeat(np.sin(ang) * scale, 2, axis=1).astype(np.float32)
    return cos, sin


def _host_percore(core, x_full, cos, sin):
    g = _gammas()
    pc = {}
    pc["x"] = np.ascontiguousarray(x_full[core * TPC:(core + 1) * TPC])
    cs = np.empty((16, 128, 2048), dtype=np.float32)
    csl = cos[core * TPC:(core + 1) * TPC]
    ssl = sin[core * TPC:(core + 1) * TPC]
    for fc in range(16):
        cs[fc, :, 0:1024] = csl[:, fc * 128:(fc + 1) * 128].T
        cs[fc, :, 1024:2048] = ssl[:, fc * 128:(fc + 1) * 128].T
    pc["cs_img"] = cs
    wt = np.zeros((NCORES, 128, H * 128), dtype=np.float32)
    for b in range(core):
        row = np.concatenate(
            [np.full(128, g[h] ** (1024.0 * (core - 1 - b))) for h in range(H)])
        wt[b] = row.astype(np.float32)[None, :]
    pc["wtab"] = wt
    return pc


def build_nc(taps=(), stop_after="ffn"):
    PH = ["a1", "a2p0", "a2p1", "a2p2", "a2p3", "a2p", "a2c0", "a2", "p5", "post", "ffn"]
    lim = PH.index(stop_after)

    def ph(name):
        return PH.index(name) <= lim

    nc = bass.Bass("TRN2", target_bir_lowering=False)
    g64 = _gammas()
    g128 = [float(np.float32(g64[h] ** 128.0)) for h in range(H)]

    x = nc.dram_tensor("x", [TPC, C], F32, kind="ExternalInput").ap()
    out = nc.dram_tensor("out", [TPC, C], F32, kind="ExternalOutput").ap()
    wqkv_img = nc.dram_tensor("wqkv_img", [12, 128, 8192], F32, kind="ExternalInput").ap()
    wg_img = nc.dram_tensor("wg_img", [4, 128, 8192], F32, kind="ExternalInput").ap()
    wp_img = nc.dram_tensor("wp_img", [4, 128, 8192], F32, kind="ExternalInput").ap()
    wgate_img = nc.dram_tensor("wgate_img", [64, 128, 2048], I8, kind="ExternalInput").ap()
    wup_img = nc.dram_tensor("wup_img", [64, 128, 2048], I8, kind="ExternalInput").ap()
    wdown_img = nc.dram_tensor("wdown_img", [16, 128, 8192], I8, kind="ExternalInput").ap()
    maskT_d = nc.dram_tensor("maskT", [128, 2048], F32, kind="ExternalInput").ap()
    qdecb_d = nc.dram_tensor("qdecb", [128, 2048], F32, kind="ExternalInput").ap()
    ghat_d = nc.dram_tensor("ghat", [128, 2048], F32, kind="ExternalInput").ap()
    pdec_d = nc.dram_tensor("pdec", [NCH, 128, 2048], F32, kind="ExternalInput").ap()
    pmat_d = nc.dram_tensor("pmat", [128, 128], F32, kind="ExternalInput").ap()
    gnw_d = nc.dram_tensor("gnw_b", [128, 2048], F32, kind="ExternalInput").ap()
    gnb_d = nc.dram_tensor("gnb_b", [128, 2048], F32, kind="ExternalInput").ap()
    sdown_d = nc.dram_tensor("sdown_b", [128, 2048], F32, kind="ExternalInput").ap()
    sg_d = nc.dram_tensor("sg_c", [128, 64], F32, kind="ExternalInput").ap()
    su_d = nc.dram_tensor("su_c", [128, 64], F32, kind="ExternalInput").ap()
    cs_d = nc.dram_tensor("cs_img", [16, 128, 2048], F32, kind="ExternalInput").ap()
    wtab_d = nc.dram_tensor("wtab", [NCORES, 128, 2048], F32, kind="ExternalInput").ap()

    d_ln = nc.dram_tensor("d_ln", [TPC, C], BF16)
    d_q = nc.dram_tensor("d_q", [C, TPC], BF16)
    d_k = nc.dram_tensor("d_k", [C, TPC], BF16)
    d_v = nc.dram_tensor("d_v", [C, TPC], BF16)
    d_g = nc.dram_tensor("d_g", [TPC, C], BF16)
    d_x2 = nc.dram_tensor("d_x2", [TPC, C], F32)
    d_ln2 = nc.dram_tensor("d_ln2", [TPC, C], BF16)
    d_wgp = nc.dram_tensor("d_wgp", [8, 128, 8192], BF16)
    cc_in = nc.dram_tensor("cc_in", [H * 128, 128], F32)
    cc_out = nc.dram_tensor("cc_out", [NCORES * H * 128, 128], F32)

    tap_outs = {}
    for tname in taps:
        src = {"d_ln": d_ln, "d_q": d_q, "d_k": d_k, "d_v": d_v, "d_g": d_g,
               "d_x2": d_x2, "d_ln2": d_ln2, "cc_in": cc_in,
               "cc_out": cc_out}.get(tname)
        if src is not None:
            tap_outs[tname] = (
                nc.dram_tensor("tap_" + tname, list(src.shape), src.dtype,
                               kind="ExternalOutput").ap(), src.ap())

    ARENA_KB = 204
    ar = nc.alloc_sbuf_tensor("arena", [128, ARENA_KB * KBf], F32)
    arb = ar.bitcast(BF16)

    def f32(kb, n):
        o = int(kb * KBf)
        return ar[:, o:o + n]

    def bf(kb, n):
        o = int(kb * KBf) * 2
        return arb[:, o:o + n]

    lnf = lambda cc: bf(0, 16 * 1024)[:, cc * 1024:(cc + 1) * 1024]
    ghat_s = bf(120, 2048)
    ZmS = [bf(116, 512), bf(116.5, 512)]
    SbfS = [bf(117, 512), bf(117.5, 512)]
    lnh = lambda fc: bf(0, 16 * 512)[:, fc * 512:(fc + 1) * 512]
    sgh = lambda tc: bf(16, 4 * 2048)[:, tc * 2048:(tc + 1) * 2048]
    ln2f = lambda cc: bf(0, 16 * 1024)[:, cc * 1024:(cc + 1) * 1024]
    qch = [bf(32, 2048), bf(36, 2048)]
    kch = [bf(40, 2048), bf(44, 2048)]
    ktch = [bf(48, 2048), bf(52, 2048)]
    vch = [bf(56, 2048), bf(60, 2048)]
    pdec_s = f32(40, 2048)
    gfh = lambda fc: bf(48, 16 * 512)[:, fc * 512:(fc + 1) * 512]
    fuse = lambda oc: bf(32, 16 * 1024)[:, oc * 1024:(oc + 1) * 1024]
    xs = [f32(64, 2048), f32(72, 2048)]
    cs_s = [f32(80, 2048), f32(88, 2048)]
    gall = lambda b: f32(64, 8 * 2048)[:, b * 2048:(b + 1) * 2048]
    gnw_s = f32(64, 2048)
    gnb_s = f32(72, 2048)
    x0s = [f32(80, 512), f32(82, 512)]
    x2o = [f32(84, 512), f32(86, 512)]
    yn_s = f32(88, 2048)
    acc = lambda t8: f32(64, 8 * 2048)[:, t8 * 2048:(t8 + 1) * 2048]
    wqb = [bf(128, 8192), bf(144, 8192)]
    ypart = lambda c8: bf(128, 8 * 2048)[:, c8 * 2048:(c8 + 1) * 2048]
    wga = [bf(128, 2048), bf(132, 2048)]
    wua = [bf(136, 2048), bf(140, 2048)]
    wdns = [bf(144, 8192), bf(160, 8192)]
    x2f = [f32(128, 2048), f32(152, 2048)]
    fin = [f32(136, 2048), f32(144, 2048)]
    maskT_s = bf(160, 2048)
    qdecb_s = bf(164, 2048)
    wpq = [bf(168, 2048), bf(172, 2048)]
    SP = f32(176, 2048)
    sdown_s = f32(176, 2048)
    PT = bf(184, 2048)
    wtb_s = f32(184, 2048)
    Qhat = bf(188, 2048)
    Tb = bf(192, 2048)
    TbA1 = [bf(192, 512), bf(193, 512)]
    Qp = [bf(194, 512), bf(195, 512)]
    gss = [bf(192, 512), bf(193, 512)]
    T1 = f32(196, 1024)
    st_s1 = f32(200, 64)
    st_s2 = f32(200.25, 64)
    st_mu = f32(200.5, 64)
    st_msq = f32(200.75, 64)
    st_var = f32(201.0, 64)
    st_sq = f32(201.25, 64)
    st_rstd = f32(201.5, 64)
    sgc_s = f32(201.75, 64)
    suc_s = f32(202.0, 64)
    pmat_s = bf(202.25, 128)

    PS = [nc.alloc_psum_tensor(f"ps{i}", [128, 512], F32) for i in range(8)]

    s = Sched()
    K = s.key

    psb = [Buf() for _ in range(8)]

    x_t = x.rearrange("(n p) c -> n p c", p=128)
    out_t = out.rearrange("(n p) c -> n p c", p=128)
    dln_t = d_ln.ap().rearrange("(n p) c -> n p c", p=128)
    dg_t = d_g.ap().rearrange("(n p) c -> n p c", p=128)
    dx2_t = d_x2.ap().rearrange("(n p) c -> n p c", p=128)
    dln2_t = d_ln2.ap().rearrange("(n p) c -> n p c", p=128)
    dq_rows = d_q.ap().rearrange("(n p) t -> n p t", p=128)
    dk_rows = d_k.ap().rearrange("(n p) t -> n p t", p=128)
    dv_rows = d_v.ap().rearrange("(n p) t -> n p t", p=128)

    def dq_col(dram, c8):
        return dram.ap().rearrange("(h d) t -> d h t", d=128)[:, :, c8 * 128:(c8 + 1) * 128]

    k_maskT = K("maskT"); s.dma("gp", "tb", maskT_s, maskT_d, key=k_maskT, max_dma_last_dim=8192)
    k_qdecb = K("qdecb"); s.dma("gp", "tb2", qdecb_s, qdecb_d, key=k_qdecb, max_dma_last_dim=8192)
    k_pmat = K("pmat"); s.dma("gp", "tb3", pmat_s, pmat_d, key=k_pmat, max_dma_last_dim=8192)
    k_sgc = K("sgc"); s.dma("sy", "tb4", sgc_s, sg_d, key=k_sgc)
    k_suc = K("suc"); s.dma("sy", "tb5", suc_s, su_d, key=k_suc)

    wq_keys = {}
    for half in range(2):
        k = K("wq"); wq_keys[(0, half)] = k
        s.dma("gp", f"wq{0}{half}", wqb[0][:, half * 4096:(half + 1) * 4096],
              wqkv_img[0][:, half * 4096:(half + 1) * 4096], key=k,
              max_dma_last_dim=8192)
    b_wqb = [Buf(), Buf()]
    b_wqb[0].writer = wq_keys[(0, 1)]

    cs_keys = {}
    b_cs = [Buf(), Buf()]
    for fcg in range(2):
        k = K("cs"); cs_keys[fcg] = k
        s.dma("sy", f"cs{fcg}", cs_s[fcg], cs_d[fcg], key=k)
        b_cs[fcg].writer = k

    b_xs = [Buf(), Buf()]
    b_Tb = Buf()

    def ln_pass(tag, src_t, dst_t, src_pre, first_extra, tiles=range(8),
                bufs=None):
        if bufs is None:
            xs_l, Tb_l, b_xs_l, b_Tb_l = xs, Tb, b_xs, b_Tb
        else:
            xs_l, Tb_l, b_xs_l, b_Tb_l = bufs
        wkeys = []
        for tc in tiles:
            sl = tc % 2
            kx = K(tag + "x")
            aft = b_xs_l[sl].wr(kx) + (src_pre(tc) or []) + \
                (first_extra if tc < min(tiles) + 2 else [])
            s.dma("sy", f"{tag}x{sl}", xs_l[sl], src_t[tc], key=kx, after=aft)
            k1 = K(); s.dv.reduce_sum(st_s1[:, tc:tc + 1], xs_l[sl], AX.X).after(
                *b_xs_l[sl].rd(k1)).inc("dv", key=k1)
            k2 = K(); r = s.dv.scalar_tensor_tensor(
                Tb_l[:, :], xs_l[sl], 1.0, xs_l[sl], ALU.mult, ALU.mult,
                accum_out=st_s2[:, tc:tc + 1])
            r.after(*(b_xs_l[sl].rd(k2) + b_Tb_l.wr(k2))).inc("dv", key=k2)
            kmu = K(); s.dv.tensor_scalar(st_mu[:, tc:tc + 1], st_s1[:, tc:tc + 1],
                                          1.0 / C, None, ALU.mult).after(k1).inc(
                "dv", key=kmu)
            kms = K(); s.dv.scalar_tensor_tensor(
                st_msq[:, tc:tc + 1], st_mu[:, tc:tc + 1], 1.0,
                st_mu[:, tc:tc + 1], ALU.mult, ALU.mult).after(kmu).inc("dv", key=kms)
            kv = K(); s.dv.scalar_tensor_tensor(
                st_var[:, tc:tc + 1], st_s2[:, tc:tc + 1], 1.0 / C,
                st_msq[:, tc:tc + 1], ALU.mult, ALU.subtract).after(
                k2, kms).inc("dv", key=kv)
            kv2 = K(); s.dv.tensor_scalar(st_var[:, tc:tc + 1], st_var[:, tc:tc + 1],
                                          EPS, None, ALU.add).after(kv).inc("dv", key=kv2)
            kq = K(); s.ac.activation(st_sq[:, tc:tc + 1], st_var[:, tc:tc + 1],
                                      AF.Sqrt).after(kv2).inc("ac", key=kq)
            krc = K(); s.dv.reciprocal(st_rstd[:, tc:tc + 1],
                                       st_sq[:, tc:tc + 1]).after(kq).inc("dv", key=krc)
            kn = K(tag + "N")
            r = s.dv.tensor_scalar(Tb_l[:, :], xs_l[sl], st_mu[:, tc:tc + 1],
                                   st_rstd[:, tc:tc + 1], ALU.subtract, ALU.mult)
            r.after(*(b_xs_l[sl].rd(kn) + b_Tb_l.wr(kn) + [krc, kmu])).inc("dv", key=kn)
            kw = K(tag + "w")
            s.dma("sy", tag + "w", dst_t[tc], Tb_l[:, :], key=kw,
                  after=b_Tb_l.rd(kw) + [kn])
            wkeys.append(kw)
        return wkeys

    ln1_w = ln_pass("L1", x_t, dln_t, lambda tc: [], [])

    b_lnf = Buf()
    klnf = []
    for fc in range(16):
        k = K("lnfld"); klnf.append(k)
        s.dma("ac", "lnfld", lnf(fc), d_ln.ap()[:, fc * 128:(fc + 1) * 128],
              transpose=True, key=k, after=[ln1_w[7]])
    b_lnf.writer = klnf[-1]
    lnf_all = klnf

    b_tb1 = [Buf(), Buf()]
    b_qp = [Buf(), Buf()]
    b_T1 = [Buf(), Buf()]
    for _b in b_tb1 + b_qp:
        _b.readers += b_Tb.readers + [b_Tb.writer]
    last_dqw = {0: None, 1: None}
    last_dvw = {0: None, 1: None}

    for f4 in range(12):
        wpar = f4 % 2
        if f4 + 1 < 12:
            npar = (f4 + 1) % 2
            for half in range(2):
                k = K("wq"); wq_keys[(f4 + 1, half)] = k
                aft = b_wqb[npar].wr(k) if half == 0 else []
                s.dma("gp", f"wq{npar}{half}",
                      wqb[npar][:, half * 4096:(half + 1) * 4096],
                      wqkv_img[f4 + 1][:, half * 4096:(half + 1) * 4096],
                      key=k, after=aft, max_dma_last_dim=8192)
        for fc in range(4):
            fcg = f4 * 4 + fc
            h = fcg % 16
            kind = "q" if fcg < 16 else ("k" if fcg < 32 else "v")
            kmm = {0: K("a1mm"), 1: K("a1mm")}
            kraw = {}
            pbank = {0: (fc % 2) * 2, 1: (fc % 2) * 2 + 1}
            for cc in range(16):
                half = 0 if cc < 8 else 1
                for tt in range(2):
                    aft = [wq_keys[(f4, half)]]
                    if f4 == 0 and fc == 0 and tt == 0 and cc == 0:
                        aft += lnf_all
                    if cc == 0:
                        aft += psb[pbank[tt]].wr(kmm[tt])
                    r = s.pe.matmul(PS[pbank[tt]][:, :],
                                    wqb[wpar][:, cc * 512 + fc * 128:cc * 512 + (fc + 1) * 128],
                                    lnf(cc)[:, tt * 512:(tt + 1) * 512],
                                    start=(cc == 0), stop=(cc == 15)).after(*aft)
                    b_lnf.readers.append(kmm[tt])
                    if cc == 15:
                        b_wqb[wpar].rd(kmm[tt])
                        r.inc("pe", key=kmm[tt])
                        last_mm = kmm[tt]
            for tt in range(2):
                kraw[tt] = K("a1raw")
                aft = [kmm[tt]] + b_tb1[tt].wr(kraw[tt]) + psb[pbank[tt]].rd(kraw[tt])
                s.ac.activation(TbA1[tt], PS[pbank[tt]][:, :], AF.Copy,
                                scale=(RD if kind == "q" else 1.0)).after(*aft).inc(
                    "ac", key=kraw[tt])
            for tt in range(2):
                if kind == "v":
                    kd = K("dvw")
                    s.dma("sy", f"dvw{tt}", dv_rows[h][:, tt * 512:(tt + 1) * 512],
                          TbA1[tt], key=kd, after=b_tb1[tt].rd(kd) + [kraw[tt]])
                    last_dvw[tt] = kd
                else:
                    krot = K("a1rot")
                    aft = [kraw[tt], k_pmat] + psb[4 + tt].wr(krot) + b_tb1[tt].rd(krot)
                    s.pe.matmul(PS[4 + tt][:, :], pmat_s, TbA1[tt],
                                start=True, stop=True).after(*aft).inc("pe", key=krot)
                    cpar = fcg % 2
                    kt1 = K("a1t1")
                    aft = [kraw[tt], cs_keys[fcg]] + b_cs[cpar].rd(kt1) + \
                        b_tb1[tt].rd(kt1) + b_T1[0].wr(kt1)
                    s.dv.tensor_tensor(T1[:, 0:512], TbA1[tt],
                                       cs_s[cpar][:, tt * 512:(tt + 1) * 512],
                                       ALU.mult).after(*aft).inc("dv", key=kt1)
                    kt2 = K("a1t2")
                    aft = [krot, cs_keys[fcg]] + b_cs[cpar].rd(kt2) + \
                        psb[4 + tt].rd(kt2) + b_T1[1].wr(kt2)
                    s.dv.tensor_tensor(T1[:, 512:1024], PS[4 + tt][:, :],
                                       cs_s[cpar][:, 1024 + tt * 512:1024 + (tt + 1) * 512],
                                       ALU.mult).after(*aft).inc("dv", key=kt2)
                    kqp = K("a1qp")
                    s.dv.tensor_tensor(Qp[tt], T1[:, 0:512], T1[:, 512:1024],
                                       ALU.add).after(kt1, kt2, *(b_qp[tt].wr(kqp) +
                                       b_T1[0].rd(kqp) + b_T1[1].rd(kqp))).inc("dv", key=kqp)
                    dst = dq_rows[h] if kind == "q" else dk_rows[h]
                    kd = K("dqw")
                    s.dma("sy", f"dqw{tt}", dst[:, tt * 512:(tt + 1) * 512],
                          Qp[tt], key=kd, after=b_qp[tt].rd(kd) + [kqp])
                    last_dqw[tt] = kd
            nxt = fcg + 2
            if 2 <= nxt < 32:
                k = K("cs"); cs_keys[nxt] = k
                s.dma("sy", f"cs{nxt % 2}", cs_s[nxt % 2], cs_d[nxt % 16], key=k,
                      after=b_cs[nxt % 2].wr(k))

    tap_ready = {"d_ln": [ln1_w[7]],
                 "d_q": [last_dqw[0], last_dqw[1]],
                 "d_k": [last_dqw[0], last_dqw[1]],
                 "d_v": [last_dvw[0], last_dvw[1]]}
    def _finish(ow_keys):
        fin_keys = list(ow_keys)
        for tname, (tdst, tsrc) in tap_outs.items():
            rdy = tap_ready.get(tname)
            if rdy is None:
                continue
            kt = K("tap")
            s.dma("sy", "tap" + tname, tdst, tsrc, key=kt,
                  after=rdy + list(ow_keys))
            fin_keys.append(kt)
        r = s.sy.nop().after(*fin_keys)
        for name, v in s.cnt.items():
            if name.startswith("D"):
                if r.op.waits.get(name, -1) < v:
                    r.op.waits[name] = v
        build_nc._last_sched = s
        sems = {}
        for name in s.cnt:
            sems[name] = nc.alloc_semaphore("s_" + name.replace(":", "_"))
        with nc.Block("main") as blk:
            def mk(engname):
                def f(e):
                    seen = {}
                    for op in s.q[engname]:
                        for (sem, v) in sorted(op.waits.items()):
                            if seen.get(sem, -1) < v:
                                e.wait_ge(sems[sem], v)
                                seen[sem] = v
                        inst = getattr(e, op.method)(*op.args, **op.kwargs)
                        for (sem, st) in op.incs:
                            inst.then_inc(sems[sem], st)
                return f
            blk.tensor(mk("pe"))
            blk.vector(mk("dv"))
            blk.scalar(mk("ac"))
            blk.sync(mk("sy"))
            blk.gpsimd(mk("gp"))
        build_nc._ldw_skipped = 0
        return nc

    if not ph("a2p"):
        return _finish([])

    kbar = last_mm
    k_ghat = K("ghat")
    s.dma("gp", "ghld", ghat_s, ghat_d, key=k_ghat, max_dma_last_dim=8192)

    b_wpq = [Buf(), Buf()]
    wgp_keys = {}

    def precast_round(i):
        idx, p = divmod(i, 4)
        srcimg = wg_img[idx] if idx < 4 else wp_img[idx - 4]
        kc = K("wgc")
        s.dma("gp", f"wgc{i % 2}", wpq[i % 2],
              srcimg[:, p * 2048:(p + 1) * 2048], key=kc,
              after=b_wpq[i % 2].wr(kc), max_dma_last_dim=8192)
        kw = K("wgw")
        s.dma("sy", f"wgw{i % 2}",
              d_wgp.ap()[idx][:, p * 2048:(p + 1) * 2048], wpq[i % 2],
              key=kw, after=b_wpq[i % 2].rd(kw) + [kc])
        wgp_keys[(idx, p)] = kw

    b_SP = Buf()
    ksr = K("srun0")
    s.dv.memset(SP[:, :], 0.0).inc("dv", key=ksr)
    b_SP.writer = ksr

    b_lnh = Buf()
    b_sgh = [Buf() for _ in range(4)]
    wgq_keys = {}
    lnh_keys = {}
    gse = {}

    def emit_lnh(hf):
        kl = []
        for fc in range(16):
            k = K("lnh"); kl.append(k)
            aft = [ln1_w[7]]
            if fc == 0:
                aft += b_lnh.wr(k)
                if hf == 0:
                    aft += [kbar] + list(b_lnf.readers)
            s.dma("ac", "lnh", lnh(fc),
                  d_ln.ap()[hf * 512:(hf + 1) * 512, fc * 128:(fc + 1) * 128],
                  transpose=True, key=k, after=aft)
        b_lnh.writer = kl[-1]
        lnh_keys[hf] = kl

    def emit_g_c4(hf, c4):
        for tp in range(2):
            for qb in range(2):
                k = K("wg"); wgq_keys[(hf, c4, tp, qb)] = k
                s.dma("sy", f"wpq{qb % 2}", wpq[qb % 2],
                      d_wgp.ap()[c4][:, qb * 2048:(qb + 1) * 2048],
                      key=k, after=b_wpq[qb % 2].wr(k) + [wgp_keys[(c4, qb)]])
            kg = {}
            for cc in range(16):
                qb = cc // 4
                for ti in range(2):
                    tc = tp * 2 + ti
                    if cc == 0:
                        kg[tc] = K("gmm")
                    aft = [wgq_keys[(hf, c4, tp, qb)]]
                    if cc == 0:
                        aft += lnh_keys[hf] + psb[6 + ti].wr(kg[tc])
                    r = s.pe.matmul(PS[6 + ti][:, :],
                                    lnh(cc)[:, tc * 128:(tc + 1) * 128],
                                    wpq[qb % 2][:, (cc % 4) * 512:(cc % 4 + 1) * 512],
                                    start=(cc == 0), stop=(cc == 15)).after(*aft)
                    b_lnh.readers.append(kg[tc])
                    if cc % 4 == 3 and ti == 1 and cc != 15:
                        kqr = K("gqr")
                        r.inc("pe", key=kqr)
                        b_wpq[qb % 2].rd(kqr)
                        nqb = qb + 2
                        if nqb < 4:
                            k2 = K("wg"); wgq_keys[(hf, c4, tp, nqb)] = k2
                            s.dma("sy", f"wpq{nqb % 2}", wpq[nqb % 2],
                                  d_wgp.ap()[c4][:, nqb * 2048:(nqb + 1) * 2048],
                                  key=k2, after=b_wpq[nqb % 2].wr(k2) +
                                  [wgp_keys[(c4, nqb)]])
                    if cc == 15:
                        if ti == 1:
                            b_wpq[qb % 2].rd(kg[tc])
                        r.inc("pe", key=kg[tc])
            for ti in range(2):
                tc = tp * 2 + ti
                ks = K("gse")
                aft = [kg[tc]] + b_sgh[tc].wr(ks) + psb[6 + ti].rd(ks)
                s.ac.activation(sgh(tc)[:, c4 * 512:(c4 + 1) * 512],
                                PS[6 + ti][:, :], AF.Silu).after(*aft).inc(
                    "ac", key=ks)
                gse[(hf, c4, tc)] = ks

    b_qch = [Buf(), Buf()]
    b_kch = [Buf(), Buf()]
    b_ktch = [Buf(), Buf()]
    b_vch = [Buf(), Buf()]
    b_Qhat = Buf()
    b_Zm = [Buf(), Buf()]
    b_Sbf = [Buf(), Buf()]
    b_yp = [Buf() for _ in range(NCH)]

    chunk_keys = {}

    dqk_done = [last_dqw[0], last_dqw[1]]
    dv_done = [last_dvw[0], last_dvw[1]]

    def ld_chunk(c8, parts="qktv"):
        par = c8 % 2
        kq = K("qch"); kk = K("kch"); kt = K("ktch"); kv = K("vch")
        chunk_keys[c8] = dict(q=kq, k=kk, kt=kt, v=kv)
        if "q" in parts:
            s.dma("sy", f"qch{par}", qch[par].rearrange("p (h t) -> p h t", h=16),
                  dq_col(d_q, c8), key=kq,
                  after=b_qch[par].wr(kq) + dqk_done)
        if "k" in parts:
            s.dma("sy", f"kch{par}", kch[par].rearrange("p (h t) -> p h t", h=16),
                  dq_col(d_k, c8), key=kk,
                  after=b_kch[par].wr(kk) + dqk_done)
        if "t" in parts:
            s.dma("ac", f"ktch{par}", ktch[par],
                  d_k.ap()[:, c8 * 128:(c8 + 1) * 128], transpose=True,
                  key=kt, after=b_ktch[par].wr(kt) + dqk_done)
        if "v" in parts:
            s.dma("ac", f"vch{par}", vch[par],
                  d_v.ap()[:, c8 * 128:(c8 + 1) * 128], transpose=True,
                  key=kv, after=b_vch[par].wr(kv) + dv_done)

    if stop_after == "a2p0":
        return _finish([])
    ld_chunk(0)
    if stop_after == "a2p1":
        return _finish([])
    if stop_after == "a2p2":
        ld_chunk(1, parts="qk")
        return _finish([])
    if stop_after == "a2p3":
        ld_chunk(1, parts="tv")
        return _finish([])
    ld_chunk(1)
    if stop_after == "a2p":
        return _finish([])

    sbf_of = {}
    ye_of = {}
    for c8 in range(NCH):
        par = c8 % 2
        ck = chunk_keys[c8]
        kqh = K("a2qh")
        aft = [ck["q"], k_qdecb] + b_Qhat.wr(kqh) + b_qch[par].rd(kqh)
        s.dv.tensor_tensor(Qhat, qch[par], qdecb_s, ALU.mult).after(*aft).inc(
            "dv", key=kqh)
        kkh = K("a2kh")
        aft = [ck["kt"], k_ghat] + b_ktch[par].rd(kkh)
        b_ktch[par].writer = kkh
        b_ktch[par].readers = []
        s.gp.tensor_tensor(ktch[par], ktch[par], ghat_s, ALU.mult).after(
            *aft).inc("gp", key=kkh)
        for grp in range(4):
            pz = (c8 * 4 + grp) % 2
            gs4 = slice(grp * 512, (grp + 1) * 512)
            kz = K("a2z")
            for q in range(4):
                h = grp * 4 + q
                hs = slice(h * 128, (h + 1) * 128)
                aft = [ck["k"], ck["q"]] + b_kch[par].rd(kz) + b_qch[par].rd(kz)
                if q == 0:
                    aft += psb[pz].wr(kz)
                r = s.pe.matmul(PS[pz][:, q * 128:(q + 1) * 128],
                                kch[par][:, hs], qch[par][:, hs],
                                start=True, stop=True).after(*aft)
                if q == 3:
                    r.inc("pe", key=kz)
            kzm = K("a2zm")
            aft = [kz, k_maskT] + b_Zm[pz].wr(kzm) + psb[pz].rd(kzm)
            s.dv.tensor_tensor(ZmS[pz], PS[pz][:, 0:512], maskT_s[:, gs4],
                               ALU.mult).after(*aft).inc("dv", key=kzm)
            ksb = K("a2sb")
            aft = b_Sbf[pz].wr(ksb) + b_SP.rd(ksb)
            s.ac.activation(SbfS[pz], SP[:, gs4], AF.Copy).after(*aft).inc(
                "ac", key=ksb)
            sbf_of[(c8, grp)] = ksb
            kyg = K("a2y2")
            for q in range(4):
                h = grp * 4 + q
                hs = slice(h * 128, (h + 1) * 128)
                qsl = slice(q * 128, (q + 1) * 128)
                aft = [kzm, ck["v"]] + b_Zm[pz].rd(kyg) + b_vch[par].rd(kyg)
                if q == 0:
                    aft += psb[2 + pz].wr(kyg)
                s.pe.matmul(PS[2 + pz][:, qsl], ZmS[pz][:, qsl],
                            vch[par][:, hs], start=True, stop=False).after(*aft)
                aft2 = [kqh, ksb] + b_Qhat.rd(kyg) + b_Sbf[pz].rd(kyg)
                r = s.pe.matmul(PS[2 + pz][:, qsl], Qhat[:, hs],
                                SbfS[pz][:, qsl], start=False,
                                stop=True).after(*aft2)
                if q == 3:
                    r.inc("pe", key=kyg)
            kye = K("a2ye")
            aft = [kyg] + psb[2 + pz].rd(kye)
            if grp == 0:
                aft += b_yp[c8].wr(kye)
            s.ac.activation(ypart(c8)[:, gs4], PS[2 + pz][:, 0:512],
                            AF.Copy).after(*aft).inc("ac", key=kye)
            ye_of[(c8, grp)] = kye
            ksc = K("a2sc")
            for q in range(4):
                h = grp * 4 + q
                hs = slice(h * 128, (h + 1) * 128)
                aft = [kkh, ck["v"]] + b_ktch[par].rd(ksc) + b_vch[par].rd(ksc)
                if q == 0:
                    aft += psb[4 + pz].wr(ksc)
                r = s.pe.matmul(PS[4 + pz][:, q * 128:(q + 1) * 128],
                                ktch[par][:, hs], vch[par][:, hs],
                                start=True, stop=True).after(*aft)
                if q == 3:
                    r.inc("pe", key=ksc)
            for q in range(4):
                h = grp * 4 + q
                hs = slice(h * 128, (h + 1) * 128)
                ksu = K("a2su")
                aft = [ksc, ksb] + psb[4 + pz].rd(ksu)
                b_SP.readers.append(ksu)
                s.dv.scalar_tensor_tensor(SP[:, hs], SP[:, hs], g128[h],
                                          PS[4 + pz][:, q * 128:(q + 1) * 128],
                                          ALU.mult, ALU.add).after(*aft).inc(
                    "dv", key=ksu)
                if h == 15:
                    b_SP.writer = ksu
                    b_SP.readers = []
        if c8 + 2 < NCH:
            ld_chunk(c8 + 2)
        for i in range(c8 * 4, c8 * 4 + 4):
            precast_round(i)
        if c8 == 2:
            emit_lnh(0)
        if c8 >= 4:
            emit_g_c4(0, c8 - 4)
        if stop_after == "a2c0":
            break

    kccw = K("ccw")
    s.dma("sy", "ccw", cc_in.ap().rearrange("(h d) e -> d h e", d=128),
          SP[:, :].rearrange("p (h e) -> p h e", h=16), key=kccw,
          after=b_SP.rd(kccw))
    tap_ready["cc_in"] = [kccw]
    if not ph("p5"):
        return _finish([])

    kcc = K("cc")
    s.gp.collective_compute(
        "AllGather", ALU.bypass, replica_groups=[list(range(NCORES))],
        ins=[cc_in.ap().opt()], outs=[cc_out.ap().opt()]).after(kccw).inc(
        "cc", key=kcc)
    cc_out_r = cc_out.ap().rearrange("(b h d) e -> b d h e", d=128, h=16)
    b_gall = Buf()
    gall_keys = []
    for b in range(NCORES):
        k = K("gall"); gall_keys.append(k)
        aft = [kcc]
        if b == 0:
            aft += b_xs[0].readers + b_xs[1].readers + \
                b_cs[0].readers + b_cs[1].readers + \
                b_Zm[0].readers + b_Zm[1].readers + \
                b_Sbf[0].readers + b_Sbf[1].readers
        s.dma("gp", "gall", gall(b).rearrange("p (h e) -> p h e", h=16),
              cc_out_r[b], key=k, after=aft)
    b_gall.writer = gall_keys[-1]
    b_wtb = Buf()
    b_yn = Buf()
    ptmp_s = f32(48, 2048)
    b_ptmp = Buf()
    pcm = None
    for b in range(NCORES):
        kw = K("wtb")
        aft = b_wtb.wr(kw)
        if b == 0:
            aft += b_Qhat.readers + [b_Qhat.writer]
        s.dma("gp", "wtb", wtb_s, wtab_d[b], key=kw, after=aft)
        if b == 0:
            kp = K("pcm")
            aft = [kccw, kw] + gall_keys + b_SP.wr(kp) + b_wtb.rd(kp) + b_gall.rd(kp)
            s.dv.tensor_tensor(SP[:, :], gall(0), wtb_s, ALU.mult).after(
                *aft).inc("dv", key=kp)
            pcm = kp
        else:
            kt = K()
            aft = [kw] + b_ptmp.wr(kt) + b_wtb.rd(kt) + b_gall.rd(kt)
            if b == 1:
                aft += b_ktch[0].readers + b_ktch[1].readers +                     b_vch[0].readers + b_vch[1].readers +                     [b_ktch[0].writer, b_ktch[1].writer,
                     b_vch[0].writer, b_vch[1].writer]
            s.dv.tensor_tensor(ptmp_s, gall(b), wtb_s, ALU.mult).after(*aft).inc(
                "dv", key=kt)
            kp = K("pcm")
            b_SP.readers.append(kp)
            s.dv.tensor_tensor(SP[:, :], SP[:, :], ptmp_s, ALU.add).after(
                kt, *b_ptmp.rd(kp)).inc("dv", key=kp)
            b_SP.writer = kp
            b_SP.readers = []
            pcm = kp

    kgnw = K("gnw")
    s.dma("gp", "gnw", gnw_s, gnw_d, key=kgnw, after=[pcm] + b_gall.readers)
    kgnb = K("gnb")
    s.dma("gp", "gnb", gnb_s, gnb_d, key=kgnb, after=[pcm])
    b_gnw = Buf(); b_gnw.writer = kgnw
    b_gnb = Buf(); b_gnb.writer = kgnb
    if not ph("post"):
        return _finish([])

    b_gfh = Buf()
    b_gfh.readers += b_ptmp.readers + [b_ptmp.writer]
    xs2 = [f32(96, 2048), f32(104, 2048)]
    Tb2 = bf(112, 2048)
    b_xs2 = [Buf(), Buf()]
    b_Tb2 = Buf()
    ln2_w = []

    def l2_pre2(tc8):
        hf_, tc_ = divmod(tc8, 4)
        return [x2w_keys[(hf_, c4_, tc_)] for c4_ in range(4)]
    b_PT = Buf()
    b_pdec = Buf()
    b_x0 = [Buf(), Buf()]
    b_x2o = [Buf(), Buf()]
    x2w_keys = {}
    grow_keys = {}

    def emit_sweep_tc(hf, tc):
        c8g = hf * 4 + tc
        par = tc % 2
        kq = K("qch2")
        s.dma("sy", f"qch{par}", qch[par].rearrange("p (h t) -> p h t", h=16),
              dq_col(d_q, c8g), key=kq, after=b_qch[par].wr(kq))
        kp = K("pdec")
        aft = b_pdec.wr(kp)
        if hf == 0 and tc == 0:
            aft += b_kch[0].readers + b_kch[1].readers + \
                [b_kch[0].writer, b_kch[1].writer]
        s.dma("sy", "pdec", pdec_s, pdec_d[c8g], key=kp, after=aft)
        kqh = K("qh2")
        aft = [kq, k_qdecb] + b_Qhat.wr(kqh) + b_qch[par].rd(kqh)
        s.dv.tensor_tensor(Qhat, qch[par], qdecb_s, ALU.mult).after(
            *aft).inc("dv", key=kqh)
        kpt = K("ptl")
        aft = [kp, pcm] + b_PT.wr(kpt) + b_pdec.rd(kpt) + b_SP.rd(kpt)
        s.dv.tensor_tensor(PT, SP[:, :], pdec_s, ALU.mult).after(*aft).inc(
            "dv", key=kpt)
        last = None
        for h in range(16):
            hs = slice(h * 128, (h + 1) * 128)
            kmm = K("pth")
            aft = [kqh, kpt] + psb[6 + h % 2].wr(kmm) + \
                b_Qhat.rd(kmm) + b_PT.rd(kmm)
            s.pe.matmul(PS[6 + h % 2][:, 0:128], Qhat[:, hs], PT[:, hs],
                        start=True, stop=True).after(*aft).inc("pe", key=kmm)
            ky = K("ysl")
            aft = [kmm, ye_of[(c8g, h // 4)]] + psb[6 + h % 2].rd(ky)
            b_yp[c8g].readers.append(ky)
            s.dv.tensor_tensor(ypart(c8g)[:, hs], ypart(c8g)[:, hs],
                               PS[6 + h % 2][:, 0:128], ALU.add).after(
                *aft).inc("dv", key=ky)
            b_yp[c8g].writer = ky
            krd = K()
            s.dv.reduce_sum(st_s1[:, h:h + 1], ypart(c8g)[:, hs],
                            AX.X).after(ky).inc("dv", key=krd)
            kst = K("yst")
            r = s.dv.scalar_tensor_tensor(
                Tb[:, 0:128], ypart(c8g)[:, hs], 1.0, ypart(c8g)[:, hs],
                ALU.mult, ALU.mult, accum_out=st_s2[:, h:h + 1])
            r.after(*(b_Tb.wr(kst) + [ky])).inc("dv", key=kst)
            last = (krd, kst)
        kmu_b = K()
        s.dv.tensor_scalar(st_mu[:, 0:16], st_s1[:, 0:16], 1.0 / D, None,
                           ALU.mult).after(*last).inc("dv", key=kmu_b)
        kms_b = K()
        s.dv.scalar_tensor_tensor(st_msq[:, 0:16], st_mu[:, 0:16], 1.0,
                                  st_mu[:, 0:16], ALU.mult, ALU.mult).after(
            kmu_b).inc("dv", key=kms_b)
        kv = K("gnv")
        s.dv.scalar_tensor_tensor(st_var[:, 0:16], st_s2[:, 0:16], 1.0 / D,
                                  st_msq[:, 0:16], ALU.mult,
                                  ALU.subtract).after(kms_b).inc("dv", key=kv)
        kv2 = K()
        s.dv.tensor_scalar(st_var[:, 0:16], st_var[:, 0:16], EPS, None,
                           ALU.add).after(kv).inc("dv", key=kv2)
        kq_ = K("gnq")
        s.ac.activation(st_sq[:, 0:16], st_var[:, 0:16], AF.Sqrt).after(
            kv2).inc("ac", key=kq_)
        kr = K("gnr")
        s.dv.reciprocal(st_rstd[:, 0:16], st_sq[:, 0:16]).after(kq_).inc(
            "dv", key=kr)
        kn_last = None
        for h in range(16):
            hs = slice(h * 128, (h + 1) * 128)
            kn = K()
            aft = [kr, kmu_b] + (b_yn.wr(kn) if h == 0 else [])
            b_yp[c8g].readers.append(kn)
            s.dv.tensor_scalar(yn_s[:, hs], ypart(c8g)[:, hs],
                               st_mu[:, h:h + 1], st_rstd[:, h:h + 1],
                               ALU.subtract, ALU.mult).after(*aft).inc(
                "dv", key=kn)
            kn_last = kn
        ka = K()
        s.dv.tensor_tensor(yn_s, yn_s, gnw_s, ALU.mult).after(
            kgnw, kn_last).inc("dv", key=ka)
        b_gnw.readers.append(ka)
        kb = K()
        s.dv.tensor_tensor(yn_s, yn_s, gnb_s, ALU.add).after(
            kgnb, ka).inc("dv", key=kb)
        b_gnb.readers.append(kb)
        kgr = K("grow")
        aft = [gse[(hf, 3, tc)], kb] + b_Tb.wr(kgr) + b_sgh[tc].rd(kgr)
        s.dv.tensor_tensor(Tb[:, :], yn_s, sgh(tc), ALU.mult).after(
            *aft).inc("dv", key=kgr)
        b_yn.readers.append(kgr)
        kw = K("dgw")
        s.dma("sy", "dgw", dg_t[c8g], Tb[:, :], key=kw,
              after=b_Tb.rd(kw) + [kgr])
        grow_keys[(hf, tc)] = kw

    def emit_gfh(hf):
        kgf = []
        for fc in range(16):
            k = K("gfh"); kgf.append(k)
            aft = [grow_keys[(hf, 3)]]
            if fc == 0:
                aft += b_gfh.wr(k)
                if hf == 0:
                    aft += b_ktch[0].readers + b_ktch[1].readers + \
                        b_vch[0].readers + b_vch[1].readers + \
                        [b_ktch[0].writer, b_ktch[1].writer,
                         b_vch[0].writer, b_vch[1].writer]
            s.dma("ac", "gfh", gfh(fc),
                  d_g.ap()[hf * 512:(hf + 1) * 512, fc * 128:(fc + 1) * 128],
                  transpose=True, key=k, after=aft)
        b_gfh.writer = kgf[-1]
        return kgf

    def emit_proj_c4(hf, c4, gfh_all):
        for qb in range(2):
            k = K("wp"); wgq_keys[("p", hf, c4, qb)] = k
            s.dma("sy", f"wpq{qb % 2}", wpq[qb % 2],
                  d_wgp.ap()[4 + c4][:, qb * 2048:(qb + 1) * 2048],
                  key=k, after=b_wpq[qb % 2].wr(k) + [wgp_keys[(4 + c4, qb)]])
        kpj = {}
        for tc in range(2):
            sl = tc % 2
            kx = K("x0")
            s.dma("sy", f"x0{sl}", x0s[sl],
                  bass.AP(x.tensor, (hf * 512 + tc * 128) * C + c4 * 512,
                          [[C, 128], [1, 512]]),
                  key=kx, after=b_x0[sl].wr(kx))
            wgq_keys[("x0", hf, c4, tc)] = kx
        for cc in range(16):
            qb = cc // 4
            for tc in range(4):
                if cc == 0:
                    kpj[tc] = K("pjmm")
                aft = [wgq_keys[("p", hf, c4, qb)]]
                if c4 == 0 and cc == 0:
                    aft += gfh_all
                if cc == 0:
                    aft += psb[tc].wr(kpj[tc])
                r = s.pe.matmul(PS[tc][:, :],
                                gfh(cc)[:, tc * 128:(tc + 1) * 128],
                                wpq[qb % 2][:, (cc % 4) * 512:(cc % 4 + 1) * 512],
                                start=(cc == 0), stop=(cc == 15)).after(*aft)
                b_gfh.readers.append(kpj[tc])
                if cc % 4 == 3 and tc == 3:
                    if cc == 15:
                        b_wpq[qb % 2].rd(kpj[tc])
                    else:
                        kqr = K("pqr")
                        r.inc("pe", key=kqr)
                        b_wpq[qb % 2].rd(kqr)
                        nqb = qb + 2
                        if nqb < 4:
                            k2 = K("wp"); wgq_keys[("p", hf, c4, nqb)] = k2
                            s.dma("sy", f"wpq{nqb % 2}", wpq[nqb % 2],
                                  d_wgp.ap()[4 + c4][:, nqb * 2048:(nqb + 1) * 2048],
                                  key=k2, after=b_wpq[nqb % 2].wr(k2) +
                                  [wgp_keys[(4 + c4, nqb)]])
                if cc == 15:
                    r.inc("pe", key=kpj[tc])
        for tc in range(4):
            sl = tc % 2
            ke = K("x2e")
            aft = [kpj[tc], wgq_keys[("x0", hf, c4, tc)]] + \
                b_x2o[sl].wr(ke) + psb[tc].rd(ke) + b_x0[sl].rd(ke)
            s.dv.tensor_tensor(x2o[sl], PS[tc][:, :], x0s[sl],
                               ALU.add).after(*aft).inc("dv", key=ke)
            kw = K("x2w")
            s.dma("sy", f"x2w{sl}",
                  bass.AP(d_x2.ap().tensor, (hf * 512 + tc * 128) * C + c4 * 512,
                          [[C, 128], [1, 512]]),
                  x2o[sl], key=kw, after=b_x2o[sl].rd(kw) + [ke])
            x2w_keys[(hf, c4, tc)] = kw
            if tc + 2 < 4:
                kx = K("x0")
                s.dma("sy", f"x0{sl}", x0s[sl],
                      bass.AP(x.tensor,
                              (hf * 512 + (tc + 2) * 128) * C + c4 * 512,
                              [[C, 128], [1, 512]]),
                      key=kx, after=b_x0[sl].wr(kx))
                wgq_keys[("x0", hf, c4, tc + 2)] = kx

    emit_lnh(1)
    for tc in range(4):
        emit_sweep_tc(0, tc)
    gfh0 = emit_gfh(0)
    for c4 in range(4):
        emit_g_c4(1, c4)
        emit_proj_c4(0, c4, gfh0)
    for tc in range(4):
        emit_sweep_tc(1, tc)
    ln2_w += ln_pass("L2", dx2_t, dln2_t, l2_pre2, [pcm],
                     tiles=range(0, 4), bufs=(xs2, Tb2, b_xs2, b_Tb2))
    gfh1 = emit_gfh(1)
    for c4 in range(4):
        emit_proj_c4(1, c4, gfh1)
    ln2_w += ln_pass("L2", dx2_t, dln2_t, l2_pre2, [],
                     tiles=range(4, 8), bufs=(xs2, Tb2, b_xs2, b_Tb2))

    tap_ready["d_g"] = [x2w_keys[(1, 3, 3)]]
    tap_ready["d_x2"] = [x2w_keys[(hf_, c4_, tc_)]
                        for hf_ in range(2) for c4_ in range(4) for tc_ in range(4)]
    if not ph("post") or not ph("ffn"):
        return _finish([])

    l2_pre = l2_pre2
    tap_ready["d_ln2"] = [ln2_w[7]]

    b_ln2f = Buf()
    kl2 = []
    for fc in range(16):
        k = K("ln2ld"); kl2.append(k)
        aft = [ln2_w[7]]
        if fc == 0:
            aft += b_lnh.readers + b_sgh[0].readers + b_sgh[1].readers + \
                b_sgh[2].readers + b_sgh[3].readers + b_ln2f.wr(k)
        s.dma("ac", "ln2ld", ln2f(fc), d_ln2.ap()[:, fc * 128:(fc + 1) * 128],
              transpose=True, key=k, after=aft)
    b_ln2f.writer = kl2[-1]
    ln2_all = kl2

    ksd = K("sdl")
    s.dma("sy", "sdl", sdown_s, sdown_d, key=ksd,
          after=b_SP.readers + [b_SP.writer])
    b_sdown = Buf(); b_sdown.writer = ksd

    b_wga = [Buf(), Buf()]
    b_wua = [Buf(), Buf()]
    b_wdn = [Buf(), Buf()]
    b_fuse = [Buf() for _ in range(16)]
    b_gs = [Buf(), Buf()]
    b_acc = [Buf() for _ in range(8)]
    for oc in range(2):
        for t in range(4):
            b_wga[oc].readers += b_yp[t].readers + [b_yp[t].writer]
            b_wua[oc].readers += b_yp[t].readers + [b_yp[t].writer]
    for t in range(4, 8):
        b_wdn[0].readers += b_yp[t].readers + [b_yp[t].writer]
    b_wdn[1].readers += b_Qhat.readers + [b_Qhat.writer] + \
        b_wpq[0].readers + b_wpq[1].readers + \
        [b_wpq[0].writer, b_wpq[1].writer]
    for tt in range(2):
        b_gs[tt].readers += b_Tb.readers + [b_Tb.writer]
    fuse_prime = b_qch[0].readers + b_qch[1].readers + \
        [b_qch[0].writer, b_qch[1].writer] + b_pdec.readers + \
        [b_pdec.writer] + b_gfh.readers + [b_gfh.writer]
    for oc in range(16):
        b_fuse[oc].readers += fuse_prime
    acc_prime = b_gnw.readers + b_gnb.readers + b_yn.readers + \
        b_x0[0].readers + b_x0[1].readers + b_x2o[0].readers + \
        b_x2o[1].readers + b_xs[0].readers + b_xs[1].readers + \
        b_xs2[0].readers + b_xs2[1].readers + b_Tb2.readers + [b_Tb2.writer]
    for t8 in range(8):
        b_acc[t8].readers += acc_prime

    fgs_keys = {}
    ffu_keys = {}
    fdm_keys = {}
    fde_keys = {}
    for og in range(4):
        for oc in range(16):
            ocg = og * 16 + oc
            kwg = K("wga")
            s.dma("gp", f"wga{oc % 2}", wga[oc % 2], wgate_img[ocg], key=kwg,
                  after=b_wga[oc % 2].wr(kwg), max_dma_last_dim=2048)
            kwu = K("wua")
            s.dma("gp", f"wua{oc % 2}", wua[oc % 2], wup_img[ocg], key=kwu,
                  after=b_wua[oc % 2].wr(kwu), max_dma_last_dim=2048)
            kgm = {0: K("fgm"), 1: K("fgm")}
            kum = {0: K("fum"), 1: K("fum")}
            for cc in range(16):
                for tt in range(2):
                    aft = [kwg]
                    if ocg == 0 and tt == 0 and cc == 0:
                        aft += ln2_all
                    if cc == 0:
                        aft += psb[tt].wr(kgm[tt])
                    r = s.pe.matmul(PS[tt][:, :],
                                    wga[oc % 2][:, cc * 128:(cc + 1) * 128],
                                    ln2f(cc)[:, tt * 512:(tt + 1) * 512],
                                    start=(cc == 0), stop=(cc == 15)).after(*aft)
                    b_ln2f.readers.append(kgm[tt])
                    if cc == 15:
                        b_wga[oc % 2].rd(kgm[tt])
                        r.inc("pe", key=kgm[tt])
            for cc in range(16):
                for tt in range(2):
                    aft = [kwu]
                    if cc == 0:
                        aft += psb[2 + tt].wr(kum[tt])
                    r = s.pe.matmul(PS[2 + tt][:, :],
                                    wua[oc % 2][:, cc * 128:(cc + 1) * 128],
                                    ln2f(cc)[:, tt * 512:(tt + 1) * 512],
                                    start=(cc == 0), stop=(cc == 15)).after(*aft)
                    b_ln2f.readers.append(kum[tt])
                    if cc == 15:
                        b_wua[oc % 2].rd(kum[tt])
                        r.inc("pe", key=kum[tt])
            for tt in range(2):
                kgs = K("fgs")
                aft = [kgm[tt], k_sgc] + b_gs[tt].wr(kgs) + psb[tt].rd(kgs)
                s.ac.activation(gss[tt], PS[tt][:, :], AF.Silu,
                                scale=sgc_s[:, ocg:ocg + 1]).after(*aft).inc(
                    "ac", key=kgs)
                fgs_keys[(ocg, tt)] = kgs
                kfu = K("ffu")
                aft = [kum[tt], kgs, k_suc]
                if tt == 0:
                    aft += b_fuse[oc].wr(kfu)
                aft += psb[2 + tt].rd(kfu) + b_gs[tt].rd(kfu)
                s.dv.scalar_tensor_tensor(
                    fuse(oc)[:, tt * 512:(tt + 1) * 512], PS[2 + tt][:, :],
                    suc_s[:, ocg:ocg + 1], gss[tt], ALU.mult, ALU.mult).after(
                    *aft).inc("dv", key=kfu)
                ffu_keys[(ocg, tt)] = kfu
        for ht in range(4):
            kwd = K("wdn")
            s.dma("gp", f"wdn{ht % 2}", wdns[ht % 2], wdown_img[og * 4 + ht],
                  key=kwd, after=b_wdn[ht % 2].wr(kwd), max_dma_last_dim=2048)
            for hf2 in range(2):
                kdm = {}
                for tc in range(4):
                    kdm[tc] = K("fdm")
                for oc in range(16):
                    for tc in range(4):
                        aft = [kwd, ffu_keys[(og * 16 + oc, hf2)]]
                        if oc == 0:
                            aft += psb[4 + tc].wr(kdm[tc])
                        r = s.pe.matmul(
                            PS[4 + tc][:, :],
                            fuse(oc)[:, hf2 * 512 + tc * 128:hf2 * 512 + (tc + 1) * 128],
                            wdns[ht % 2][:, oc * 512:(oc + 1) * 512],
                            start=(oc == 0), stop=(oc == 15)).after(*aft)
                        b_fuse[oc].readers.append(kdm[tc])
                        if oc == 15:
                            b_wdn[ht % 2].rd(kdm[tc])
                            r.inc("pe", key=kdm[tc])
                for tc in range(4):
                    t8 = hf2 * 4 + tc
                    accsl = acc(t8)[:, ht * 512:(ht + 1) * 512]
                    ke = K("fde")
                    aft = [kdm[tc]] + psb[4 + tc].rd(ke)
                    if og == 0:
                        aft += b_acc[t8].wr(ke)
                        r = s.dv.tensor_copy(accsl, PS[4 + tc][:, :]).after(*aft)
                    else:
                        b_acc[t8].readers.append(ke)
                        r = s.dv.tensor_tensor(accsl, accsl, PS[4 + tc][:, :],
                                               ALU.add).after(*aft)
                        b_acc[t8].writer = ke
                        b_acc[t8].readers = []
                    r.inc("dv", key=ke)
                    fde_keys[(og, ht, hf2, tc)] = ke
    b_x2f = [Buf(), Buf()]
    b_fin = [Buf(), Buf()]
    prime_fin = []
    for bb in (b_wga[0], b_wga[1], b_wua[0], b_wua[1], b_wdn[0], b_gs[0], b_gs[1]):
        prime_fin += bb.readers + ([bb.writer] if bb.writer else [])
    b_x2f[0].readers += prime_fin
    b_fin[0].readers += prime_fin
    b_fin[1].readers += prime_fin
    b_x2f[1].readers += prime_fin
    ow_keys = []
    for t8 in range(8):
        par = t8 % 2
        kx = K("x2f")
        aft = l2_pre(t8) + b_x2f[par].wr(kx)
        s.dma("sy", f"x2f{par}", x2f[par], dx2_t[t8], key=kx, after=aft)
        km = K("finm")
        aft = [fde_keys[(3, ht, t8 // 4, t8 % 4)] for ht in range(4)] + \
            [ksd] + b_fin[par].wr(km) + b_acc[t8].rd(km) + b_sdown.rd(km)
        s.dv.tensor_tensor(fin[par], acc(t8), sdown_s, ALU.mult).after(
            *aft).inc("dv", key=km)
        kf = K("fin")
        s.dv.tensor_tensor(fin[par], fin[par], x2f[par], ALU.add).after(
            kx, km, *b_x2f[par].rd(kf)).inc("dv", key=kf)
        ko = K("ow")
        s.dma("sy", f"ow{par}", out_t[t8], fin[par], key=ko,
              after=b_fin[par].rd(ko) + [kf])
        ow_keys.append(ko)

    return _finish(ow_keys)


def kernel(L0_attn_x, L0_attn_w_qkv, L0_attn_w_g, L0_attn_w_proj,
           L0_attn_gn_weight, L0_attn_gn_bias, L0_ffn_w_gate, L0_ffn_w_up,
           L0_ffn_w_down, L0_ffn_s_gate, L0_ffn_s_up, L0_ffn_s_down):
    x_full = np.ascontiguousarray(np.asarray(L0_attn_x, np.float32).reshape(T, C))
    sh = _host_shared(L0_attn_w_qkv, L0_attn_w_g, L0_attn_w_proj,
                      L0_ffn_w_gate, L0_ffn_w_up, L0_ffn_w_down,
                      L0_attn_gn_weight, L0_attn_gn_bias,
                      L0_ffn_s_gate, L0_ffn_s_up, L0_ffn_s_down)
    cos, sin = _host_xpos()

    nc = build_nc(taps=tuple(_DEBUG_TAPS))

    in_maps = []
    for core in range(NCORES):
        pc = _host_percore(core, x_full, cos, sin)
        m = dict(x=pc["x"], cs_img=pc["cs_img"], wtab=pc["wtab"],
                 wqkv_img=sh["wqkv_img"], wg_img=sh["wg_img"], wp_img=sh["wp_img"],
                 wgate_img=sh["wgate_img"], wup_img=sh["wup_img"],
                 wdown_img=sh["wdown_img"], maskT=sh["maskT"], qdecb=sh["qdecb"],
                 ghat=sh["ghat"], pdec=sh["pdec"], pmat=sh["pmat"],
                 gnw_b=sh["gnw_b"], gnb_b=sh["gnb_b"], sdown_b=sh["sdown_b"],
                 sg_c=sh["sg_c"], su_c=sh["su_c"])
        in_maps.append(m)

    res = run_bass_kernel_spmd(nc, in_maps, core_ids=list(range(NCORES)))
    outs = [res.results[c]["out"] for c in range(NCORES)]
    full = np.concatenate(outs, axis=0).reshape(1, T, C).astype(np.float32)
    if _DEBUG_TAPS:
        kernel._tap_results = [
            {k: v for k, v in res.results[c].items() if k.startswith("tap_")}
            for c in range(NCORES)]
    return full


# version 09
# speedup vs baseline: 1.0213x; 1.0213x over previous
import numpy as np
import concourse.bass as bass
import concourse.mybir as mybir
from concourse.bass_utils import run_bass_kernel_spmd

F32 = mybir.dt.float32
BF16 = mybir.dt.bfloat16
I8 = mybir.dt.int8
AF = mybir.ActivationFunctionType
ALU = mybir.AluOpType
AX = mybir.AxisListType

NCORES = 8
T = 8192
C = 2048
H = 16
D = 128
TPC = T // NCORES
NCH = TPC // 128
INTER = 8192
EPS = 1e-5
RD = float(D) ** -0.5

_DEBUG_TAPS = []

KBf = 256


ENGS = ("pe", "dv", "ac", "sy", "gp")


class _Op:
    __slots__ = ("method", "args", "kwargs", "waits", "incs")

    def __init__(self, method, args, kwargs):
        self.method = method
        self.args = args
        self.kwargs = kwargs
        self.waits = {}
        self.incs = []


class _OpRef:
    def __init__(self, s, op):
        self.s = s
        self.op = op

    def after(self, *keys):
        for k in keys:
            if k is None:
                continue
            if isinstance(k, list):
                self.after(*k)
                continue
            sem, v = self.s.tick[k]
            if self.op.waits.get(sem, -1) < v:
                self.op.waits[sem] = v
        return self

    def inc(self, sem, key=None, dma=False):
        step = 16 if dma else 1
        self.s.cnt[sem] = self.s.cnt.get(sem, 0) + step
        v = self.s.cnt[sem]
        self.op.incs.append((sem, step))
        if key is not None:
            keys = key if isinstance(key, list) else [key]
            for k in keys:
                assert k not in self.s.tick, f"dup key {k}"
                self.s.tick[k] = (sem, v)
        return self


class _EngProxy:
    def __init__(self, s, eng):
        self._s = s
        self._e = eng

    def __getattr__(self, m):
        def call(*a, **kw):
            op = _Op(m, a, kw)
            self._s.q[self._e].append(op)
            return _OpRef(self._s, op)
        return call


class Sched:
    def __init__(self):
        self.q = {e: [] for e in ENGS}
        self.cnt = {}
        self.tick = {}
        self.nkey = 0
        for e in ENGS:
            setattr(self, e, _EngProxy(self, e))

    def key(self, tag=""):
        self.nkey += 1
        return (tag, self.nkey)

    def dma(self, eng, slot, out, in_, key=None, after=(), **kw):
        r = getattr(self, eng).dma_start(out=out, in_=in_, **kw)
        r.after(*after)
        r.inc("D" + slot, key=key, dma=True)
        return r


class Buf:

    def __init__(self):
        self.writer = None
        self.readers = []

    def wr(self, key):
        aft = list(self.readers) if self.readers else (
            [self.writer] if self.writer is not None else [])
        self.writer = key
        self.readers = []
        return aft

    def rd(self, key):
        self.readers.append(key)
        return [self.writer] if self.writer is not None else []


def _gammas():
    return np.array([1.0 - 2.0 ** (-5.0 - h) for h in range(H)], dtype=np.float64)


def _host_shared(wqkv, wg, wp, wgate, wup, wdown, gnw, gnb, sg, su, sd):
    g = _gammas()
    sh = {}
    sh["wqkv_img"] = np.ascontiguousarray(
        np.asarray(wqkv, np.float32).reshape(16, 128, 12, 512)
        .transpose(2, 1, 0, 3).reshape(12, 128, 8192))
    sh["wg_img"] = np.ascontiguousarray(
        np.asarray(wg, np.float32).reshape(16, 128, 4, 512)
        .transpose(2, 1, 0, 3).reshape(4, 128, 8192))
    sh["wp_img"] = np.ascontiguousarray(
        np.asarray(wp, np.float32).reshape(16, 128, 4, 512)
        .transpose(2, 1, 0, 3).reshape(4, 128, 8192))
    sh["wgate_img"] = np.ascontiguousarray(
        np.asarray(wgate).reshape(16, 128, 64, 128)
        .transpose(2, 1, 0, 3).reshape(64, 128, 2048))
    sh["wup_img"] = np.ascontiguousarray(
        np.asarray(wup).reshape(16, 128, 64, 128)
        .transpose(2, 1, 0, 3).reshape(64, 128, 2048))
    sh["wdown_img"] = np.ascontiguousarray(
        np.asarray(wdown).reshape(4, 16, 128, 4, 512)
        .transpose(0, 3, 2, 1, 4).reshape(16, 128, 8192))
    j = np.arange(128, dtype=np.float64)
    maskT = np.zeros((128, H * 128), dtype=np.float64)
    for h in range(H):
        e = j[None, :] - j[:, None]
        maskT[:, h * 128:(h + 1) * 128] = np.where(e >= 0, g[h] ** np.maximum(e, 0), 0.0)
    sh["maskT"] = maskT.astype(np.float32)
    qd = np.concatenate([g[h] ** (j + 1.0) for h in range(H)])
    sh["qdecb"] = np.broadcast_to(qd.astype(np.float32)[None, :], (128, H * 128)).copy()
    gh = np.zeros((128, H * 128), dtype=np.float64)
    for h in range(H):
        gh[:, h * 128:(h + 1) * 128] = (g[h] ** (127.0 - j))[:, None]
    sh["ghat"] = gh.astype(np.float32)
    pdec = np.zeros((NCH, 128, H * 128), dtype=np.float32)
    for c8 in range(NCH):
        row = np.concatenate([np.full(128, g[h] ** (128.0 * c8)) for h in range(H)])
        pdec[c8] = row.astype(np.float32)[None, :]
    sh["pdec"] = pdec
    P = np.zeros((128, 128), dtype=np.float32)
    ii = np.arange(0, 128, 2)
    P[ii + 1, ii] = -1.0
    P[ii, ii + 1] = 1.0
    sh["pmat"] = P
    sh["gnw_b"] = np.broadcast_to(np.asarray(gnw, np.float32).reshape(1, C), (128, C)).copy()
    sh["gnb_b"] = np.broadcast_to(np.asarray(gnb, np.float32).reshape(1, C), (128, C)).copy()
    sh["sdown_b"] = np.broadcast_to(np.asarray(sd, np.float32).reshape(1, C), (128, C)).copy()
    sh["sg_c"] = np.ascontiguousarray(np.asarray(sg, np.float32).reshape(64, 128).T)
    sh["su_c"] = np.ascontiguousarray(np.asarray(su, np.float32).reshape(64, 128).T)
    return sh


def _host_xpos():
    i = np.arange(C // 2, dtype=np.float64)
    zeta = (2.0 * i + 0.4 * C) / (1.4 * C)
    invf = 1.0 / (10000.0 ** (2.0 * i / C))
    t = np.arange(T, dtype=np.float64)
    scale = zeta[None, :] ** (t[:, None] / 512.0)
    ang = t[:, None] * invf[None, :]
    cos = np.repeat(np.cos(ang) * scale, 2, axis=1).astype(np.float32)
    sin = np.repeat(np.sin(ang) * scale, 2, axis=1).astype(np.float32)
    return cos, sin


def _host_percore(core, x_full, cos, sin):
    g = _gammas()
    pc = {}
    pc["x"] = np.ascontiguousarray(x_full[core * TPC:(core + 1) * TPC])
    cs = np.empty((16, 128, 2048), dtype=np.float32)
    csl = cos[core * TPC:(core + 1) * TPC]
    ssl = sin[core * TPC:(core + 1) * TPC]
    for fc in range(16):
        cs[fc, :, 0:1024] = csl[:, fc * 128:(fc + 1) * 128].T
        cs[fc, :, 1024:2048] = ssl[:, fc * 128:(fc + 1) * 128].T
    pc["cs_img"] = cs
    wt = np.zeros((NCORES, 128, H * 128), dtype=np.float32)
    for b in range(core):
        row = np.concatenate(
            [np.full(128, g[h] ** (1024.0 * (core - 1 - b))) for h in range(H)])
        wt[b] = row.astype(np.float32)[None, :]
    pc["wtab"] = wt
    return pc


def build_nc(taps=(), stop_after="ffn"):
    PH = ["a1", "a2p0", "a2p1", "a2p2", "a2p3", "a2p", "a2c0", "a2", "p5", "post", "ffn"]
    lim = PH.index(stop_after)

    def ph(name):
        return PH.index(name) <= lim

    nc = bass.Bass("TRN2", target_bir_lowering=False)
    g64 = _gammas()
    g128 = [float(np.float32(g64[h] ** 128.0)) for h in range(H)]

    x = nc.dram_tensor("x", [TPC, C], F32, kind="ExternalInput").ap()
    out = nc.dram_tensor("out", [TPC, C], F32, kind="ExternalOutput").ap()
    wqkv_img = nc.dram_tensor("wqkv_img", [12, 128, 8192], F32, kind="ExternalInput").ap()
    wg_img = nc.dram_tensor("wg_img", [4, 128, 8192], F32, kind="ExternalInput").ap()
    wp_img = nc.dram_tensor("wp_img", [4, 128, 8192], F32, kind="ExternalInput").ap()
    wgate_img = nc.dram_tensor("wgate_img", [64, 128, 2048], I8, kind="ExternalInput").ap()
    wup_img = nc.dram_tensor("wup_img", [64, 128, 2048], I8, kind="ExternalInput").ap()
    wdown_img = nc.dram_tensor("wdown_img", [16, 128, 8192], I8, kind="ExternalInput").ap()
    maskT_d = nc.dram_tensor("maskT", [128, 2048], F32, kind="ExternalInput").ap()
    qdecb_d = nc.dram_tensor("qdecb", [128, 2048], F32, kind="ExternalInput").ap()
    ghat_d = nc.dram_tensor("ghat", [128, 2048], F32, kind="ExternalInput").ap()
    pdec_d = nc.dram_tensor("pdec", [NCH, 128, 2048], F32, kind="ExternalInput").ap()
    pmat_d = nc.dram_tensor("pmat", [128, 128], F32, kind="ExternalInput").ap()
    gnw_d = nc.dram_tensor("gnw_b", [128, 2048], F32, kind="ExternalInput").ap()
    gnb_d = nc.dram_tensor("gnb_b", [128, 2048], F32, kind="ExternalInput").ap()
    sdown_d = nc.dram_tensor("sdown_b", [128, 2048], F32, kind="ExternalInput").ap()
    sg_d = nc.dram_tensor("sg_c", [128, 64], F32, kind="ExternalInput").ap()
    su_d = nc.dram_tensor("su_c", [128, 64], F32, kind="ExternalInput").ap()
    cs_d = nc.dram_tensor("cs_img", [16, 128, 2048], F32, kind="ExternalInput").ap()
    wtab_d = nc.dram_tensor("wtab", [NCORES, 128, 2048], F32, kind="ExternalInput").ap()

    d_ln = nc.dram_tensor("d_ln", [TPC, C], BF16)
    d_q = nc.dram_tensor("d_q", [C, TPC], BF16)
    d_k = nc.dram_tensor("d_k", [C, TPC], BF16)
    d_v = nc.dram_tensor("d_v", [C, TPC], BF16)
    d_g = nc.dram_tensor("d_g", [TPC, C], BF16)
    d_x2 = nc.dram_tensor("d_x2", [TPC, C], F32)
    d_ln2 = nc.dram_tensor("d_ln2", [TPC, C], BF16)
    d_wgp = nc.dram_tensor("d_wgp", [8, 128, 8192], BF16)
    cc_in = nc.dram_tensor("cc_in", [H * 128, 128], F32)
    cc_out = nc.dram_tensor("cc_out", [NCORES * H * 128, 128], F32)

    tap_outs = {}
    for tname in taps:
        src = {"d_ln": d_ln, "d_q": d_q, "d_k": d_k, "d_v": d_v, "d_g": d_g,
               "d_x2": d_x2, "d_ln2": d_ln2, "cc_in": cc_in,
               "cc_out": cc_out}.get(tname)
        if src is not None:
            tap_outs[tname] = (
                nc.dram_tensor("tap_" + tname, list(src.shape), src.dtype,
                               kind="ExternalOutput").ap(), src.ap())

    ARENA_KB = 204
    ar = nc.alloc_sbuf_tensor("arena", [128, ARENA_KB * KBf], F32)
    arb = ar.bitcast(BF16)

    def f32(kb, n):
        o = int(kb * KBf)
        return ar[:, o:o + n]

    def bf(kb, n):
        o = int(kb * KBf) * 2
        return arb[:, o:o + n]

    lnf = lambda cc: bf(0, 16 * 1024)[:, cc * 1024:(cc + 1) * 1024]
    ghat_s = bf(120, 2048)
    ZmS = [bf(116, 512), bf(116.5, 512)]
    SbfS = [bf(117, 512), bf(117.5, 512)]
    lnh = lambda fc: bf(0, 16 * 512)[:, fc * 512:(fc + 1) * 512]
    sgh = lambda tc: bf(16, 4 * 2048)[:, tc * 2048:(tc + 1) * 2048]
    ln2f = lambda cc: bf(0, 16 * 1024)[:, cc * 1024:(cc + 1) * 1024]
    qch = [bf(32, 2048), bf(36, 2048)]
    kch = [bf(40, 2048), bf(44, 2048)]
    ktch = [bf(48, 2048), bf(52, 2048)]
    vch = [bf(56, 2048), bf(60, 2048)]
    pdec_s = f32(40, 2048)
    gfh = lambda fc: bf(48, 16 * 512)[:, fc * 512:(fc + 1) * 512]
    fuse = lambda oc: bf(32, 16 * 1024)[:, oc * 1024:(oc + 1) * 1024]
    xs = [f32(64, 2048), f32(72, 2048)]
    cs_s = [f32(80, 2048), f32(88, 2048)]
    gall = lambda b: f32(64, 8 * 2048)[:, b * 2048:(b + 1) * 2048]
    gnw_s = f32(64, 2048)
    gnb_s = f32(72, 2048)
    x0s = [f32(80, 512), f32(82, 512)]
    x2o = [f32(84, 512), f32(86, 512)]
    yn_s = f32(88, 2048)
    acc = lambda t8: f32(64, 8 * 2048)[:, t8 * 2048:(t8 + 1) * 2048]
    wqb = [bf(128, 8192), bf(144, 8192)]
    ypart = lambda c8: bf(128, 8 * 2048)[:, c8 * 2048:(c8 + 1) * 2048]
    wga = [bf(128, 2048), bf(132, 2048)]
    wua = [bf(136, 2048), bf(140, 2048)]
    wdns = [bf(144, 8192), bf(160, 8192)]
    x2f = [f32(128, 2048), f32(152, 2048)]
    fin = [f32(136, 2048), f32(144, 2048)]
    maskT_s = bf(160, 2048)
    qdecb_s = bf(164, 2048)
    wpq = [bf(168, 2048), bf(172, 2048)]
    SP = f32(176, 2048)
    sdown_s = f32(176, 2048)
    PT = bf(184, 2048)
    wtb_s = f32(184, 2048)
    Qhat = bf(188, 2048)
    Tb = bf(192, 2048)
    TbA1 = [bf(192, 512), bf(193, 512)]
    Qp = [bf(194, 512), bf(195, 512)]
    gss = [bf(192, 512), bf(193, 512)]
    T1 = f32(196, 1024)
    st_s1 = f32(200, 64)
    st_s2 = f32(200.25, 64)
    st_mu = f32(200.5, 64)
    st_msq = f32(200.75, 64)
    st_var = f32(201.0, 64)
    st_sq = f32(201.25, 64)
    st_rstd = f32(201.5, 64)
    sgc_s = f32(201.75, 64)
    suc_s = f32(202.0, 64)
    pmat_s = bf(202.25, 128)

    PS = [nc.alloc_psum_tensor(f"ps{i}", [128, 512], F32) for i in range(8)]

    s = Sched()
    K = s.key

    psb = [Buf() for _ in range(8)]

    x_t = x.rearrange("(n p) c -> n p c", p=128)
    out_t = out.rearrange("(n p) c -> n p c", p=128)
    dln_t = d_ln.ap().rearrange("(n p) c -> n p c", p=128)
    dg_t = d_g.ap().rearrange("(n p) c -> n p c", p=128)
    dx2_t = d_x2.ap().rearrange("(n p) c -> n p c", p=128)
    dln2_t = d_ln2.ap().rearrange("(n p) c -> n p c", p=128)
    dq_rows = d_q.ap().rearrange("(n p) t -> n p t", p=128)
    dk_rows = d_k.ap().rearrange("(n p) t -> n p t", p=128)
    dv_rows = d_v.ap().rearrange("(n p) t -> n p t", p=128)

    def dq_col(dram, c8):
        return dram.ap().rearrange("(h d) t -> d h t", d=128)[:, :, c8 * 128:(c8 + 1) * 128]

    k_maskT = K("maskT"); s.dma("gp", "tb", maskT_s, maskT_d, key=k_maskT, max_dma_last_dim=8192)
    k_qdecb = K("qdecb"); s.dma("gp", "tb2", qdecb_s, qdecb_d, key=k_qdecb, max_dma_last_dim=8192)
    k_pmat = K("pmat"); s.dma("gp", "tb3", pmat_s, pmat_d, key=k_pmat, max_dma_last_dim=8192)
    k_sgc = K("sgc"); s.dma("sy", "tb4", sgc_s, sg_d, key=k_sgc)
    k_suc = K("suc"); s.dma("sy", "tb5", suc_s, su_d, key=k_suc)

    wq_keys = {}
    for half in range(2):
        k = K("wq"); wq_keys[(0, half)] = k
        s.dma("gp", f"wq{0}{half}", wqb[0][:, half * 4096:(half + 1) * 4096],
              wqkv_img[0][:, half * 4096:(half + 1) * 4096], key=k,
              max_dma_last_dim=8192)
    b_wqb = [Buf(), Buf()]
    b_wqb[0].writer = wq_keys[(0, 1)]

    cs_keys = {}
    b_cs = [Buf(), Buf()]
    for fcg in range(2):
        k = K("cs"); cs_keys[fcg] = k
        s.dma("sy", f"cs{fcg}", cs_s[fcg], cs_d[fcg], key=k)
        b_cs[fcg].writer = k

    b_xs = [Buf(), Buf()]
    b_Tb = Buf()

    def ln_pass(tag, src_t, dst_t, src_pre, first_extra, tiles=range(8),
                bufs=None):
        if bufs is None:
            xs_l, Tb_l, b_xs_l, b_Tb_l = xs, Tb, b_xs, b_Tb
        else:
            xs_l, Tb_l, b_xs_l, b_Tb_l = bufs
        wkeys = []
        for tc in tiles:
            sl = tc % 2
            kx = K(tag + "x")
            aft = b_xs_l[sl].wr(kx) + (src_pre(tc) or []) + \
                (first_extra if tc < min(tiles) + 2 else [])
            s.dma("sy", f"{tag}x{sl}", xs_l[sl], src_t[tc], key=kx, after=aft)
            k1 = K(); s.dv.reduce_sum(st_s1[:, tc:tc + 1], xs_l[sl], AX.X).after(
                *b_xs_l[sl].rd(k1)).inc("dv", key=k1)
            k2 = K(); r = s.dv.scalar_tensor_tensor(
                Tb_l[:, :], xs_l[sl], 1.0, xs_l[sl], ALU.mult, ALU.mult,
                accum_out=st_s2[:, tc:tc + 1])
            r.after(*(b_xs_l[sl].rd(k2) + b_Tb_l.wr(k2))).inc("dv", key=k2)
            kmu = K(); s.dv.tensor_scalar(st_mu[:, tc:tc + 1], st_s1[:, tc:tc + 1],
                                          1.0 / C, None, ALU.mult).after(k1).inc(
                "dv", key=kmu)
            kms = K(); s.dv.scalar_tensor_tensor(
                st_msq[:, tc:tc + 1], st_mu[:, tc:tc + 1], 1.0,
                st_mu[:, tc:tc + 1], ALU.mult, ALU.mult).after(kmu).inc("dv", key=kms)
            kv = K(); s.dv.scalar_tensor_tensor(
                st_var[:, tc:tc + 1], st_s2[:, tc:tc + 1], 1.0 / C,
                st_msq[:, tc:tc + 1], ALU.mult, ALU.subtract).after(
                k2, kms).inc("dv", key=kv)
            kv2 = K(); s.dv.tensor_scalar(st_var[:, tc:tc + 1], st_var[:, tc:tc + 1],
                                          EPS, None, ALU.add).after(kv).inc("dv", key=kv2)
            kq = K(); s.ac.activation(st_sq[:, tc:tc + 1], st_var[:, tc:tc + 1],
                                      AF.Sqrt).after(kv2).inc("ac", key=kq)
            krc = K(); s.dv.reciprocal(st_rstd[:, tc:tc + 1],
                                       st_sq[:, tc:tc + 1]).after(kq).inc("dv", key=krc)
            kn = K(tag + "N")
            r = s.dv.tensor_scalar(Tb_l[:, :], xs_l[sl], st_mu[:, tc:tc + 1],
                                   st_rstd[:, tc:tc + 1], ALU.subtract, ALU.mult)
            r.after(*(b_xs_l[sl].rd(kn) + b_Tb_l.wr(kn) + [krc, kmu])).inc("dv", key=kn)
            kw = K(tag + "w")
            s.dma("sy", tag + "w", dst_t[tc], Tb_l[:, :], key=kw,
                  after=b_Tb_l.rd(kw) + [kn])
            wkeys.append(kw)
        return wkeys

    ln1_w = ln_pass("L1", x_t, dln_t, lambda tc: [], [])

    b_lnf = Buf()
    klnf = []
    for fc in range(16):
        k = K("lnfld"); klnf.append(k)
        s.dma("ac", "lnfld", lnf(fc), d_ln.ap()[:, fc * 128:(fc + 1) * 128],
              transpose=True, key=k, after=[ln1_w[7]])
    b_lnf.writer = klnf[-1]
    lnf_all = klnf

    b_tb1 = [Buf(), Buf()]
    b_qp = [Buf(), Buf()]
    b_T1 = [Buf(), Buf()]
    for _b in b_tb1 + b_qp:
        _b.readers += b_Tb.readers + [b_Tb.writer]
    last_dqw = {0: None, 1: None}
    last_dvw = {0: None, 1: None}

    for f4 in range(12):
        wpar = f4 % 2
        if f4 + 1 < 12:
            npar = (f4 + 1) % 2
            for half in range(2):
                k = K("wq"); wq_keys[(f4 + 1, half)] = k
                aft = b_wqb[npar].wr(k) if half == 0 else []
                s.dma("gp", f"wq{npar}{half}",
                      wqb[npar][:, half * 4096:(half + 1) * 4096],
                      wqkv_img[f4 + 1][:, half * 4096:(half + 1) * 4096],
                      key=k, after=aft, max_dma_last_dim=8192)
        for fc in range(4):
            fcg = f4 * 4 + fc
            h = fcg % 16
            kind = "q" if fcg < 16 else ("k" if fcg < 32 else "v")
            kmm = {0: K("a1mm"), 1: K("a1mm")}
            kraw = {}
            pbank = {0: (fc % 2) * 2, 1: (fc % 2) * 2 + 1}
            for cc in range(16):
                half = 0 if cc < 8 else 1
                for tt in range(2):
                    aft = [wq_keys[(f4, half)]]
                    if f4 == 0 and fc == 0 and tt == 0 and cc == 0:
                        aft += lnf_all
                    if cc == 0:
                        aft += psb[pbank[tt]].wr(kmm[tt])
                    r = s.pe.matmul(PS[pbank[tt]][:, :],
                                    wqb[wpar][:, cc * 512 + fc * 128:cc * 512 + (fc + 1) * 128],
                                    lnf(cc)[:, tt * 512:(tt + 1) * 512],
                                    start=(cc == 0), stop=(cc == 15)).after(*aft)
                    b_lnf.readers.append(kmm[tt])
                    if cc == 15:
                        b_wqb[wpar].rd(kmm[tt])
                        r.inc("pe", key=kmm[tt])
                        last_mm = kmm[tt]
            for tt in range(2):
                kraw[tt] = K("a1raw")
                aft = [kmm[tt]] + b_tb1[tt].wr(kraw[tt]) + psb[pbank[tt]].rd(kraw[tt])
                s.ac.activation(TbA1[tt], PS[pbank[tt]][:, :], AF.Copy,
                                scale=(RD if kind == "q" else 1.0)).after(*aft).inc(
                    "ac", key=kraw[tt])
            for tt in range(2):
                if kind == "v":
                    kd = K("dvw")
                    s.dma("sy", f"dvw{tt}", dv_rows[h][:, tt * 512:(tt + 1) * 512],
                          TbA1[tt], key=kd, after=b_tb1[tt].rd(kd) + [kraw[tt]])
                    last_dvw[tt] = kd
                else:
                    krot = K("a1rot")
                    aft = [kraw[tt], k_pmat] + psb[4 + tt].wr(krot) + b_tb1[tt].rd(krot)
                    s.pe.matmul(PS[4 + tt][:, :], pmat_s, TbA1[tt],
                                start=True, stop=True).after(*aft).inc("pe", key=krot)
                    cpar = fcg % 2
                    kt1 = K("a1t1")
                    aft = [kraw[tt], cs_keys[fcg]] + b_cs[cpar].rd(kt1) + \
                        b_tb1[tt].rd(kt1) + b_T1[0].wr(kt1)
                    s.dv.tensor_tensor(T1[:, 0:512], TbA1[tt],
                                       cs_s[cpar][:, tt * 512:(tt + 1) * 512],
                                       ALU.mult).after(*aft).inc("dv", key=kt1)
                    kt2 = K("a1t2")
                    aft = [krot, cs_keys[fcg]] + b_cs[cpar].rd(kt2) + \
                        psb[4 + tt].rd(kt2) + b_T1[1].wr(kt2)
                    s.dv.tensor_tensor(T1[:, 512:1024], PS[4 + tt][:, :],
                                       cs_s[cpar][:, 1024 + tt * 512:1024 + (tt + 1) * 512],
                                       ALU.mult).after(*aft).inc("dv", key=kt2)
                    kqp = K("a1qp")
                    s.dv.tensor_tensor(Qp[tt], T1[:, 0:512], T1[:, 512:1024],
                                       ALU.add).after(kt1, kt2, *(b_qp[tt].wr(kqp) +
                                       b_T1[0].rd(kqp) + b_T1[1].rd(kqp))).inc("dv", key=kqp)
                    dst = dq_rows[h] if kind == "q" else dk_rows[h]
                    kd = K("dqw")
                    s.dma("sy", f"dqw{tt}", dst[:, tt * 512:(tt + 1) * 512],
                          Qp[tt], key=kd, after=b_qp[tt].rd(kd) + [kqp])
                    last_dqw[tt] = kd
            nxt = fcg + 2
            if 2 <= nxt < 32:
                k = K("cs"); cs_keys[nxt] = k
                s.dma("sy", f"cs{nxt % 2}", cs_s[nxt % 2], cs_d[nxt % 16], key=k,
                      after=b_cs[nxt % 2].wr(k))

    tap_ready = {"d_ln": [ln1_w[7]],
                 "d_q": [last_dqw[0], last_dqw[1]],
                 "d_k": [last_dqw[0], last_dqw[1]],
                 "d_v": [last_dvw[0], last_dvw[1]]}
    def _finish(ow_keys):
        fin_keys = list(ow_keys)
        for tname, (tdst, tsrc) in tap_outs.items():
            rdy = tap_ready.get(tname)
            if rdy is None:
                continue
            kt = K("tap")
            s.dma("sy", "tap" + tname, tdst, tsrc, key=kt,
                  after=rdy + list(ow_keys))
            fin_keys.append(kt)
        r = s.sy.nop().after(*fin_keys)
        for name, v in s.cnt.items():
            if name.startswith("D"):
                if r.op.waits.get(name, -1) < v:
                    r.op.waits[name] = v
        build_nc._last_sched = s
        sems = {}
        for name in s.cnt:
            sems[name] = nc.alloc_semaphore("s_" + name.replace(":", "_"))
        with nc.Block("main") as blk:
            def mk(engname):
                def f(e):
                    seen = {}
                    for op in s.q[engname]:
                        for (sem, v) in sorted(op.waits.items()):
                            if seen.get(sem, -1) < v:
                                e.wait_ge(sems[sem], v)
                                seen[sem] = v
                        inst = getattr(e, op.method)(*op.args, **op.kwargs)
                        for (sem, st) in op.incs:
                            inst.then_inc(sems[sem], st)
                return f
            blk.tensor(mk("pe"))
            blk.vector(mk("dv"))
            blk.scalar(mk("ac"))
            blk.sync(mk("sy"))
            blk.gpsimd(mk("gp"))
        build_nc._ldw_skipped = 0
        return nc

    if not ph("a2p"):
        return _finish([])

    kbar = last_mm
    k_ghat = K("ghat")
    s.dma("gp", "ghld", ghat_s, ghat_d, key=k_ghat, max_dma_last_dim=8192)

    b_wpq = [Buf(), Buf()]
    wgp_keys = {}

    def precast_round(i):
        idx, p = divmod(i, 4)
        srcimg = wg_img[idx] if idx < 4 else wp_img[idx - 4]
        kc = K("wgc")
        s.dma("gp", f"wgc{i % 2}", wpq[i % 2],
              srcimg[:, p * 2048:(p + 1) * 2048], key=kc,
              after=b_wpq[i % 2].wr(kc), max_dma_last_dim=8192)
        kw = K("wgw")
        s.dma("sy", f"wgw{i % 2}",
              d_wgp.ap()[idx][:, p * 2048:(p + 1) * 2048], wpq[i % 2],
              key=kw, after=b_wpq[i % 2].rd(kw) + [kc])
        wgp_keys[(idx, p)] = kw

    b_SP = Buf()
    ksr = K("srun0")
    s.dv.memset(SP[:, :], 0.0).inc("dv", key=ksr)
    b_SP.writer = ksr

    b_lnh = Buf()
    b_sgh = [Buf() for _ in range(4)]
    wgq_keys = {}
    lnh_keys = {}
    gse = {}

    def emit_lnh(hf):
        kl = []
        for fc in range(16):
            k = K("lnh"); kl.append(k)
            aft = [ln1_w[7]]
            if fc == 0:
                aft += b_lnh.wr(k)
                if hf == 0:
                    aft += [kbar] + list(b_lnf.readers)
            s.dma("ac", "lnh", lnh(fc),
                  d_ln.ap()[hf * 512:(hf + 1) * 512, fc * 128:(fc + 1) * 128],
                  transpose=True, key=k, after=aft)
        b_lnh.writer = kl[-1]
        lnh_keys[hf] = kl

    def emit_g_c4(hf, c4):
        for tp in range(2):
            for qb in range(2):
                k = K("wg"); wgq_keys[(hf, c4, tp, qb)] = k
                s.dma("sy", f"wpq{qb % 2}", wpq[qb % 2],
                      d_wgp.ap()[c4][:, qb * 2048:(qb + 1) * 2048],
                      key=k, after=b_wpq[qb % 2].wr(k) + [wgp_keys[(c4, qb)]])
            kg = {}
            for cc in range(16):
                qb = cc // 4
                for ti in range(2):
                    tc = tp * 2 + ti
                    if cc == 0:
                        kg[tc] = K("gmm")
                    aft = [wgq_keys[(hf, c4, tp, qb)]]
                    if cc == 0:
                        aft += lnh_keys[hf] + psb[6 + ti].wr(kg[tc])
                    r = s.pe.matmul(PS[6 + ti][:, :],
                                    lnh(cc)[:, tc * 128:(tc + 1) * 128],
                                    wpq[qb % 2][:, (cc % 4) * 512:(cc % 4 + 1) * 512],
                                    start=(cc == 0), stop=(cc == 15)).after(*aft)
                    b_lnh.readers.append(kg[tc])
                    if cc % 4 == 3 and ti == 1 and cc != 15:
                        kqr = K("gqr")
                        r.inc("pe", key=kqr)
                        b_wpq[qb % 2].rd(kqr)
                        nqb = qb + 2
                        if nqb < 4:
                            k2 = K("wg"); wgq_keys[(hf, c4, tp, nqb)] = k2
                            s.dma("sy", f"wpq{nqb % 2}", wpq[nqb % 2],
                                  d_wgp.ap()[c4][:, nqb * 2048:(nqb + 1) * 2048],
                                  key=k2, after=b_wpq[nqb % 2].wr(k2) +
                                  [wgp_keys[(c4, nqb)]])
                    if cc == 15:
                        if ti == 1:
                            b_wpq[qb % 2].rd(kg[tc])
                        r.inc("pe", key=kg[tc])
            for ti in range(2):
                tc = tp * 2 + ti
                ks = K("gse")
                aft = [kg[tc]] + b_sgh[tc].wr(ks) + psb[6 + ti].rd(ks)
                s.ac.activation(sgh(tc)[:, c4 * 512:(c4 + 1) * 512],
                                PS[6 + ti][:, :], AF.Silu).after(*aft).inc(
                    "ac", key=ks)
                gse[(hf, c4, tc)] = ks

    b_qch = [Buf(), Buf()]
    b_kch = [Buf(), Buf()]
    b_ktch = [Buf(), Buf()]
    b_vch = [Buf(), Buf()]
    b_Qhat = Buf()
    b_Zm = [Buf(), Buf()]
    b_Sbf = [Buf(), Buf()]
    b_yp = [Buf() for _ in range(NCH)]

    chunk_keys = {}

    dqk_done = [last_dqw[0], last_dqw[1]]
    dv_done = [last_dvw[0], last_dvw[1]]

    def ld_chunk(c8, parts="qktv"):
        par = c8 % 2
        kq = K("qch"); kk = K("kch"); kt = K("ktch"); kv = K("vch")
        chunk_keys[c8] = dict(q=kq, k=kk, kt=kt, v=kv)
        if "q" in parts:
            s.dma("sy", f"qch{par}", qch[par].rearrange("p (h t) -> p h t", h=16),
                  dq_col(d_q, c8), key=kq,
                  after=b_qch[par].wr(kq) + dqk_done)
        if "k" in parts:
            s.dma("sy", f"kch{par}", kch[par].rearrange("p (h t) -> p h t", h=16),
                  dq_col(d_k, c8), key=kk,
                  after=b_kch[par].wr(kk) + dqk_done)
        if "t" in parts:
            s.dma("ac", f"ktch{par}", ktch[par],
                  d_k.ap()[:, c8 * 128:(c8 + 1) * 128], transpose=True,
                  key=kt, after=b_ktch[par].wr(kt) + dqk_done)
        if "v" in parts:
            s.dma("ac", f"vch{par}", vch[par],
                  d_v.ap()[:, c8 * 128:(c8 + 1) * 128], transpose=True,
                  key=kv, after=b_vch[par].wr(kv) + dv_done)

    if stop_after == "a2p0":
        return _finish([])
    ld_chunk(0)
    if stop_after == "a2p1":
        return _finish([])
    if stop_after == "a2p2":
        ld_chunk(1, parts="qk")
        return _finish([])
    if stop_after == "a2p3":
        ld_chunk(1, parts="tv")
        return _finish([])
    ld_chunk(1)
    if stop_after == "a2p":
        return _finish([])

    sbf_of = {}
    ye_of = {}
    for c8 in range(NCH):
        par = c8 % 2
        ck = chunk_keys[c8]
        kqh = K("a2qh")
        aft = [ck["q"], k_qdecb] + b_Qhat.wr(kqh) + b_qch[par].rd(kqh)
        s.dv.tensor_tensor(Qhat, qch[par], qdecb_s, ALU.mult).after(*aft).inc(
            "dv", key=kqh)
        kkh = K("a2kh")
        aft = [ck["kt"], k_ghat] + b_ktch[par].rd(kkh)
        b_ktch[par].writer = kkh
        b_ktch[par].readers = []
        s.gp.tensor_tensor(ktch[par], ktch[par], ghat_s, ALU.mult).after(
            *aft).inc("gp", key=kkh)
        for grp in range(4):
            pz = (c8 * 4 + grp) % 2
            gs4 = slice(grp * 512, (grp + 1) * 512)
            kz = K("a2z")
            for q in range(4):
                h = grp * 4 + q
                hs = slice(h * 128, (h + 1) * 128)
                aft = [ck["k"], ck["q"]] + b_kch[par].rd(kz) + b_qch[par].rd(kz)
                if q == 0:
                    aft += psb[pz].wr(kz)
                r = s.pe.matmul(PS[pz][:, q * 128:(q + 1) * 128],
                                kch[par][:, hs], qch[par][:, hs],
                                start=True, stop=True).after(*aft)
                if q == 3:
                    r.inc("pe", key=kz)
            kzm = K("a2zm")
            aft = [kz, k_maskT] + b_Zm[pz].wr(kzm) + psb[pz].rd(kzm)
            s.dv.tensor_tensor(ZmS[pz], PS[pz][:, 0:512], maskT_s[:, gs4],
                               ALU.mult).after(*aft).inc("dv", key=kzm)
            ksb = K("a2sb")
            aft = b_Sbf[pz].wr(ksb) + b_SP.rd(ksb)
            s.ac.activation(SbfS[pz], SP[:, gs4], AF.Copy).after(*aft).inc(
                "ac", key=ksb)
            sbf_of[(c8, grp)] = ksb
            kyg = K("a2y2")
            for q in range(4):
                h = grp * 4 + q
                hs = slice(h * 128, (h + 1) * 128)
                qsl = slice(q * 128, (q + 1) * 128)
                aft = [kzm, ck["v"]] + b_Zm[pz].rd(kyg) + b_vch[par].rd(kyg)
                if q == 0:
                    aft += psb[2 + pz].wr(kyg)
                s.pe.matmul(PS[2 + pz][:, qsl], ZmS[pz][:, qsl],
                            vch[par][:, hs], start=True, stop=False).after(*aft)
                aft2 = [kqh, ksb] + b_Qhat.rd(kyg) + b_Sbf[pz].rd(kyg)
                r = s.pe.matmul(PS[2 + pz][:, qsl], Qhat[:, hs],
                                SbfS[pz][:, qsl], start=False,
                                stop=True).after(*aft2)
                if q == 3:
                    r.inc("pe", key=kyg)
            kye = K("a2ye")
            aft = [kyg] + psb[2 + pz].rd(kye)
            if grp == 0:
                aft += b_yp[c8].wr(kye)
            s.ac.activation(ypart(c8)[:, gs4], PS[2 + pz][:, 0:512],
                            AF.Copy).after(*aft).inc("ac", key=kye)
            ye_of[(c8, grp)] = kye
            ksc = K("a2sc")
            for q in range(4):
                h = grp * 4 + q
                hs = slice(h * 128, (h + 1) * 128)
                aft = [kkh, ck["v"]] + b_ktch[par].rd(ksc) + b_vch[par].rd(ksc)
                if q == 0:
                    aft += psb[4 + pz].wr(ksc)
                r = s.pe.matmul(PS[4 + pz][:, q * 128:(q + 1) * 128],
                                ktch[par][:, hs], vch[par][:, hs],
                                start=True, stop=True).after(*aft)
                if q == 3:
                    r.inc("pe", key=ksc)
            for q in range(4):
                h = grp * 4 + q
                hs = slice(h * 128, (h + 1) * 128)
                ksu = K("a2su")
                aft = [ksc, ksb] + psb[4 + pz].rd(ksu)
                b_SP.readers.append(ksu)
                s.dv.scalar_tensor_tensor(SP[:, hs], SP[:, hs], g128[h],
                                          PS[4 + pz][:, q * 128:(q + 1) * 128],
                                          ALU.mult, ALU.add).after(*aft).inc(
                    "dv", key=ksu)
                if h == 15:
                    b_SP.writer = ksu
                    b_SP.readers = []
        if c8 + 2 < NCH:
            ld_chunk(c8 + 2)
        for i in range(c8 * 4, c8 * 4 + 4):
            precast_round(i)
        if c8 == 2:
            emit_lnh(0)
        if stop_after == "a2c0":
            break

    kccw = K("ccw")
    s.dma("sy", "ccw", cc_in.ap().rearrange("(h d) e -> d h e", d=128),
          SP[:, :].rearrange("p (h e) -> p h e", h=16), key=kccw,
          after=b_SP.rd(kccw))
    tap_ready["cc_in"] = [kccw]
    if not ph("p5"):
        return _finish([])

    kcc = K("cc")
    s.gp.collective_compute(
        "AllGather", ALU.bypass, replica_groups=[list(range(NCORES))],
        ins=[cc_in.ap().opt()], outs=[cc_out.ap().opt()]).after(kccw).inc(
        "cc", key=kcc)
    cc_out_r = cc_out.ap().rearrange("(b h d) e -> b d h e", d=128, h=16)
    b_gall = Buf()
    gall_keys = []
    for b in range(NCORES):
        k = K("gall"); gall_keys.append(k)
        aft = [kcc]
        if b == 0:
            aft += b_xs[0].readers + b_xs[1].readers + \
                b_cs[0].readers + b_cs[1].readers + \
                b_Zm[0].readers + b_Zm[1].readers + \
                b_Sbf[0].readers + b_Sbf[1].readers
        s.dma("gp", "gall", gall(b).rearrange("p (h e) -> p h e", h=16),
              cc_out_r[b], key=k, after=aft)
    b_gall.writer = gall_keys[-1]
    b_wtb = Buf()
    b_yn = Buf()
    ptmp_s = f32(48, 2048)
    b_ptmp = Buf()
    pcm = None
    for b in range(NCORES):
        kw = K("wtb")
        aft = b_wtb.wr(kw)
        if b == 0:
            aft += b_Qhat.readers + [b_Qhat.writer]
        s.dma("gp", "wtb", wtb_s, wtab_d[b], key=kw, after=aft)
        if b == 0:
            kp = K("pcm")
            aft = [kccw, kw] + gall_keys + b_SP.wr(kp) + b_wtb.rd(kp) + b_gall.rd(kp)
            s.dv.tensor_tensor(SP[:, :], gall(0), wtb_s, ALU.mult).after(
                *aft).inc("dv", key=kp)
            pcm = kp
        else:
            kt = K()
            aft = [kw] + b_ptmp.wr(kt) + b_wtb.rd(kt) + b_gall.rd(kt)
            if b == 1:
                aft += b_ktch[0].readers + b_ktch[1].readers +                     b_vch[0].readers + b_vch[1].readers +                     [b_ktch[0].writer, b_ktch[1].writer,
                     b_vch[0].writer, b_vch[1].writer]
            s.dv.tensor_tensor(ptmp_s, gall(b), wtb_s, ALU.mult).after(*aft).inc(
                "dv", key=kt)
            kp = K("pcm")
            b_SP.readers.append(kp)
            s.dv.tensor_tensor(SP[:, :], SP[:, :], ptmp_s, ALU.add).after(
                kt, *b_ptmp.rd(kp)).inc("dv", key=kp)
            b_SP.writer = kp
            b_SP.readers = []
            pcm = kp

    kgnw = K("gnw")
    s.dma("gp", "gnw", gnw_s, gnw_d, key=kgnw, after=[pcm] + b_gall.readers)
    kgnb = K("gnb")
    s.dma("gp", "gnb", gnb_s, gnb_d, key=kgnb, after=[pcm])
    b_gnw = Buf(); b_gnw.writer = kgnw
    b_gnb = Buf(); b_gnb.writer = kgnb
    if not ph("post"):
        return _finish([])

    b_gfh = Buf()
    b_gfh.readers += b_ptmp.readers + [b_ptmp.writer]
    xs2 = [f32(96, 2048), f32(104, 2048)]
    Tb2 = bf(112, 2048)
    b_xs2 = [Buf(), Buf()]
    b_Tb2 = Buf()
    ln2_w = []

    def l2_pre2(tc8):
        hf_, tc_ = divmod(tc8, 4)
        return [x2w_keys[(hf_, c4_, tc_)] for c4_ in range(4)]
    b_PT = Buf()
    b_pdec = Buf()
    b_x0 = [Buf(), Buf()]
    b_x2o = [Buf(), Buf()]
    x2w_keys = {}
    grow_keys = {}

    def emit_sweep_tc(hf, tc):
        c8g = hf * 4 + tc
        par = tc % 2
        kq = K("qch2")
        s.dma("sy", f"qch{par}", qch[par].rearrange("p (h t) -> p h t", h=16),
              dq_col(d_q, c8g), key=kq, after=b_qch[par].wr(kq))
        kp = K("pdec")
        aft = b_pdec.wr(kp)
        if hf == 0 and tc == 0:
            aft += b_kch[0].readers + b_kch[1].readers + \
                [b_kch[0].writer, b_kch[1].writer]
        s.dma("sy", "pdec", pdec_s, pdec_d[c8g], key=kp, after=aft)
        kqh = K("qh2")
        aft = [kq, k_qdecb] + b_Qhat.wr(kqh) + b_qch[par].rd(kqh)
        s.dv.tensor_tensor(Qhat, qch[par], qdecb_s, ALU.mult).after(
            *aft).inc("dv", key=kqh)
        kpt = K("ptl")
        aft = [kp, pcm] + b_PT.wr(kpt) + b_pdec.rd(kpt) + b_SP.rd(kpt)
        s.dv.tensor_tensor(PT, SP[:, :], pdec_s, ALU.mult).after(*aft).inc(
            "dv", key=kpt)
        last = None
        for h in range(16):
            hs = slice(h * 128, (h + 1) * 128)
            kmm = K("pth")
            aft = [kqh, kpt] + psb[6 + h % 2].wr(kmm) + \
                b_Qhat.rd(kmm) + b_PT.rd(kmm)
            s.pe.matmul(PS[6 + h % 2][:, 0:128], Qhat[:, hs], PT[:, hs],
                        start=True, stop=True).after(*aft).inc("pe", key=kmm)
            ky = K("ysl")
            aft = [kmm, ye_of[(c8g, h // 4)]] + psb[6 + h % 2].rd(ky)
            b_yp[c8g].readers.append(ky)
            s.dv.tensor_tensor(ypart(c8g)[:, hs], ypart(c8g)[:, hs],
                               PS[6 + h % 2][:, 0:128], ALU.add).after(
                *aft).inc("dv", key=ky)
            b_yp[c8g].writer = ky
            krd = K()
            s.dv.reduce_sum(st_s1[:, h:h + 1], ypart(c8g)[:, hs],
                            AX.X).after(ky).inc("dv", key=krd)
            kst = K("yst")
            r = s.dv.scalar_tensor_tensor(
                Tb[:, 0:128], ypart(c8g)[:, hs], 1.0, ypart(c8g)[:, hs],
                ALU.mult, ALU.mult, accum_out=st_s2[:, h:h + 1])
            r.after(*(b_Tb.wr(kst) + [ky])).inc("dv", key=kst)
            last = (krd, kst)
        kmu_b = K()
        s.dv.tensor_scalar(st_mu[:, 0:16], st_s1[:, 0:16], 1.0 / D, None,
                           ALU.mult).after(*last).inc("dv", key=kmu_b)
        kms_b = K()
        s.dv.scalar_tensor_tensor(st_msq[:, 0:16], st_mu[:, 0:16], 1.0,
                                  st_mu[:, 0:16], ALU.mult, ALU.mult).after(
            kmu_b).inc("dv", key=kms_b)
        kv = K("gnv")
        s.dv.scalar_tensor_tensor(st_var[:, 0:16], st_s2[:, 0:16], 1.0 / D,
                                  st_msq[:, 0:16], ALU.mult,
                                  ALU.subtract).after(kms_b).inc("dv", key=kv)
        kv2 = K()
        s.dv.tensor_scalar(st_var[:, 0:16], st_var[:, 0:16], EPS, None,
                           ALU.add).after(kv).inc("dv", key=kv2)
        kq_ = K("gnq")
        s.ac.activation(st_sq[:, 0:16], st_var[:, 0:16], AF.Sqrt).after(
            kv2).inc("ac", key=kq_)
        kr = K("gnr")
        s.dv.reciprocal(st_rstd[:, 0:16], st_sq[:, 0:16]).after(kq_).inc(
            "dv", key=kr)
        kn_last = None
        for h in range(16):
            hs = slice(h * 128, (h + 1) * 128)
            kn = K()
            aft = [kr, kmu_b] + (b_yn.wr(kn) if h == 0 else [])
            b_yp[c8g].readers.append(kn)
            s.dv.tensor_scalar(yn_s[:, hs], ypart(c8g)[:, hs],
                               st_mu[:, h:h + 1], st_rstd[:, h:h + 1],
                               ALU.subtract, ALU.mult).after(*aft).inc(
                "dv", key=kn)
            kn_last = kn
        ka = K()
        s.dv.tensor_tensor(yn_s, yn_s, gnw_s, ALU.mult).after(
            kgnw, kn_last).inc("dv", key=ka)
        b_gnw.readers.append(ka)
        kb = K()
        s.dv.tensor_tensor(yn_s, yn_s, gnb_s, ALU.add).after(
            kgnb, ka).inc("dv", key=kb)
        b_gnb.readers.append(kb)
        kgr = K("grow")
        aft = [gse[(hf, 3, tc)], kb] + b_Tb.wr(kgr) + b_sgh[tc].rd(kgr)
        s.dv.tensor_tensor(Tb[:, :], yn_s, sgh(tc), ALU.mult).after(
            *aft).inc("dv", key=kgr)
        b_yn.readers.append(kgr)
        kw = K("dgw")
        s.dma("sy", "dgw", dg_t[c8g], Tb[:, :], key=kw,
              after=b_Tb.rd(kw) + [kgr])
        grow_keys[(hf, tc)] = kw

    def emit_gfh(hf):
        kgf = []
        for fc in range(16):
            k = K("gfh"); kgf.append(k)
            aft = [grow_keys[(hf, 3)]]
            if fc == 0:
                aft += b_gfh.wr(k)
                if hf == 0:
                    aft += b_ktch[0].readers + b_ktch[1].readers + \
                        b_vch[0].readers + b_vch[1].readers + \
                        [b_ktch[0].writer, b_ktch[1].writer,
                         b_vch[0].writer, b_vch[1].writer]
            s.dma("ac", "gfh", gfh(fc),
                  d_g.ap()[hf * 512:(hf + 1) * 512, fc * 128:(fc + 1) * 128],
                  transpose=True, key=k, after=aft)
        b_gfh.writer = kgf[-1]
        return kgf

    def emit_proj_c4(hf, c4, gfh_all):
        for qb in range(2):
            k = K("wp"); wgq_keys[("p", hf, c4, qb)] = k
            s.dma("sy", f"wpq{qb % 2}", wpq[qb % 2],
                  d_wgp.ap()[4 + c4][:, qb * 2048:(qb + 1) * 2048],
                  key=k, after=b_wpq[qb % 2].wr(k) + [wgp_keys[(4 + c4, qb)]])
        kpj = {}
        for tc in range(2):
            sl = tc % 2
            kx = K("x0")
            s.dma("sy", f"x0{sl}", x0s[sl],
                  bass.AP(x.tensor, (hf * 512 + tc * 128) * C + c4 * 512,
                          [[C, 128], [1, 512]]),
                  key=kx, after=b_x0[sl].wr(kx))
            wgq_keys[("x0", hf, c4, tc)] = kx
        for cc in range(16):
            qb = cc // 4
            for tc in range(4):
                if cc == 0:
                    kpj[tc] = K("pjmm")
                aft = [wgq_keys[("p", hf, c4, qb)]]
                if c4 == 0 and cc == 0:
                    aft += gfh_all
                if cc == 0:
                    aft += psb[tc].wr(kpj[tc])
                r = s.pe.matmul(PS[tc][:, :],
                                gfh(cc)[:, tc * 128:(tc + 1) * 128],
                                wpq[qb % 2][:, (cc % 4) * 512:(cc % 4 + 1) * 512],
                                start=(cc == 0), stop=(cc == 15)).after(*aft)
                b_gfh.readers.append(kpj[tc])
                if cc % 4 == 3 and tc == 3:
                    if cc == 15:
                        b_wpq[qb % 2].rd(kpj[tc])
                    else:
                        kqr = K("pqr")
                        r.inc("pe", key=kqr)
                        b_wpq[qb % 2].rd(kqr)
                        nqb = qb + 2
                        if nqb < 4:
                            k2 = K("wp"); wgq_keys[("p", hf, c4, nqb)] = k2
                            s.dma("sy", f"wpq{nqb % 2}", wpq[nqb % 2],
                                  d_wgp.ap()[4 + c4][:, nqb * 2048:(nqb + 1) * 2048],
                                  key=k2, after=b_wpq[nqb % 2].wr(k2) +
                                  [wgp_keys[(4 + c4, nqb)]])
                if cc == 15:
                    r.inc("pe", key=kpj[tc])
        for tc in range(4):
            sl = tc % 2
            ke = K("x2e")
            aft = [kpj[tc], wgq_keys[("x0", hf, c4, tc)]] + \
                b_x2o[sl].wr(ke) + psb[tc].rd(ke) + b_x0[sl].rd(ke)
            s.dv.tensor_tensor(x2o[sl], PS[tc][:, :], x0s[sl],
                               ALU.add).after(*aft).inc("dv", key=ke)
            kw = K("x2w")
            s.dma("sy", f"x2w{sl}",
                  bass.AP(d_x2.ap().tensor, (hf * 512 + tc * 128) * C + c4 * 512,
                          [[C, 128], [1, 512]]),
                  x2o[sl], key=kw, after=b_x2o[sl].rd(kw) + [ke])
            x2w_keys[(hf, c4, tc)] = kw
            if tc + 2 < 4:
                kx = K("x0")
                s.dma("sy", f"x0{sl}", x0s[sl],
                      bass.AP(x.tensor,
                              (hf * 512 + (tc + 2) * 128) * C + c4 * 512,
                              [[C, 128], [1, 512]]),
                      key=kx, after=b_x0[sl].wr(kx))
                wgq_keys[("x0", hf, c4, tc + 2)] = kx

    for c4 in range(4):
        emit_g_c4(0, c4)
    emit_lnh(1)
    for tc in range(4):
        emit_sweep_tc(0, tc)
    gfh0 = emit_gfh(0)
    for c4 in range(4):
        emit_g_c4(1, c4)
    for tc in range(4):
        emit_sweep_tc(1, tc)
        emit_proj_c4(0, tc, gfh0)
    ln2_w += ln_pass("L2", dx2_t, dln2_t, l2_pre2, [pcm],
                     tiles=range(0, 4), bufs=(xs2, Tb2, b_xs2, b_Tb2))
    gfh1 = emit_gfh(1)
    for c4 in range(4):
        emit_proj_c4(1, c4, gfh1)
    ln2_w += ln_pass("L2", dx2_t, dln2_t, l2_pre2, [],
                     tiles=range(4, 8), bufs=(xs2, Tb2, b_xs2, b_Tb2))

    tap_ready["d_g"] = [x2w_keys[(1, 3, 3)]]
    tap_ready["d_x2"] = [x2w_keys[(hf_, c4_, tc_)]
                        for hf_ in range(2) for c4_ in range(4) for tc_ in range(4)]
    if not ph("post") or not ph("ffn"):
        return _finish([])

    l2_pre = l2_pre2
    tap_ready["d_ln2"] = [ln2_w[7]]

    b_ln2f = Buf()
    kl2 = []
    for fc in range(16):
        k = K("ln2ld"); kl2.append(k)
        aft = [ln2_w[7]]
        if fc == 0:
            aft += b_lnh.readers + b_sgh[0].readers + b_sgh[1].readers + \
                b_sgh[2].readers + b_sgh[3].readers + b_ln2f.wr(k)
        s.dma("ac", "ln2ld", ln2f(fc), d_ln2.ap()[:, fc * 128:(fc + 1) * 128],
              transpose=True, key=k, after=aft)
    b_ln2f.writer = kl2[-1]
    ln2_all = kl2

    ksd = K("sdl")
    s.dma("sy", "sdl", sdown_s, sdown_d, key=ksd,
          after=b_SP.readers + [b_SP.writer])
    b_sdown = Buf(); b_sdown.writer = ksd

    b_wga = [Buf(), Buf()]
    b_wua = [Buf(), Buf()]
    b_wdn = [Buf(), Buf()]
    b_fuse = [Buf() for _ in range(16)]
    b_gs = [Buf(), Buf()]
    b_acc = [Buf() for _ in range(8)]
    for oc in range(2):
        for t in range(4):
            b_wga[oc].readers += b_yp[t].readers + [b_yp[t].writer]
            b_wua[oc].readers += b_yp[t].readers + [b_yp[t].writer]
    for t in range(4, 8):
        b_wdn[0].readers += b_yp[t].readers + [b_yp[t].writer]
    b_wdn[1].readers += b_Qhat.readers + [b_Qhat.writer] + \
        b_wpq[0].readers + b_wpq[1].readers + \
        [b_wpq[0].writer, b_wpq[1].writer]
    for tt in range(2):
        b_gs[tt].readers += b_Tb.readers + [b_Tb.writer]
    fuse_prime = b_qch[0].readers + b_qch[1].readers + \
        [b_qch[0].writer, b_qch[1].writer] + b_pdec.readers + \
        [b_pdec.writer] + b_gfh.readers + [b_gfh.writer]
    for oc in range(16):
        b_fuse[oc].readers += fuse_prime
    acc_prime = b_gnw.readers + b_gnb.readers + b_yn.readers + \
        b_x0[0].readers + b_x0[1].readers + b_x2o[0].readers + \
        b_x2o[1].readers + b_xs[0].readers + b_xs[1].readers + \
        b_xs2[0].readers + b_xs2[1].readers + b_Tb2.readers + [b_Tb2.writer]
    for t8 in range(8):
        b_acc[t8].readers += acc_prime

    fgs_keys = {}
    ffu_keys = {}
    fdm_keys = {}
    fde_keys = {}
    for og in range(4):
        for oc in range(16):
            ocg = og * 16 + oc
            kwg = K("wga")
            s.dma("gp", f"wga{oc % 2}", wga[oc % 2], wgate_img[ocg], key=kwg,
                  after=b_wga[oc % 2].wr(kwg), max_dma_last_dim=2048)
            kwu = K("wua")
            s.dma("gp", f"wua{oc % 2}", wua[oc % 2], wup_img[ocg], key=kwu,
                  after=b_wua[oc % 2].wr(kwu), max_dma_last_dim=2048)
            kgm = {0: K("fgm"), 1: K("fgm")}
            kum = {0: K("fum"), 1: K("fum")}
            for cc in range(16):
                for tt in range(2):
                    aft = [kwg]
                    if ocg == 0 and tt == 0 and cc == 0:
                        aft += ln2_all
                    if cc == 0:
                        aft += psb[tt].wr(kgm[tt])
                    r = s.pe.matmul(PS[tt][:, :],
                                    wga[oc % 2][:, cc * 128:(cc + 1) * 128],
                                    ln2f(cc)[:, tt * 512:(tt + 1) * 512],
                                    start=(cc == 0), stop=(cc == 15)).after(*aft)
                    b_ln2f.readers.append(kgm[tt])
                    if cc == 15:
                        b_wga[oc % 2].rd(kgm[tt])
                        r.inc("pe", key=kgm[tt])
            for cc in range(16):
                for tt in range(2):
                    aft = [kwu]
                    if cc == 0:
                        aft += psb[2 + tt].wr(kum[tt])
                    r = s.pe.matmul(PS[2 + tt][:, :],
                                    wua[oc % 2][:, cc * 128:(cc + 1) * 128],
                                    ln2f(cc)[:, tt * 512:(tt + 1) * 512],
                                    start=(cc == 0), stop=(cc == 15)).after(*aft)
                    b_ln2f.readers.append(kum[tt])
                    if cc == 15:
                        b_wua[oc % 2].rd(kum[tt])
                        r.inc("pe", key=kum[tt])
            for tt in range(2):
                kgs = K("fgs")
                aft = [kgm[tt], k_sgc] + b_gs[tt].wr(kgs) + psb[tt].rd(kgs)
                s.ac.activation(gss[tt], PS[tt][:, :], AF.Silu,
                                scale=sgc_s[:, ocg:ocg + 1]).after(*aft).inc(
                    "ac", key=kgs)
                fgs_keys[(ocg, tt)] = kgs
                kfu = K("ffu")
                aft = [kum[tt], kgs, k_suc]
                if tt == 0:
                    aft += b_fuse[oc].wr(kfu)
                aft += psb[2 + tt].rd(kfu) + b_gs[tt].rd(kfu)
                s.dv.scalar_tensor_tensor(
                    fuse(oc)[:, tt * 512:(tt + 1) * 512], PS[2 + tt][:, :],
                    suc_s[:, ocg:ocg + 1], gss[tt], ALU.mult, ALU.mult).after(
                    *aft).inc("dv", key=kfu)
                ffu_keys[(ocg, tt)] = kfu
        for ht in range(4):
            kwd = K("wdn")
            s.dma("gp", f"wdn{ht % 2}", wdns[ht % 2], wdown_img[og * 4 + ht],
                  key=kwd, after=b_wdn[ht % 2].wr(kwd), max_dma_last_dim=2048)
            for hf2 in range(2):
                kdm = {}
                for tc in range(4):
                    kdm[tc] = K("fdm")
                for oc in range(16):
                    for tc in range(4):
                        aft = [kwd, ffu_keys[(og * 16 + oc, hf2)]]
                        if oc == 0:
                            aft += psb[4 + tc].wr(kdm[tc])
                        r = s.pe.matmul(
                            PS[4 + tc][:, :],
                            fuse(oc)[:, hf2 * 512 + tc * 128:hf2 * 512 + (tc + 1) * 128],
                            wdns[ht % 2][:, oc * 512:(oc + 1) * 512],
                            start=(oc == 0), stop=(oc == 15)).after(*aft)
                        b_fuse[oc].readers.append(kdm[tc])
                        if oc == 15:
                            b_wdn[ht % 2].rd(kdm[tc])
                            r.inc("pe", key=kdm[tc])
                for tc in range(4):
                    t8 = hf2 * 4 + tc
                    accsl = acc(t8)[:, ht * 512:(ht + 1) * 512]
                    ke = K("fde")
                    aft = [kdm[tc]] + psb[4 + tc].rd(ke)
                    if og == 0:
                        aft += b_acc[t8].wr(ke)
                        r = s.dv.tensor_copy(accsl, PS[4 + tc][:, :]).after(*aft)
                    else:
                        b_acc[t8].readers.append(ke)
                        r = s.dv.tensor_tensor(accsl, accsl, PS[4 + tc][:, :],
                                               ALU.add).after(*aft)
                        b_acc[t8].writer = ke
                        b_acc[t8].readers = []
                    r.inc("dv", key=ke)
                    fde_keys[(og, ht, hf2, tc)] = ke
    b_x2f = [Buf(), Buf()]
    b_fin = [Buf(), Buf()]
    prime_fin = []
    for bb in (b_wga[0], b_wga[1], b_wua[0], b_wua[1], b_wdn[0], b_gs[0], b_gs[1]):
        prime_fin += bb.readers + ([bb.writer] if bb.writer else [])
    b_x2f[0].readers += prime_fin
    b_fin[0].readers += prime_fin
    b_fin[1].readers += prime_fin
    b_x2f[1].readers += prime_fin
    ow_keys = []
    for t8 in range(8):
        par = t8 % 2
        kx = K("x2f")
        aft = l2_pre(t8) + b_x2f[par].wr(kx)
        s.dma("sy", f"x2f{par}", x2f[par], dx2_t[t8], key=kx, after=aft)
        km = K("finm")
        aft = [fde_keys[(3, ht, t8 // 4, t8 % 4)] for ht in range(4)] + \
            [ksd] + b_fin[par].wr(km) + b_acc[t8].rd(km) + b_sdown.rd(km)
        s.dv.tensor_tensor(fin[par], acc(t8), sdown_s, ALU.mult).after(
            *aft).inc("dv", key=km)
        kf = K("fin")
        s.dv.tensor_tensor(fin[par], fin[par], x2f[par], ALU.add).after(
            kx, km, *b_x2f[par].rd(kf)).inc("dv", key=kf)
        ko = K("ow")
        s.dma("sy", f"ow{par}", out_t[t8], fin[par], key=ko,
              after=b_fin[par].rd(ko) + [kf])
        ow_keys.append(ko)

    return _finish(ow_keys)


def kernel(L0_attn_x, L0_attn_w_qkv, L0_attn_w_g, L0_attn_w_proj,
           L0_attn_gn_weight, L0_attn_gn_bias, L0_ffn_w_gate, L0_ffn_w_up,
           L0_ffn_w_down, L0_ffn_s_gate, L0_ffn_s_up, L0_ffn_s_down):
    x_full = np.ascontiguousarray(np.asarray(L0_attn_x, np.float32).reshape(T, C))
    sh = _host_shared(L0_attn_w_qkv, L0_attn_w_g, L0_attn_w_proj,
                      L0_ffn_w_gate, L0_ffn_w_up, L0_ffn_w_down,
                      L0_attn_gn_weight, L0_attn_gn_bias,
                      L0_ffn_s_gate, L0_ffn_s_up, L0_ffn_s_down)
    cos, sin = _host_xpos()

    nc = build_nc(taps=tuple(_DEBUG_TAPS))

    in_maps = []
    for core in range(NCORES):
        pc = _host_percore(core, x_full, cos, sin)
        m = dict(x=pc["x"], cs_img=pc["cs_img"], wtab=pc["wtab"],
                 wqkv_img=sh["wqkv_img"], wg_img=sh["wg_img"], wp_img=sh["wp_img"],
                 wgate_img=sh["wgate_img"], wup_img=sh["wup_img"],
                 wdown_img=sh["wdown_img"], maskT=sh["maskT"], qdecb=sh["qdecb"],
                 ghat=sh["ghat"], pdec=sh["pdec"], pmat=sh["pmat"],
                 gnw_b=sh["gnw_b"], gnb_b=sh["gnb_b"], sdown_b=sh["sdown_b"],
                 sg_c=sh["sg_c"], su_c=sh["su_c"])
        in_maps.append(m)

    res = run_bass_kernel_spmd(nc, in_maps, core_ids=list(range(NCORES)))
    outs = [res.results[c]["out"] for c in range(NCORES)]
    full = np.concatenate(outs, axis=0).reshape(1, T, C).astype(np.float32)
    if _DEBUG_TAPS:
        kernel._tap_results = [
            {k: v for k, v in res.results[c].items() if k.startswith("tap_")}
            for c in range(NCORES)]
    return full


# version 10
# speedup vs baseline: 1.0559x; 1.0339x over previous
import numpy as np
import concourse.bass as bass
import concourse.mybir as mybir
from concourse.bass_utils import run_bass_kernel_spmd

F32 = mybir.dt.float32
BF16 = mybir.dt.bfloat16
I8 = mybir.dt.int8
AF = mybir.ActivationFunctionType
ALU = mybir.AluOpType
AX = mybir.AxisListType

NCORES = 8
T = 8192
C = 2048
H = 16
D = 128
TPC = T // NCORES
NCH = TPC // 128
INTER = 8192
EPS = 1e-5
RD = float(D) ** -0.5

_DEBUG_TAPS = []

KBf = 256


ENGS = ("pe", "dv", "ac", "sy", "gp")


class _Op:
    __slots__ = ("method", "args", "kwargs", "waits", "incs")

    def __init__(self, method, args, kwargs):
        self.method = method
        self.args = args
        self.kwargs = kwargs
        self.waits = {}
        self.incs = []


class _OpRef:
    def __init__(self, s, op):
        self.s = s
        self.op = op

    def after(self, *keys):
        for k in keys:
            if k is None:
                continue
            if isinstance(k, list):
                self.after(*k)
                continue
            sem, v = self.s.tick[k]
            if self.op.waits.get(sem, -1) < v:
                self.op.waits[sem] = v
        return self

    def inc(self, sem, key=None, dma=False):
        step = 16 if dma else 1
        self.s.cnt[sem] = self.s.cnt.get(sem, 0) + step
        v = self.s.cnt[sem]
        self.op.incs.append((sem, step))
        if key is not None:
            keys = key if isinstance(key, list) else [key]
            for k in keys:
                assert k not in self.s.tick, f"dup key {k}"
                self.s.tick[k] = (sem, v)
        return self


class _EngProxy:
    def __init__(self, s, eng):
        self._s = s
        self._e = eng

    def __getattr__(self, m):
        def call(*a, **kw):
            op = _Op(m, a, kw)
            self._s.q[self._e].append(op)
            return _OpRef(self._s, op)
        return call


class Sched:
    def __init__(self):
        self.q = {e: [] for e in ENGS}
        self.cnt = {}
        self.tick = {}
        self.nkey = 0
        for e in ENGS:
            setattr(self, e, _EngProxy(self, e))

    def key(self, tag=""):
        self.nkey += 1
        return (tag, self.nkey)

    def dma(self, eng, slot, out, in_, key=None, after=(), **kw):
        r = getattr(self, eng).dma_start(out=out, in_=in_, **kw)
        r.after(*after)
        r.inc("D" + slot, key=key, dma=True)
        return r


class Buf:

    def __init__(self):
        self.writer = None
        self.readers = []

    def wr(self, key):
        aft = list(self.readers) if self.readers else (
            [self.writer] if self.writer is not None else [])
        self.writer = key
        self.readers = []
        return aft

    def rd(self, key):
        self.readers.append(key)
        return [self.writer] if self.writer is not None else []


def _gammas():
    return np.array([1.0 - 2.0 ** (-5.0 - h) for h in range(H)], dtype=np.float64)


def _host_shared(wqkv, wg, wp, wgate, wup, wdown, gnw, gnb, sg, su, sd):
    g = _gammas()
    sh = {}
    sh["wqkv_img"] = np.ascontiguousarray(
        np.asarray(wqkv, np.float32).reshape(16, 128, 12, 512)
        .transpose(2, 1, 0, 3).reshape(12, 128, 8192))
    sh["wg_img"] = np.ascontiguousarray(
        np.asarray(wg, np.float32).reshape(16, 128, 4, 512)
        .transpose(2, 1, 0, 3).reshape(4, 128, 8192))
    sh["wp_img"] = np.ascontiguousarray(
        np.asarray(wp, np.float32).reshape(16, 128, 4, 512)
        .transpose(2, 1, 0, 3).reshape(4, 128, 8192))
    sh["wgate_img"] = np.ascontiguousarray(
        np.asarray(wgate).reshape(16, 128, 64, 128)
        .transpose(2, 1, 0, 3).reshape(64, 128, 2048))
    sh["wup_img"] = np.ascontiguousarray(
        np.asarray(wup).reshape(16, 128, 64, 128)
        .transpose(2, 1, 0, 3).reshape(64, 128, 2048))
    sh["wdown_img"] = np.ascontiguousarray(
        np.asarray(wdown).reshape(4, 16, 128, 4, 512)
        .transpose(0, 3, 2, 1, 4).reshape(16, 128, 8192))
    j = np.arange(128, dtype=np.float64)
    maskT = np.zeros((128, H * 128), dtype=np.float64)
    for h in range(H):
        e = j[None, :] - j[:, None]
        maskT[:, h * 128:(h + 1) * 128] = np.where(e >= 0, g[h] ** np.maximum(e, 0), 0.0)
    sh["maskT"] = maskT.astype(np.float32)
    qd = np.concatenate([g[h] ** (j + 1.0) for h in range(H)])
    sh["qdecb"] = np.broadcast_to(qd.astype(np.float32)[None, :], (128, H * 128)).copy()
    gh = np.zeros((128, H * 128), dtype=np.float64)
    for h in range(H):
        gh[:, h * 128:(h + 1) * 128] = (g[h] ** (127.0 - j))[:, None]
    sh["ghat"] = gh.astype(np.float32)
    pdec = np.zeros((NCH, 128, H * 128), dtype=np.float32)
    for c8 in range(NCH):
        row = np.concatenate([np.full(128, g[h] ** (128.0 * c8)) for h in range(H)])
        pdec[c8] = row.astype(np.float32)[None, :]
    sh["pdec"] = pdec
    P = np.zeros((128, 128), dtype=np.float32)
    ii = np.arange(0, 128, 2)
    P[ii + 1, ii] = -1.0
    P[ii, ii + 1] = 1.0
    sh["pmat"] = P
    sh["gnw_b"] = np.broadcast_to(np.asarray(gnw, np.float32).reshape(1, C), (128, C)).copy()
    sh["gnb_b"] = np.broadcast_to(np.asarray(gnb, np.float32).reshape(1, C), (128, C)).copy()
    sh["sdown_b"] = np.broadcast_to(np.asarray(sd, np.float32).reshape(1, C), (128, C)).copy()
    sh["sg_c"] = np.ascontiguousarray(np.asarray(sg, np.float32).reshape(64, 128).T)
    sh["su_c"] = np.ascontiguousarray(np.asarray(su, np.float32).reshape(64, 128).T)
    return sh


def _host_xpos():
    i = np.arange(C // 2, dtype=np.float64)
    zeta = (2.0 * i + 0.4 * C) / (1.4 * C)
    invf = 1.0 / (10000.0 ** (2.0 * i / C))
    t = np.arange(T, dtype=np.float64)
    scale = zeta[None, :] ** (t[:, None] / 512.0)
    ang = t[:, None] * invf[None, :]
    cos = np.repeat(np.cos(ang) * scale, 2, axis=1).astype(np.float32)
    sin = np.repeat(np.sin(ang) * scale, 2, axis=1).astype(np.float32)
    return cos, sin


def _host_percore(core, x_full, cos, sin):
    g = _gammas()
    pc = {}
    pc["x"] = np.ascontiguousarray(x_full[core * TPC:(core + 1) * TPC])
    cs = np.empty((16, 128, 2048), dtype=np.float32)
    csl = cos[core * TPC:(core + 1) * TPC]
    ssl = sin[core * TPC:(core + 1) * TPC]
    for fc in range(16):
        cs[fc, :, 0:1024] = csl[:, fc * 128:(fc + 1) * 128].T
        cs[fc, :, 1024:2048] = ssl[:, fc * 128:(fc + 1) * 128].T
    pc["cs_img"] = cs
    wt = np.zeros((NCORES, 128, H * 128), dtype=np.float32)
    for b in range(core):
        row = np.concatenate(
            [np.full(128, g[h] ** (1024.0 * (core - 1 - b))) for h in range(H)])
        wt[b] = row.astype(np.float32)[None, :]
    pc["wtab"] = wt
    return pc


def build_nc(taps=(), stop_after="ffn"):
    PH = ["a1", "a2p0", "a2p1", "a2p2", "a2p3", "a2p", "a2c0", "a2", "p5", "post", "ffn"]
    lim = PH.index(stop_after)

    def ph(name):
        return PH.index(name) <= lim

    nc = bass.Bass("TRN2", target_bir_lowering=False)
    g64 = _gammas()
    g128 = [float(np.float32(g64[h] ** 128.0)) for h in range(H)]

    x = nc.dram_tensor("x", [TPC, C], F32, kind="ExternalInput").ap()
    out = nc.dram_tensor("out", [TPC, C], F32, kind="ExternalOutput").ap()
    wqkv_img = nc.dram_tensor("wqkv_img", [12, 128, 8192], F32, kind="ExternalInput").ap()
    wg_img = nc.dram_tensor("wg_img", [4, 128, 8192], F32, kind="ExternalInput").ap()
    wp_img = nc.dram_tensor("wp_img", [4, 128, 8192], F32, kind="ExternalInput").ap()
    wgate_img = nc.dram_tensor("wgate_img", [64, 128, 2048], I8, kind="ExternalInput").ap()
    wup_img = nc.dram_tensor("wup_img", [64, 128, 2048], I8, kind="ExternalInput").ap()
    wdown_img = nc.dram_tensor("wdown_img", [16, 128, 8192], I8, kind="ExternalInput").ap()
    maskT_d = nc.dram_tensor("maskT", [128, 2048], F32, kind="ExternalInput").ap()
    qdecb_d = nc.dram_tensor("qdecb", [128, 2048], F32, kind="ExternalInput").ap()
    ghat_d = nc.dram_tensor("ghat", [128, 2048], F32, kind="ExternalInput").ap()
    pdec_d = nc.dram_tensor("pdec", [NCH, 128, 2048], F32, kind="ExternalInput").ap()
    pmat_d = nc.dram_tensor("pmat", [128, 128], F32, kind="ExternalInput").ap()
    gnw_d = nc.dram_tensor("gnw_b", [128, 2048], F32, kind="ExternalInput").ap()
    gnb_d = nc.dram_tensor("gnb_b", [128, 2048], F32, kind="ExternalInput").ap()
    sdown_d = nc.dram_tensor("sdown_b", [128, 2048], F32, kind="ExternalInput").ap()
    sg_d = nc.dram_tensor("sg_c", [128, 64], F32, kind="ExternalInput").ap()
    su_d = nc.dram_tensor("su_c", [128, 64], F32, kind="ExternalInput").ap()
    cs_d = nc.dram_tensor("cs_img", [16, 128, 2048], F32, kind="ExternalInput").ap()
    wtab_d = nc.dram_tensor("wtab", [NCORES, 128, 2048], F32, kind="ExternalInput").ap()

    d_ln = nc.dram_tensor("d_ln", [TPC, C], BF16)
    d_q = nc.dram_tensor("d_q", [C, TPC], BF16)
    d_k = nc.dram_tensor("d_k", [C, TPC], BF16)
    d_v = nc.dram_tensor("d_v", [C, TPC], BF16)
    d_g = nc.dram_tensor("d_g", [TPC, C], BF16)
    d_x2 = nc.dram_tensor("d_x2", [TPC, C], F32)
    d_ln2 = nc.dram_tensor("d_ln2", [TPC, C], BF16)
    d_wgp = nc.dram_tensor("d_wgp", [8, 128, 8192], BF16)
    cc_in = nc.dram_tensor("cc_in", [H * 128, 128], F32)
    cc_out = nc.dram_tensor("cc_out", [NCORES * H * 128, 128], F32)

    tap_outs = {}
    for tname in taps:
        src = {"d_ln": d_ln, "d_q": d_q, "d_k": d_k, "d_v": d_v, "d_g": d_g,
               "d_x2": d_x2, "d_ln2": d_ln2, "cc_in": cc_in,
               "cc_out": cc_out}.get(tname)
        if src is not None:
            tap_outs[tname] = (
                nc.dram_tensor("tap_" + tname, list(src.shape), src.dtype,
                               kind="ExternalOutput").ap(), src.ap())

    ARENA_KB = 204
    ar = nc.alloc_sbuf_tensor("arena", [128, ARENA_KB * KBf], F32)
    arb = ar.bitcast(BF16)

    def f32(kb, n):
        o = int(kb * KBf)
        return ar[:, o:o + n]

    def bf(kb, n):
        o = int(kb * KBf) * 2
        return arb[:, o:o + n]

    lnf = lambda cc: bf(0, 16 * 1024)[:, cc * 1024:(cc + 1) * 1024]
    ghat_s = bf(0, 2048)
    ZmS = [bf(48, 512), bf(49, 512)]
    lnh = lambda fc: bf(0, 16 * 512)[:, fc * 512:(fc + 1) * 512]
    sgh = lambda tc: bf(16, 4 * 2048)[:, tc * 2048:(tc + 1) * 2048]
    ln2f = lambda cc: bf(0, 16 * 1024)[:, cc * 1024:(cc + 1) * 1024]
    qch = [bf(32, 2048), bf(36, 2048)]
    kch = [bf(40, 2048), bf(44, 2048)]
    ktch = [bf(48, 2048), bf(52, 2048)]
    vch = [bf(56, 2048), bf(60, 2048)]
    pdec_s = f32(40, 2048)
    gfh = lambda fc: bf(48, 16 * 512)[:, fc * 512:(fc + 1) * 512]
    fuse = lambda oc: bf(32, 16 * 1024)[:, oc * 1024:(oc + 1) * 1024]
    xs = [f32(64, 2048), f32(72, 2048)]
    cs_s = [f32(80, 2048), f32(88, 2048)]
    gall = lambda b: f32(64, 8 * 2048)[:, b * 2048:(b + 1) * 2048]
    gnw_s = f32(64, 2048)
    gnb_s = f32(72, 2048)
    x0s = [f32(80, 512), f32(82, 512)]
    x2o = [f32(84, 512), f32(86, 512)]
    yn_s = f32(88, 2048)
    acc = lambda t8: f32(64, 8 * 2048)[:, t8 * 2048:(t8 + 1) * 2048]
    wqb = [bf(128, 8192), bf(144, 8192)]
    ypart = lambda c8: bf(128, 8 * 2048)[:, c8 * 2048:(c8 + 1) * 2048]
    wga = [bf(128, 2048), bf(132, 2048)]
    wua = [bf(136, 2048), bf(140, 2048)]
    wdns = [bf(144, 8192), bf(160, 8192)]
    x2f = [f32(128, 2048), f32(152, 2048)]
    fin = [f32(136, 2048), f32(144, 2048)]
    maskT_s = bf(160, 2048)
    qdecb_s = bf(164, 2048)
    wpq = [bf(168, 2048), bf(172, 2048)]
    SP = f32(176, 2048)
    sdown_s = f32(176, 2048)
    PT = bf(184, 2048)
    wtb_s = f32(184, 2048)
    Qhat = bf(188, 2048)
    Tb = bf(192, 2048)
    TbA1 = [bf(192, 512), bf(193, 512)]
    Qp = [bf(194, 512), bf(195, 512)]
    gss = [bf(192, 512), bf(193, 512)]
    T1 = f32(196, 1024)
    st_s1 = f32(200, 64)
    st_s2 = f32(200.25, 64)
    st_mu = f32(200.5, 64)
    st_msq = f32(200.75, 64)
    st_var = f32(201.0, 64)
    st_sq = f32(201.25, 64)
    st_rstd = f32(201.5, 64)
    sgc_s = f32(201.75, 64)
    suc_s = f32(202.0, 64)
    pmat_s = bf(202.25, 128)

    PS = [nc.alloc_psum_tensor(f"ps{i}", [128, 512], F32) for i in range(8)]

    s = Sched()
    K = s.key

    psb = [Buf() for _ in range(8)]

    x_t = x.rearrange("(n p) c -> n p c", p=128)
    out_t = out.rearrange("(n p) c -> n p c", p=128)
    dln_t = d_ln.ap().rearrange("(n p) c -> n p c", p=128)
    dg_t = d_g.ap().rearrange("(n p) c -> n p c", p=128)
    dx2_t = d_x2.ap().rearrange("(n p) c -> n p c", p=128)
    dln2_t = d_ln2.ap().rearrange("(n p) c -> n p c", p=128)
    dq_rows = d_q.ap().rearrange("(n p) t -> n p t", p=128)
    dk_rows = d_k.ap().rearrange("(n p) t -> n p t", p=128)
    dv_rows = d_v.ap().rearrange("(n p) t -> n p t", p=128)

    def dq_col(dram, c8):
        return dram.ap().rearrange("(h d) t -> d h t", d=128)[:, :, c8 * 128:(c8 + 1) * 128]

    k_maskT = K("maskT"); s.dma("gp", "tb", maskT_s, maskT_d, key=k_maskT, max_dma_last_dim=8192)
    k_qdecb = K("qdecb"); s.dma("gp", "tb2", qdecb_s, qdecb_d, key=k_qdecb, max_dma_last_dim=8192)
    k_pmat = K("pmat"); s.dma("gp", "tb3", pmat_s, pmat_d, key=k_pmat, max_dma_last_dim=8192)
    k_sgc = K("sgc"); s.dma("sy", "tb4", sgc_s, sg_d, key=k_sgc)
    k_suc = K("suc"); s.dma("sy", "tb5", suc_s, su_d, key=k_suc)

    wq_keys = {}
    for half in range(2):
        k = K("wq"); wq_keys[(0, half)] = k
        s.dma("gp", f"wq{0}{half}", wqb[0][:, half * 4096:(half + 1) * 4096],
              wqkv_img[0][:, half * 4096:(half + 1) * 4096], key=k,
              max_dma_last_dim=8192)
    b_wqb = [Buf(), Buf()]
    b_wqb[0].writer = wq_keys[(0, 1)]

    cs_keys = {}
    b_cs = [Buf(), Buf()]
    for fcg in range(2):
        k = K("cs"); cs_keys[fcg] = k
        s.dma("sy", f"cs{fcg}", cs_s[fcg], cs_d[fcg], key=k)
        b_cs[fcg].writer = k

    b_xs = [Buf(), Buf()]
    b_Tb = Buf()

    def ln_pass(tag, src_t, dst_t, src_pre, first_extra, tiles=range(8),
                bufs=None):
        if bufs is None:
            xs_l, Tb_l, b_xs_l, b_Tb_l = xs, Tb, b_xs, b_Tb
        else:
            xs_l, Tb_l, b_xs_l, b_Tb_l = bufs
        wkeys = []
        for tc in tiles:
            sl = tc % 2
            kx = K(tag + "x")
            aft = b_xs_l[sl].wr(kx) + (src_pre(tc) or []) + \
                (first_extra if tc < min(tiles) + 2 else [])
            s.dma("sy", f"{tag}x{sl}", xs_l[sl], src_t[tc], key=kx, after=aft)
            k1 = K(); s.dv.reduce_sum(st_s1[:, tc:tc + 1], xs_l[sl], AX.X).after(
                *b_xs_l[sl].rd(k1)).inc("dv", key=k1)
            k2 = K(); r = s.dv.scalar_tensor_tensor(
                Tb_l[:, :], xs_l[sl], 1.0, xs_l[sl], ALU.mult, ALU.mult,
                accum_out=st_s2[:, tc:tc + 1])
            r.after(*(b_xs_l[sl].rd(k2) + b_Tb_l.wr(k2))).inc("dv", key=k2)
            kmu = K(); s.dv.tensor_scalar(st_mu[:, tc:tc + 1], st_s1[:, tc:tc + 1],
                                          1.0 / C, None, ALU.mult).after(k1).inc(
                "dv", key=kmu)
            kms = K(); s.dv.scalar_tensor_tensor(
                st_msq[:, tc:tc + 1], st_mu[:, tc:tc + 1], 1.0,
                st_mu[:, tc:tc + 1], ALU.mult, ALU.mult).after(kmu).inc("dv", key=kms)
            kv = K(); s.dv.scalar_tensor_tensor(
                st_var[:, tc:tc + 1], st_s2[:, tc:tc + 1], 1.0 / C,
                st_msq[:, tc:tc + 1], ALU.mult, ALU.subtract).after(
                k2, kms).inc("dv", key=kv)
            kv2 = K(); s.dv.tensor_scalar(st_var[:, tc:tc + 1], st_var[:, tc:tc + 1],
                                          EPS, None, ALU.add).after(kv).inc("dv", key=kv2)
            kq = K(); s.ac.activation(st_sq[:, tc:tc + 1], st_var[:, tc:tc + 1],
                                      AF.Sqrt).after(kv2).inc("ac", key=kq)
            krc = K(); s.dv.reciprocal(st_rstd[:, tc:tc + 1],
                                       st_sq[:, tc:tc + 1]).after(kq).inc("dv", key=krc)
            kn = K(tag + "N")
            r = s.dv.tensor_scalar(Tb_l[:, :], xs_l[sl], st_mu[:, tc:tc + 1],
                                   st_rstd[:, tc:tc + 1], ALU.subtract, ALU.mult)
            r.after(*(b_xs_l[sl].rd(kn) + b_Tb_l.wr(kn) + [krc, kmu])).inc("dv", key=kn)
            kw = K(tag + "w")
            s.dma("sy", tag + "w", dst_t[tc], Tb_l[:, :], key=kw,
                  after=b_Tb_l.rd(kw) + [kn])
            wkeys.append(kw)
        return wkeys

    ln1_w = ln_pass("L1", x_t, dln_t, lambda tc: [], [])

    b_lnf = Buf()
    klnf = []
    for fc in range(16):
        k = K("lnfld"); klnf.append(k)
        s.dma("ac", "lnfld", lnf(fc), d_ln.ap()[:, fc * 128:(fc + 1) * 128],
              transpose=True, key=k, after=[ln1_w[7]])
    b_lnf.writer = klnf[-1]
    lnf_all = klnf

    b_tb1 = [Buf(), Buf()]
    b_qp = [Buf(), Buf()]
    b_T1 = [Buf(), Buf()]
    for _b in b_tb1 + b_qp:
        _b.readers += b_Tb.readers + [b_Tb.writer]
    last_dqw = {0: None, 1: None}
    last_dvw = {0: None, 1: None}

    for f4 in range(12):
        wpar = f4 % 2
        if f4 + 1 < 12:
            npar = (f4 + 1) % 2
            for half in range(2):
                k = K("wq"); wq_keys[(f4 + 1, half)] = k
                aft = b_wqb[npar].wr(k) if half == 0 else []
                s.dma("gp", f"wq{npar}{half}",
                      wqb[npar][:, half * 4096:(half + 1) * 4096],
                      wqkv_img[f4 + 1][:, half * 4096:(half + 1) * 4096],
                      key=k, after=aft, max_dma_last_dim=8192)
        for fc in range(4):
            fcg = f4 * 4 + fc
            h = fcg % 16
            kind = "q" if fcg < 16 else ("k" if fcg < 32 else "v")
            kmm = {0: K("a1mm"), 1: K("a1mm")}
            kraw = {}
            pbank = {0: (fc % 2) * 2, 1: (fc % 2) * 2 + 1}
            for cc in range(16):
                half = 0 if cc < 8 else 1
                for tt in range(2):
                    aft = [wq_keys[(f4, half)]]
                    if f4 == 0 and fc == 0 and tt == 0 and cc == 0:
                        aft += lnf_all
                    if cc == 0:
                        aft += psb[pbank[tt]].wr(kmm[tt])
                    r = s.pe.matmul(PS[pbank[tt]][:, :],
                                    wqb[wpar][:, cc * 512 + fc * 128:cc * 512 + (fc + 1) * 128],
                                    lnf(cc)[:, tt * 512:(tt + 1) * 512],
                                    start=(cc == 0), stop=(cc == 15)).after(*aft)
                    b_lnf.readers.append(kmm[tt])
                    if cc == 15:
                        b_wqb[wpar].rd(kmm[tt])
                        r.inc("pe", key=kmm[tt])
                        last_mm = kmm[tt]
            for tt in range(2):
                kraw[tt] = K("a1raw")
                aft = [kmm[tt]] + b_tb1[tt].wr(kraw[tt]) + psb[pbank[tt]].rd(kraw[tt])
                s.ac.activation(TbA1[tt], PS[pbank[tt]][:, :], AF.Copy,
                                scale=(RD if kind == "q" else 1.0)).after(*aft).inc(
                    "ac", key=kraw[tt])
            for tt in range(2):
                if kind == "v":
                    kd = K("dvw")
                    s.dma("sy", f"dvw{tt}", dv_rows[h][:, tt * 512:(tt + 1) * 512],
                          TbA1[tt], key=kd, after=b_tb1[tt].rd(kd) + [kraw[tt]])
                    last_dvw[tt] = kd
                else:
                    krot = K("a1rot")
                    aft = [kraw[tt], k_pmat] + psb[4 + tt].wr(krot) + b_tb1[tt].rd(krot)
                    s.pe.matmul(PS[4 + tt][:, :], pmat_s, TbA1[tt],
                                start=True, stop=True).after(*aft).inc("pe", key=krot)
                    cpar = fcg % 2
                    kt1 = K("a1t1")
                    aft = [kraw[tt], cs_keys[fcg]] + b_cs[cpar].rd(kt1) + \
                        b_tb1[tt].rd(kt1) + b_T1[0].wr(kt1)
                    s.dv.tensor_tensor(T1[:, 0:512], TbA1[tt],
                                       cs_s[cpar][:, tt * 512:(tt + 1) * 512],
                                       ALU.mult).after(*aft).inc("dv", key=kt1)
                    kt2 = K("a1t2")
                    aft = [krot, cs_keys[fcg]] + b_cs[cpar].rd(kt2) + \
                        psb[4 + tt].rd(kt2) + b_T1[1].wr(kt2)
                    s.dv.tensor_tensor(T1[:, 512:1024], PS[4 + tt][:, :],
                                       cs_s[cpar][:, 1024 + tt * 512:1024 + (tt + 1) * 512],
                                       ALU.mult).after(*aft).inc("dv", key=kt2)
                    kqp = K("a1qp")
                    s.dv.tensor_tensor(Qp[tt], T1[:, 0:512], T1[:, 512:1024],
                                       ALU.add).after(kt1, kt2, *(b_qp[tt].wr(kqp) +
                                       b_T1[0].rd(kqp) + b_T1[1].rd(kqp))).inc("dv", key=kqp)
                    dst = dq_rows[h] if kind == "q" else dk_rows[h]
                    kd = K("dqw")
                    s.dma("sy", f"dqw{tt}", dst[:, tt * 512:(tt + 1) * 512],
                          Qp[tt], key=kd, after=b_qp[tt].rd(kd) + [kqp])
                    last_dqw[tt] = kd
            nxt = fcg + 2
            if 2 <= nxt < 32:
                k = K("cs"); cs_keys[nxt] = k
                s.dma("sy", f"cs{nxt % 2}", cs_s[nxt % 2], cs_d[nxt % 16], key=k,
                      after=b_cs[nxt % 2].wr(k))

    tap_ready = {"d_ln": [ln1_w[7]],
                 "d_q": [last_dqw[0], last_dqw[1]],
                 "d_k": [last_dqw[0], last_dqw[1]],
                 "d_v": [last_dvw[0], last_dvw[1]]}
    def _finish(ow_keys):
        fin_keys = list(ow_keys)
        for tname, (tdst, tsrc) in tap_outs.items():
            rdy = tap_ready.get(tname)
            if rdy is None:
                continue
            kt = K("tap")
            s.dma("sy", "tap" + tname, tdst, tsrc, key=kt,
                  after=rdy + list(ow_keys))
            fin_keys.append(kt)
        r = s.sy.nop().after(*fin_keys)
        for name, v in s.cnt.items():
            if name.startswith("D"):
                if r.op.waits.get(name, -1) < v:
                    r.op.waits[name] = v
        build_nc._last_sched = s
        sems = {}
        for name in s.cnt:
            sems[name] = nc.alloc_semaphore("s_" + name.replace(":", "_"))
        with nc.Block("main") as blk:
            def mk(engname):
                def f(e):
                    seen = {}
                    for op in s.q[engname]:
                        for (sem, v) in sorted(op.waits.items()):
                            if seen.get(sem, -1) < v:
                                e.wait_ge(sems[sem], v)
                                seen[sem] = v
                        inst = getattr(e, op.method)(*op.args, **op.kwargs)
                        for (sem, st) in op.incs:
                            inst.then_inc(sems[sem], st)
                return f
            blk.tensor(mk("pe"))
            blk.vector(mk("dv"))
            blk.scalar(mk("ac"))
            blk.sync(mk("sy"))
            blk.gpsimd(mk("gp"))
        return nc

    if not ph("a2p"):
        return _finish([])

    kbar = last_mm
    k_ghat = K("ghat")
    s.dma("gp", "ghld", ghat_s, ghat_d, key=k_ghat,
          after=list(b_lnf.readers) + [kbar], max_dma_last_dim=8192)

    b_wpq = [Buf(), Buf()]
    wgp_keys = {}

    def precast_round(i):
        idx, p = divmod(i, 4)
        srcimg = wg_img[idx] if idx < 4 else wp_img[idx - 4]
        kc = K("wgc")
        s.dma("gp", f"wgc{i % 2}", wpq[i % 2],
              srcimg[:, p * 2048:(p + 1) * 2048], key=kc,
              after=b_wpq[i % 2].wr(kc), max_dma_last_dim=8192)
        kw = K("wgw")
        s.dma("sy", f"wgw{i % 2}",
              d_wgp.ap()[idx][:, p * 2048:(p + 1) * 2048], wpq[i % 2],
              key=kw, after=b_wpq[i % 2].rd(kw) + [kc])
        wgp_keys[(idx, p)] = kw

    b_SP = Buf()
    ksr = K("srun0")
    s.dv.memset(SP[:, :], 0.0).inc("dv", key=ksr)
    b_SP.writer = ksr

    b_qch = [Buf(), Buf()]
    b_kch = [Buf(), Buf()]
    b_ktch = [Buf(), Buf()]
    b_vch = [Buf(), Buf()]
    b_Qhat = Buf()
    b_Zm = [Buf(), Buf()]
    b_yp = [Buf() for _ in range(NCH)]

    chunk_keys = {}

    dqk_done = [last_dqw[0], last_dqw[1]]
    dv_done = [last_dvw[0], last_dvw[1]]

    def ld_chunk(c8, parts="qktv"):
        par = c8 % 2
        kq = K("qch"); kk = K("kch"); kt = K("ktch"); kv = K("vch")
        chunk_keys[c8] = dict(q=kq, k=kk, kt=kt, v=kv)
        if "q" in parts:
            s.dma("sy", f"qch{par}", qch[par].rearrange("p (h t) -> p h t", h=16),
                  dq_col(d_q, c8), key=kq,
                  after=b_qch[par].wr(kq) + dqk_done)
        if "k" in parts:
            s.dma("sy", f"kch{par}", kch[par].rearrange("p (h t) -> p h t", h=16),
                  dq_col(d_k, c8), key=kk,
                  after=b_kch[par].wr(kk) + dqk_done)
        if "t" in parts:
            s.dma("ac", f"ktch{par}", ktch[par],
                  d_k.ap()[:, c8 * 128:(c8 + 1) * 128], transpose=True,
                  key=kt, after=b_ktch[par].wr(kt) + dqk_done)
        if "v" in parts:
            s.dma("ac", f"vch{par}", vch[par],
                  d_v.ap()[:, c8 * 128:(c8 + 1) * 128], transpose=True,
                  key=kv, after=b_vch[par].wr(kv) + dv_done)

    if stop_after == "a2p0":
        return _finish([])
    ld_chunk(0, parts="tv")
    if stop_after == "a2p1":
        return _finish([])
    if stop_after == "a2p2":
        ld_chunk(1, parts="qk")
        return _finish([])
    if stop_after == "a2p3":
        ld_chunk(1, parts="tv")
        return _finish([])
    ld_chunk(1, parts="tv")
    if stop_after == "a2p":
        return _finish([])

    sbf_of = {}
    ye_of = {}
    for c8 in range(NCH):
        par = c8 % 2
        ck = chunk_keys[c8]
        kkh = K("a2kh")
        aft = [ck["kt"], k_ghat] + b_ktch[par].rd(kkh)
        b_ktch[par].writer = kkh
        b_ktch[par].readers = []
        s.gp.tensor_tensor(ktch[par], ktch[par], ghat_s, ALU.mult).after(
            *aft).inc("gp", key=kkh)
        if c8 >= 1:
            for grp in range(4):
                gs4 = slice(grp * 512, (grp + 1) * 512)
                ksb = K("a2sb")
                aft = b_SP.rd(ksb)
                if grp == 0:
                    aft += b_yp[c8].wr(ksb)
                s.ac.activation(ypart(c8)[:, gs4], SP[:, gs4], AF.Copy).after(
                    *aft).inc("ac", key=ksb)
                b_yp[c8].writer = ksb
                sbf_of[(c8, grp)] = ksb
        for grp in range(4):
            pz = (c8 * 4 + grp) % 2
            ksc = K("a2sc")
            for q in range(4):
                h = grp * 4 + q
                hs = slice(h * 128, (h + 1) * 128)
                aft = [kkh, ck["v"]] + b_ktch[par].rd(ksc) + b_vch[par].rd(ksc)
                if q == 0:
                    aft += psb[4 + pz].wr(ksc)
                r = s.pe.matmul(PS[4 + pz][:, q * 128:(q + 1) * 128],
                                ktch[par][:, hs], vch[par][:, hs],
                                start=True, stop=True).after(*aft)
                if q == 3:
                    r.inc("pe", key=ksc)
            for q in range(4):
                h = grp * 4 + q
                hs = slice(h * 128, (h + 1) * 128)
                ksu = K("a2su")
                aft = [ksc] + psb[4 + pz].rd(ksu)
                if c8 == 0 and grp == 0 and q == 0:
                    aft.append(ksr)
                if c8 >= 1:
                    aft.append(sbf_of[(c8, grp)])
                b_SP.readers.append(ksu)
                s.dv.scalar_tensor_tensor(SP[:, hs], SP[:, hs], g128[h],
                                          PS[4 + pz][:, q * 128:(q + 1) * 128],
                                          ALU.mult, ALU.add).after(*aft).inc(
                    "dv", key=ksu)
                if h == 15:
                    b_SP.writer = ksu
                    b_SP.readers = []
        if c8 + 2 < NCH:
            ld_chunk(c8 + 2, parts="tv")
        for i in range(c8 * 2, c8 * 2 + 2):
            precast_round(i)
        if stop_after == "a2c0":
            break

    kccw = K("ccw")
    s.dma("sy", "ccw", cc_in.ap().rearrange("(h d) e -> d h e", d=128),
          SP[:, :].rearrange("p (h e) -> p h e", h=16), key=kccw,
          after=b_SP.rd(kccw))
    tap_ready["cc_in"] = [kccw]
    kcc = K("cc")
    s.gp.collective_compute(
        "AllGather", ALU.bypass, replica_groups=[list(range(NCORES))],
        ins=[cc_in.ap().opt()], outs=[cc_out.ap().opt()]).after(kccw).inc(
        "cc", key=kcc)

    b_gall = Buf()
    gall_keys = []
    cc_out_r = cc_out.ap().rearrange("(b h d) e -> b d h e", d=128, h=16)
    for c8 in range(NCH):
        par = c8 % 2
        kq = K("qch"); kk = K("kch"); kv = K("vch2")
        s.dma("sy", f"qch{par}", qch[par].rearrange("p (h t) -> p h t", h=16),
              dq_col(d_q, c8), key=kq,
              after=b_qch[par].wr(kq) + dqk_done)
        s.dma("ac", f"kch{par}", kch[par].rearrange("p (h t) -> p h t", h=16),
              dq_col(d_k, c8), key=kk,
              after=b_kch[par].wr(kk) + dqk_done)
        s.dma("ac", f"vch{par}", vch[par],
              d_v.ap()[:, c8 * 128:(c8 + 1) * 128], transpose=True,
              key=kv, after=b_vch[par].wr(kv))
        kqh = K("a2qh")
        aft = [kq, k_qdecb] + b_Qhat.wr(kqh) + b_qch[par].rd(kqh)
        s.dv.tensor_tensor(Qhat, qch[par], qdecb_s, ALU.mult).after(*aft).inc(
            "dv", key=kqh)
        for grp in range(4):
            pz = (c8 * 4 + grp) % 2
            gs4 = slice(grp * 512, (grp + 1) * 512)
            kz = K("a2z")
            for q in range(4):
                h = grp * 4 + q
                hs = slice(h * 128, (h + 1) * 128)
                aft = [kk, kq] + b_kch[par].rd(kz) + b_qch[par].rd(kz)
                if q == 0:
                    aft += psb[pz].wr(kz)
                r = s.pe.matmul(PS[pz][:, q * 128:(q + 1) * 128],
                                kch[par][:, hs], qch[par][:, hs],
                                start=True, stop=True).after(*aft)
                if q == 3:
                    r.inc("pe", key=kz)
            kzm = K("a2zm")
            aft = [kz, k_maskT] + b_Zm[pz].wr(kzm) + psb[pz].rd(kzm)
            if c8 == 0 and grp < 2:
                aft += b_ktch[0].readers + [b_ktch[0].writer]
            s.dv.tensor_tensor(ZmS[pz], PS[pz][:, 0:512], maskT_s[:, gs4],
                               ALU.mult).after(*aft).inc("dv", key=kzm)
            kyg = K("a2y2")
            for q in range(4):
                h = grp * 4 + q
                hs = slice(h * 128, (h + 1) * 128)
                qsl = slice(q * 128, (q + 1) * 128)
                aft = [kzm, kv] + b_Zm[pz].rd(kyg) + b_vch[par].rd(kyg)
                if q == 0:
                    aft += psb[2 + pz].wr(kyg)
                    b_yp[c8].readers.append(kyg)
                r = s.pe.matmul(PS[2 + pz][:, qsl], ZmS[pz][:, qsl],
                                vch[par][:, hs], start=True,
                                stop=(c8 == 0)).after(*aft)
                if c8 == 0 and q == 3:
                    r.inc("pe", key=kyg)
                if c8 >= 1:
                    aft2 = [kqh, sbf_of[(c8, grp)]] + b_Qhat.rd(kyg)
                    r = s.pe.matmul(PS[2 + pz][:, qsl], Qhat[:, hs],
                                    ypart(c8)[:, hs], start=False,
                                    stop=True).after(*aft2)
                    if q == 3:
                        r.inc("pe", key=kyg)
            kye = K("a2ye")
            aft = [kyg] + psb[2 + pz].rd(kye)
            if grp == 0:
                aft += b_yp[c8].wr(kye)
            s.ac.activation(ypart(c8)[:, gs4], PS[2 + pz][:, 0:512],
                            AF.Copy).after(*aft).inc("ac", key=kye)
            b_yp[c8].writer = kye
            ye_of[(c8, grp)] = kye
        if c8 == 1:
            for b in range(NCORES):
                kg_ = K("gall"); gall_keys.append(kg_)
                aft = [kcc]
                if b == 0:
                    aft += b_xs[0].readers + b_xs[1].readers + \
                        b_cs[0].readers + b_cs[1].readers
                s.dma("gp", "gall", gall(b).rearrange("p (h e) -> p h e", h=16),
                      cc_out_r[b], key=kg_, after=aft)
            b_gall.writer = gall_keys[-1]
        if stop_after == "a2c0":
            break

    if not ph("p5"):
        return _finish([])

    wgq_keys = {}
    for qb in range(2):
        k = K("wg"); wgq_keys[(0, 0, qb)] = k
        s.dma("sy", f"wpq{qb}", wpq[qb], d_wgp.ap()[0][:, qb * 2048:(qb + 1) * 2048],
              key=k, after=b_wpq[qb].wr(k) + [wgp_keys[(0, qb)]])
    b_wtb = Buf()
    b_yn = Buf()
    ptmp_s = f32(48, 2048)
    b_ptmp = Buf()
    pcm = None
    for b in range(NCORES):
        kw = K("wtb")
        aft = b_wtb.wr(kw)
        if b == 0:
            aft += b_Qhat.readers + [b_Qhat.writer]
        s.dma("gp", "wtb", wtb_s, wtab_d[b], key=kw, after=aft)
        if b == 0:
            kp = K("pcm")
            aft = [kccw, kw] + gall_keys + b_SP.wr(kp) + b_wtb.rd(kp) + b_gall.rd(kp)
            s.dv.tensor_tensor(SP[:, :], gall(0), wtb_s, ALU.mult).after(
                *aft).inc("dv", key=kp)
            pcm = kp
        else:
            kt = K()
            aft = [kw] + b_ptmp.wr(kt) + b_wtb.rd(kt) + b_gall.rd(kt)
            if b == 1:
                aft += b_ktch[0].readers + b_ktch[1].readers + \
                    b_vch[0].readers + b_vch[1].readers + \
                    b_Zm[0].readers + b_Zm[1].readers + \
                    [b_ktch[0].writer, b_ktch[1].writer,
                     b_vch[0].writer, b_vch[1].writer,
                     b_Zm[0].writer, b_Zm[1].writer]
            s.dv.tensor_tensor(ptmp_s, gall(b), wtb_s, ALU.mult).after(*aft).inc(
                "dv", key=kt)
            kp = K("pcm")
            b_SP.readers.append(kp)
            s.dv.tensor_tensor(SP[:, :], SP[:, :], ptmp_s, ALU.add).after(
                kt, *b_ptmp.rd(kp)).inc("dv", key=kp)
            b_SP.writer = kp
            b_SP.readers = []
            pcm = kp

    kgnw = K("gnw")
    s.dma("gp", "gnw", gnw_s, gnw_d, key=kgnw, after=[pcm] + b_gall.readers)
    kgnb = K("gnb")
    s.dma("gp", "gnb", gnb_s, gnb_d, key=kgnb, after=[pcm])
    b_gnw = Buf(); b_gnw.writer = kgnw
    b_gnb = Buf(); b_gnb.writer = kgnb
    if not ph("post"):
        return _finish([])

    b_lnh = Buf()
    b_sgh = [Buf() for _ in range(4)]
    b_gfh = Buf()
    b_gfh.readers += b_ptmp.readers + [b_ptmp.writer]
    xs2 = [f32(96, 2048), f32(104, 2048)]
    Tb2 = bf(112, 2048)
    b_xs2 = [Buf(), Buf()]
    b_Tb2 = Buf()
    ln2_w = []

    def l2_pre2(tc8):
        hf_, tc_ = divmod(tc8, 4)
        return [x2w_keys[(hf_, c4_, tc_)] for c4_ in range(4)]
    b_PT = Buf()
    b_pdec = Buf()
    b_x0 = [Buf(), Buf()]
    b_x2o = [Buf(), Buf()]
    b_stats = Buf()
    x2w_keys = {}

    for hf in range(2):
        kl = []
        for fc in range(16):
            k = K("lnh"); kl.append(k)
            aft = [ln1_w[7]]
            if fc == 0:
                aft += b_lnh.wr(k)
                if hf == 0:
                    aft += ([k_ghat, ye_of[(7, 3)]] +
                            b_ktch[0].readers + b_ktch[1].readers +
                            [b_ktch[0].writer, b_ktch[1].writer])
            s.dma("ac", "lnh", lnh(fc),
                  d_ln.ap()[hf * 512:(hf + 1) * 512, fc * 128:(fc + 1) * 128],
                  transpose=True, key=k, after=aft)
        b_lnh.writer = kl[-1]
        lnh_all = kl
        gse = {}
        for c4 in range(4):
            for qb in range(2):
                if hf == 0 and c4 == 0 and qb < 2:
                    continue
                k = K("wg"); wgq_keys[(hf, c4, qb)] = k
                s.dma("sy", f"wpq{qb % 2}", wpq[qb % 2],
                      d_wgp.ap()[c4][:, qb * 2048:(qb + 1) * 2048],
                      key=k, after=b_wpq[qb % 2].wr(k) + [wgp_keys[(c4, qb)]])
            kg = {}
            for cc in range(16):
                qb = cc // 4
                for tc in range(4):
                    if cc == 0:
                        kg[tc] = K("gmm")
                    aft = [wgq_keys[(hf, c4, qb)]]
                    if c4 == 0 and cc == 0:
                        aft += lnh_all
                    if cc == 0:
                        aft += psb[tc].wr(kg[tc])
                    r = s.pe.matmul(PS[tc][:, :],
                                    lnh(cc)[:, tc * 128:(tc + 1) * 128],
                                    wpq[qb % 2][:, (cc % 4) * 512:(cc % 4 + 1) * 512],
                                    start=(cc == 0), stop=(cc == 15)).after(*aft)
                    b_lnh.readers.append(kg[tc])
                    if cc % 4 == 3 and tc == 3:
                        if cc == 15:
                            b_wpq[qb % 2].rd(kg[tc])
                        else:
                            kqr = K("gqr")
                            r.inc("pe", key=kqr)
                            b_wpq[qb % 2].rd(kqr)
                            nqb = qb + 2
                            if nqb < 4:
                                k2 = K("wg"); wgq_keys[(hf, c4, nqb)] = k2
                                s.dma("sy", f"wpq{nqb % 2}", wpq[nqb % 2],
                                      d_wgp.ap()[c4][:, nqb * 2048:(nqb + 1) * 2048],
                                      key=k2, after=b_wpq[nqb % 2].wr(k2) +
                                      [wgp_keys[(c4, nqb)]])
                    if cc == 15:
                        r.inc("pe", key=kg[tc])
            for tc in range(4):
                ks = K("gse")
                aft = [kg[tc]] + b_sgh[tc].wr(ks) + psb[tc].rd(ks)
                s.ac.activation(sgh(tc)[:, c4 * 512:(c4 + 1) * 512], PS[tc][:, :],
                                AF.Silu).after(*aft).inc("ac", key=ks)
                gse[(c4, tc)] = ks
        for tc in range(4):
            c8g = hf * 4 + tc
            par = tc % 2
            kq = K("qch2")
            s.dma("sy", f"qch{par}", qch[par].rearrange("p (h t) -> p h t", h=16),
                  dq_col(d_q, c8g), key=kq, after=b_qch[par].wr(kq))
            kp = K("pdec")
            aft = b_pdec.wr(kp)
            if hf == 0 and tc == 0:
                aft += b_kch[0].readers + b_kch[1].readers + \
                    [b_kch[0].writer, b_kch[1].writer]
            s.dma("sy", "pdec", pdec_s, pdec_d[c8g], key=kp, after=aft)
            kqh = K("qh2")
            aft = [kq, k_qdecb] + b_Qhat.wr(kqh) + b_qch[par].rd(kqh)
            s.dv.tensor_tensor(Qhat, qch[par], qdecb_s, ALU.mult).after(
                *aft).inc("dv", key=kqh)
            kpt = K("ptl")
            aft = [kp, pcm] + b_PT.wr(kpt) + b_pdec.rd(kpt) + b_SP.rd(kpt)
            s.dv.tensor_tensor(PT, SP[:, :], pdec_s, ALU.mult).after(*aft).inc(
                "dv", key=kpt)
            for h in range(16):
                hs = slice(h * 128, (h + 1) * 128)
                kmm = K("pth")
                aft = [kqh, kpt] + psb[6 + h % 2].wr(kmm) + \
                    b_Qhat.rd(kmm) + b_PT.rd(kmm)
                s.pe.matmul(PS[6 + h % 2][:, 0:128], Qhat[:, hs], PT[:, hs],
                            start=True, stop=True).after(*aft).inc("pe", key=kmm)
                ky = K("ysl")
                aft = [kmm, ye_of[(c8g, h // 4)]] + psb[6 + h % 2].rd(ky)
                b_yp[c8g].readers.append(ky)
                s.dv.tensor_tensor(ypart(c8g)[:, hs], ypart(c8g)[:, hs],
                                   PS[6 + h % 2][:, 0:128], ALU.add).after(
                    *aft).inc("dv", key=ky)
                b_yp[c8g].writer = ky
                j = tc * 16 + h
                krd = K()
                s.dv.reduce_sum(st_s1[:, j:j + 1], ypart(c8g)[:, hs],
                                AX.X).after(ky).inc("dv", key=krd)
                kst = K("yst")
                r = s.dv.scalar_tensor_tensor(
                    Tb[:, 0:128], ypart(c8g)[:, hs], 1.0, ypart(c8g)[:, hs],
                    ALU.mult, ALU.mult, accum_out=st_s2[:, j:j + 1])
                r.after(*(b_Tb.wr(kst) + [ky])).inc("dv", key=kst)
                last_sweep = (krd, kst)
        kmu_b = K()
        s.dv.tensor_scalar(st_mu[:, :], st_s1[:, :], 1.0 / D, None,
                           ALU.mult).after(*last_sweep).inc("dv", key=kmu_b)
        kms_b = K()
        s.dv.scalar_tensor_tensor(st_msq[:, :], st_mu[:, :], 1.0, st_mu[:, :],
                                  ALU.mult, ALU.mult).after(kmu_b).inc("dv", key=kms_b)
        kv = K("gnv")
        s.dv.scalar_tensor_tensor(st_var[:, :], st_s2[:, :], 1.0 / D,
                                  st_msq[:, :], ALU.mult, ALU.subtract).after(
            kms_b).inc("dv", key=kv)
        kv2 = K()
        s.dv.tensor_scalar(st_var[:, :], st_var[:, :], EPS, None,
                           ALU.add).after(kv).inc("dv", key=kv2)
        kq_ = K("gnq")
        s.ac.activation(st_sq[:, :], st_var[:, :], AF.Sqrt).after(
            kv2).inc("ac", key=kq_)
        kr = K("gnr")
        s.dv.reciprocal(st_rstd[:, :], st_sq[:, :]).after(kq_).inc("dv", key=kr)
        grow_keys = []
        for tc in range(4):
            c8g = hf * 4 + tc
            for h in range(16):
                hs = slice(h * 128, (h + 1) * 128)
                j = tc * 16 + h
                kn = K()
                aft = [kr, kmu_b] + (b_yn.wr(kn) if h == 0 else [])
                b_yp[c8g].readers.append(kn)
                s.dv.tensor_scalar(yn_s[:, hs], ypart(c8g)[:, hs],
                                   st_mu[:, j:j + 1], st_rstd[:, j:j + 1],
                                   ALU.subtract, ALU.mult).after(*aft).inc("dv", key=kn)
                kn_last = kn
            ka = K()
            s.dv.tensor_tensor(yn_s, yn_s, gnw_s, ALU.mult).after(
                kgnw, kn_last).inc("dv", key=ka)
            b_gnw.readers.append(ka)
            kb = K()
            s.dv.tensor_tensor(yn_s, yn_s, gnb_s, ALU.add).after(
                kgnb, ka).inc("dv", key=kb)
            b_gnb.readers.append(kb)
            kgr = K("grow")
            aft = [gse[(3, tc)], kb] + b_Tb.wr(kgr) + b_sgh[tc].rd(kgr)
            s.dv.tensor_tensor(Tb[:, :], yn_s, sgh(tc), ALU.mult).after(
                *aft).inc("dv", key=kgr)
            b_yn.readers.append(kgr)
            kw = K("dgw")
            s.dma("sy", "dgw", dg_t[c8g], Tb[:, :], key=kw,
                  after=b_Tb.rd(kw) + [kgr])
            grow_keys.append(kw)
        kgf = []
        for fc in range(16):
            k = K("gfh"); kgf.append(k)
            aft = [grow_keys[3]]
            if fc == 0:
                aft += b_gfh.wr(k)
                if hf == 0:
                    aft += b_ktch[0].readers + b_ktch[1].readers + \
                        b_vch[0].readers + b_vch[1].readers + \
                        b_Zm[0].readers + b_Zm[1].readers + \
                        [b_ktch[0].writer, b_ktch[1].writer,
                         b_vch[0].writer, b_vch[1].writer,
                         b_Zm[0].writer, b_Zm[1].writer]
            s.dma("ac", "gfh", gfh(fc),
                  d_g.ap()[hf * 512:(hf + 1) * 512, fc * 128:(fc + 1) * 128],
                  transpose=True, key=k, after=aft)
        b_gfh.writer = kgf[-1]
        gfh_all = kgf
        for c4 in range(4):
            for qb in range(2):
                k = K("wp"); wgq_keys[("p", hf, c4, qb)] = k
                s.dma("gp", f"wpc{qb % 2}", wpq[qb % 2],
                      wp_img[c4][:, qb * 2048:(qb + 1) * 2048],
                      key=k, after=b_wpq[qb % 2].wr(k), max_dma_last_dim=8192)
            kpj = {}
            for tc in range(2):
                sl = tc % 2
                kx = K("x0")
                s.dma("sy", f"x0{sl}", x0s[sl],
                      bass.AP(x.tensor, (hf * 512 + tc * 128) * C + c4 * 512,
                              [[C, 128], [1, 512]]),
                      key=kx, after=b_x0[sl].wr(kx))
                wgq_keys[("x0", hf, c4, tc)] = kx
            for cc in range(16):
                qb = cc // 4
                for tc in range(4):
                    if cc == 0:
                        kpj[tc] = K("pjmm")
                    aft = [wgq_keys[("p", hf, c4, qb)]]
                    if c4 == 0 and cc == 0:
                        aft += gfh_all
                    if cc == 0:
                        aft += psb[tc].wr(kpj[tc])
                    r = s.pe.matmul(PS[tc][:, :],
                                    gfh(cc)[:, tc * 128:(tc + 1) * 128],
                                    wpq[qb % 2][:, (cc % 4) * 512:(cc % 4 + 1) * 512],
                                    start=(cc == 0), stop=(cc == 15)).after(*aft)
                    b_gfh.readers.append(kpj[tc])
                    if cc % 4 == 3 and tc == 3:
                        if cc == 15:
                            b_wpq[qb % 2].rd(kpj[tc])
                        else:
                            kqr = K("pqr")
                            r.inc("pe", key=kqr)
                            b_wpq[qb % 2].rd(kqr)
                            nqb = qb + 2
                            if nqb < 4:
                                k2 = K("wp"); wgq_keys[("p", hf, c4, nqb)] = k2
                                s.dma("gp", f"wpc{nqb % 2}", wpq[nqb % 2],
                                      wp_img[c4][:, nqb * 2048:(nqb + 1) * 2048],
                                      key=k2, after=b_wpq[nqb % 2].wr(k2),
                                      max_dma_last_dim=8192)
                    if cc == 15:
                        r.inc("pe", key=kpj[tc])
            for tc in range(4):
                sl = tc % 2
                ke = K("x2e")
                aft = [kpj[tc], wgq_keys[("x0", hf, c4, tc)]] + \
                    b_x2o[sl].wr(ke) + psb[tc].rd(ke) + b_x0[sl].rd(ke)
                s.dv.tensor_tensor(x2o[sl], PS[tc][:, :], x0s[sl],
                                   ALU.add).after(*aft).inc("dv", key=ke)
                kw = K("x2w")
                s.dma("sy", f"x2w{sl}",
                      bass.AP(d_x2.ap().tensor, (hf * 512 + tc * 128) * C + c4 * 512,
                              [[C, 128], [1, 512]]),
                      x2o[sl], key=kw, after=b_x2o[sl].rd(kw) + [ke])
                x2w_keys[(hf, c4, tc)] = kw
                if tc + 2 < 4:
                    kx = K("x0")
                    s.dma("sy", f"x0{sl}", x0s[sl],
                          bass.AP(x.tensor,
                                  (hf * 512 + (tc + 2) * 128) * C + c4 * 512,
                                  [[C, 128], [1, 512]]),
                          key=kx, after=b_x0[sl].wr(kx))
                    wgq_keys[("x0", hf, c4, tc + 2)] = kx
        ln2_w += ln_pass("L2", dx2_t, dln2_t, l2_pre2,
                         [pcm] if hf == 0 else [],
                         tiles=range(hf * 4, hf * 4 + 4),
                         bufs=(xs2, Tb2, b_xs2, b_Tb2))

    tap_ready["d_g"] = [x2w_keys[(1, 3, 3)]]
    tap_ready["d_x2"] = [x2w_keys[(hf_, c4_, tc_)]
                        for hf_ in range(2) for c4_ in range(4) for tc_ in range(4)]
    if not ph("post") or not ph("ffn"):
        return _finish([])

    l2_pre = l2_pre2
    tap_ready["d_ln2"] = [ln2_w[7]]

    b_ln2f = Buf()
    kl2 = []
    for fc in range(16):
        k = K("ln2ld"); kl2.append(k)
        aft = [ln2_w[7]]
        if fc == 0:
            aft += b_lnh.readers + b_sgh[0].readers + b_sgh[1].readers + \
                b_sgh[2].readers + b_sgh[3].readers + b_ln2f.wr(k)
        s.dma("ac", "ln2ld", ln2f(fc), d_ln2.ap()[:, fc * 128:(fc + 1) * 128],
              transpose=True, key=k, after=aft)
    b_ln2f.writer = kl2[-1]
    ln2_all = kl2

    ksd = K("sdl")
    s.dma("sy", "sdl", sdown_s, sdown_d, key=ksd,
          after=b_SP.readers + [b_SP.writer])
    b_sdown = Buf(); b_sdown.writer = ksd

    b_wga = [Buf(), Buf()]
    b_wua = [Buf(), Buf()]
    b_wdn = [Buf(), Buf()]
    b_fuse = [Buf() for _ in range(16)]
    b_gs = [Buf(), Buf()]
    b_acc = [Buf() for _ in range(8)]
    for oc in range(2):
        for t in range(4):
            b_wga[oc].readers += b_yp[t].readers + [b_yp[t].writer]
            b_wua[oc].readers += b_yp[t].readers + [b_yp[t].writer]
    for t in range(4, 8):
        b_wdn[0].readers += b_yp[t].readers + [b_yp[t].writer]
    b_wdn[1].readers += b_Qhat.readers + [b_Qhat.writer] + \
        b_wpq[0].readers + b_wpq[1].readers + \
        [b_wpq[0].writer, b_wpq[1].writer]
    for tt in range(2):
        b_gs[tt].readers += b_Tb.readers + [b_Tb.writer]
    fuse_prime = b_qch[0].readers + b_qch[1].readers + \
        [b_qch[0].writer, b_qch[1].writer] + b_pdec.readers + \
        [b_pdec.writer] + b_gfh.readers + [b_gfh.writer]
    for oc in range(16):
        b_fuse[oc].readers += fuse_prime
    acc_prime = b_gnw.readers + b_gnb.readers + b_yn.readers + \
        b_x0[0].readers + b_x0[1].readers + b_x2o[0].readers + \
        b_x2o[1].readers + b_xs[0].readers + b_xs[1].readers + \
        b_xs2[0].readers + b_xs2[1].readers + b_Tb2.readers + [b_Tb2.writer]
    for t8 in range(8):
        b_acc[t8].readers += acc_prime

    fgs_keys = {}
    ffu_keys = {}
    fdm_keys = {}
    fde_keys = {}
    for og in range(4):
        for oc in range(16):
            ocg = og * 16 + oc
            kwg = K("wga")
            s.dma("gp", f"wga{oc % 2}", wga[oc % 2], wgate_img[ocg], key=kwg,
                  after=b_wga[oc % 2].wr(kwg), max_dma_last_dim=2048)
            kwu = K("wua")
            s.dma("gp", f"wua{oc % 2}", wua[oc % 2], wup_img[ocg], key=kwu,
                  after=b_wua[oc % 2].wr(kwu), max_dma_last_dim=2048)
            kgm = {0: K("fgm"), 1: K("fgm")}
            kum = {0: K("fum"), 1: K("fum")}
            for cc in range(16):
                for tt in range(2):
                    aft = [kwg]
                    if ocg == 0 and tt == 0 and cc == 0:
                        aft += ln2_all
                    if cc == 0:
                        aft += psb[tt].wr(kgm[tt])
                    r = s.pe.matmul(PS[tt][:, :],
                                    wga[oc % 2][:, cc * 128:(cc + 1) * 128],
                                    ln2f(cc)[:, tt * 512:(tt + 1) * 512],
                                    start=(cc == 0), stop=(cc == 15)).after(*aft)
                    b_ln2f.readers.append(kgm[tt])
                    if cc == 15:
                        b_wga[oc % 2].rd(kgm[tt])
                        r.inc("pe", key=kgm[tt])
            for cc in range(16):
                for tt in range(2):
                    aft = [kwu]
                    if cc == 0:
                        aft += psb[2 + tt].wr(kum[tt])
                    r = s.pe.matmul(PS[2 + tt][:, :],
                                    wua[oc % 2][:, cc * 128:(cc + 1) * 128],
                                    ln2f(cc)[:, tt * 512:(tt + 1) * 512],
                                    start=(cc == 0), stop=(cc == 15)).after(*aft)
                    b_ln2f.readers.append(kum[tt])
                    if cc == 15:
                        b_wua[oc % 2].rd(kum[tt])
                        r.inc("pe", key=kum[tt])
            for tt in range(2):
                kgs = K("fgs")
                aft = [kgm[tt], k_sgc] + b_gs[tt].wr(kgs) + psb[tt].rd(kgs)
                s.ac.activation(gss[tt], PS[tt][:, :], AF.Silu,
                                scale=sgc_s[:, ocg:ocg + 1]).after(*aft).inc(
                    "ac", key=kgs)
                fgs_keys[(ocg, tt)] = kgs
                kfu = K("ffu")
                aft = [kum[tt], kgs, k_suc]
                if tt == 0:
                    aft += b_fuse[oc].wr(kfu)
                aft += psb[2 + tt].rd(kfu) + b_gs[tt].rd(kfu)
                s.dv.scalar_tensor_tensor(
                    fuse(oc)[:, tt * 512:(tt + 1) * 512], PS[2 + tt][:, :],
                    suc_s[:, ocg:ocg + 1], gss[tt], ALU.mult, ALU.mult).after(
                    *aft).inc("dv", key=kfu)
                ffu_keys[(ocg, tt)] = kfu
        for ht in range(4):
            kwd = K("wdn")
            s.dma("gp", f"wdn{ht % 2}", wdns[ht % 2], wdown_img[og * 4 + ht],
                  key=kwd, after=b_wdn[ht % 2].wr(kwd), max_dma_last_dim=2048)
            for hf2 in range(2):
                kdm = {}
                for tc in range(4):
                    kdm[tc] = K("fdm")
                for oc in range(16):
                    for tc in range(4):
                        aft = [kwd, ffu_keys[(og * 16 + oc, hf2)]]
                        if oc == 0:
                            aft += psb[4 + tc].wr(kdm[tc])
                        r = s.pe.matmul(
                            PS[4 + tc][:, :],
                            fuse(oc)[:, hf2 * 512 + tc * 128:hf2 * 512 + (tc + 1) * 128],
                            wdns[ht % 2][:, oc * 512:(oc + 1) * 512],
                            start=(oc == 0), stop=(oc == 15)).after(*aft)
                        b_fuse[oc].readers.append(kdm[tc])
                        if oc == 15:
                            b_wdn[ht % 2].rd(kdm[tc])
                            r.inc("pe", key=kdm[tc])
                for tc in range(4):
                    t8 = hf2 * 4 + tc
                    accsl = acc(t8)[:, ht * 512:(ht + 1) * 512]
                    ke = K("fde")
                    aft = [kdm[tc]] + psb[4 + tc].rd(ke)
                    if og == 0:
                        aft += b_acc[t8].wr(ke)
                        r = s.dv.tensor_copy(accsl, PS[4 + tc][:, :]).after(*aft)
                    else:
                        b_acc[t8].readers.append(ke)
                        r = s.dv.tensor_tensor(accsl, accsl, PS[4 + tc][:, :],
                                               ALU.add).after(*aft)
                        b_acc[t8].writer = ke
                        b_acc[t8].readers = []
                    r.inc("dv", key=ke)
                    fde_keys[(og, ht, hf2, tc)] = ke
    b_x2f = [Buf(), Buf()]
    b_fin = [Buf(), Buf()]
    prime_fin = []
    for bb in (b_wga[0], b_wga[1], b_wua[0], b_wua[1], b_wdn[0], b_gs[0], b_gs[1]):
        prime_fin += bb.readers + ([bb.writer] if bb.writer else [])
    b_x2f[0].readers += prime_fin
    b_fin[0].readers += prime_fin
    b_fin[1].readers += prime_fin
    b_x2f[1].readers += prime_fin
    ow_keys = []
    for t8 in range(8):
        par = t8 % 2
        kx = K("x2f")
        aft = l2_pre(t8) + b_x2f[par].wr(kx)
        s.dma("sy", f"x2f{par}", x2f[par], dx2_t[t8], key=kx, after=aft)
        km = K("finm")
        aft = [fde_keys[(3, ht, t8 // 4, t8 % 4)] for ht in range(4)] + \
            [ksd] + b_fin[par].wr(km) + b_acc[t8].rd(km) + b_sdown.rd(km)
        s.dv.tensor_tensor(fin[par], acc(t8), sdown_s, ALU.mult).after(
            *aft).inc("dv", key=km)
        kf = K("fin")
        s.dv.tensor_tensor(fin[par], fin[par], x2f[par], ALU.add).after(
            kx, km, *b_x2f[par].rd(kf)).inc("dv", key=kf)
        ko = K("ow")
        s.dma("sy", f"ow{par}", out_t[t8], fin[par], key=ko,
              after=b_fin[par].rd(ko) + [kf])
        ow_keys.append(ko)

    return _finish(ow_keys)


def kernel(L0_attn_x, L0_attn_w_qkv, L0_attn_w_g, L0_attn_w_proj,
           L0_attn_gn_weight, L0_attn_gn_bias, L0_ffn_w_gate, L0_ffn_w_up,
           L0_ffn_w_down, L0_ffn_s_gate, L0_ffn_s_up, L0_ffn_s_down):
    x_full = np.ascontiguousarray(np.asarray(L0_attn_x, np.float32).reshape(T, C))
    sh = _host_shared(L0_attn_w_qkv, L0_attn_w_g, L0_attn_w_proj,
                      L0_ffn_w_gate, L0_ffn_w_up, L0_ffn_w_down,
                      L0_attn_gn_weight, L0_attn_gn_bias,
                      L0_ffn_s_gate, L0_ffn_s_up, L0_ffn_s_down)
    cos, sin = _host_xpos()

    nc = build_nc(taps=tuple(_DEBUG_TAPS))

    in_maps = []
    for core in range(NCORES):
        pc = _host_percore(core, x_full, cos, sin)
        m = dict(x=pc["x"], cs_img=pc["cs_img"], wtab=pc["wtab"],
                 wqkv_img=sh["wqkv_img"], wg_img=sh["wg_img"], wp_img=sh["wp_img"],
                 wgate_img=sh["wgate_img"], wup_img=sh["wup_img"],
                 wdown_img=sh["wdown_img"], maskT=sh["maskT"], qdecb=sh["qdecb"],
                 ghat=sh["ghat"], pdec=sh["pdec"], pmat=sh["pmat"],
                 gnw_b=sh["gnw_b"], gnb_b=sh["gnb_b"], sdown_b=sh["sdown_b"],
                 sg_c=sh["sg_c"], su_c=sh["su_c"])
        in_maps.append(m)

    res = run_bass_kernel_spmd(nc, in_maps, core_ids=list(range(NCORES)))
    outs = [res.results[c]["out"] for c in range(NCORES)]
    full = np.concatenate(outs, axis=0).reshape(1, T, C).astype(np.float32)
    if _DEBUG_TAPS:
        kernel._tap_results = [
            {k: v for k, v in res.results[c].items() if k.startswith("tap_")}
            for c in range(NCORES)]
    return full


# version 11
# speedup vs baseline: 1.0879x; 1.0302x over previous
import numpy as np
import concourse.bass as bass
import concourse.mybir as mybir
from concourse.bass_utils import run_bass_kernel_spmd

F32 = mybir.dt.float32
BF16 = mybir.dt.bfloat16
I8 = mybir.dt.int8
AF = mybir.ActivationFunctionType
ALU = mybir.AluOpType
AX = mybir.AxisListType

NCORES = 8
T = 8192
C = 2048
H = 16
D = 128
TPC = T // NCORES
NCH = TPC // 128
INTER = 8192
EPS = 1e-5
RD = float(D) ** -0.5

_DEBUG_TAPS = []

KBf = 256


ENGS = ("pe", "dv", "ac", "sy", "gp")


class _Op:
    __slots__ = ("method", "args", "kwargs", "waits", "incs")

    def __init__(self, method, args, kwargs):
        self.method = method
        self.args = args
        self.kwargs = kwargs
        self.waits = {}
        self.incs = []


class _OpRef:
    def __init__(self, s, op):
        self.s = s
        self.op = op

    def after(self, *keys):
        for k in keys:
            if k is None:
                continue
            if isinstance(k, list):
                self.after(*k)
                continue
            sem, v = self.s.tick[k]
            if self.op.waits.get(sem, -1) < v:
                self.op.waits[sem] = v
        return self

    def inc(self, sem, key=None, dma=False):
        step = 16 if dma else 1
        self.s.cnt[sem] = self.s.cnt.get(sem, 0) + step
        v = self.s.cnt[sem]
        self.op.incs.append((sem, step))
        if key is not None:
            keys = key if isinstance(key, list) else [key]
            for k in keys:
                assert k not in self.s.tick, f"dup key {k}"
                self.s.tick[k] = (sem, v)
        return self


class _EngProxy:
    def __init__(self, s, eng):
        self._s = s
        self._e = eng

    def __getattr__(self, m):
        def call(*a, **kw):
            op = _Op(m, a, kw)
            self._s.q[self._e].append(op)
            return _OpRef(self._s, op)
        return call


class Sched:
    def __init__(self):
        self.q = {e: [] for e in ENGS}
        self.cnt = {}
        self.tick = {}
        self.nkey = 0
        for e in ENGS:
            setattr(self, e, _EngProxy(self, e))

    def key(self, tag=""):
        self.nkey += 1
        return (tag, self.nkey)

    def dma(self, eng, slot, out, in_, key=None, after=(), **kw):
        r = getattr(self, eng).dma_start(out=out, in_=in_, **kw)
        r.after(*after)
        r.inc("D" + slot, key=key, dma=True)
        return r


class Buf:

    def __init__(self):
        self.writer = None
        self.readers = []

    def wr(self, key):
        aft = list(self.readers) if self.readers else (
            [self.writer] if self.writer is not None else [])
        self.writer = key
        self.readers = []
        return aft

    def rd(self, key):
        self.readers.append(key)
        return [self.writer] if self.writer is not None else []


def _gammas():
    return np.array([1.0 - 2.0 ** (-5.0 - h) for h in range(H)], dtype=np.float64)


def _host_shared(wqkv, wg, wp, wgate, wup, wdown, gnw, gnb, sg, su, sd):
    g = _gammas()
    sh = {}
    sh["wqkv_img"] = np.ascontiguousarray(
        np.asarray(wqkv, np.float32).reshape(16, 128, 12, 512)
        .transpose(2, 1, 0, 3).reshape(12, 128, 8192))
    sh["wg_img"] = np.ascontiguousarray(
        np.asarray(wg, np.float32).reshape(16, 128, 4, 512)
        .transpose(2, 1, 0, 3).reshape(4, 128, 8192))
    sh["wp_img"] = np.ascontiguousarray(
        np.asarray(wp, np.float32).reshape(16, 128, 4, 512)
        .transpose(2, 1, 0, 3).reshape(4, 128, 8192))
    sh["wgate_img"] = np.ascontiguousarray(
        np.asarray(wgate).reshape(16, 128, 64, 128)
        .transpose(2, 1, 0, 3).reshape(64, 128, 2048))
    sh["wup_img"] = np.ascontiguousarray(
        np.asarray(wup).reshape(16, 128, 64, 128)
        .transpose(2, 1, 0, 3).reshape(64, 128, 2048))
    sh["wdown_img"] = np.ascontiguousarray(
        np.asarray(wdown).reshape(4, 16, 128, 4, 512)
        .transpose(0, 3, 2, 1, 4).reshape(16, 128, 8192))
    j = np.arange(128, dtype=np.float64)
    maskT = np.zeros((128, H * 128), dtype=np.float64)
    for h in range(H):
        e = j[None, :] - j[:, None]
        maskT[:, h * 128:(h + 1) * 128] = np.where(e >= 0, g[h] ** np.maximum(e, 0), 0.0)
    sh["maskT"] = maskT.astype(np.float32)
    qd = np.concatenate([g[h] ** (j + 1.0) for h in range(H)])
    sh["qdecb"] = np.broadcast_to(qd.astype(np.float32)[None, :], (128, H * 128)).copy()
    gh = np.zeros((128, H * 128), dtype=np.float64)
    for h in range(H):
        gh[:, h * 128:(h + 1) * 128] = (g[h] ** (127.0 - j))[:, None]
    sh["ghat"] = gh.astype(np.float32)
    pdec = np.zeros((NCH, 128, H * 128), dtype=np.float32)
    for c8 in range(NCH):
        row = np.concatenate([np.full(128, g[h] ** (128.0 * c8)) for h in range(H)])
        pdec[c8] = row.astype(np.float32)[None, :]
    sh["pdec"] = pdec
    P = np.zeros((128, 128), dtype=np.float32)
    ii = np.arange(0, 128, 2)
    P[ii + 1, ii] = -1.0
    P[ii, ii + 1] = 1.0
    sh["pmat"] = P
    sh["gnw_b"] = np.broadcast_to(np.asarray(gnw, np.float32).reshape(1, C), (128, C)).copy()
    sh["gnb_b"] = np.broadcast_to(np.asarray(gnb, np.float32).reshape(1, C), (128, C)).copy()
    sh["sdown_b"] = np.broadcast_to(np.asarray(sd, np.float32).reshape(1, C), (128, C)).copy()
    sh["sg_c"] = np.ascontiguousarray(np.asarray(sg, np.float32).reshape(64, 128).T)
    sh["su_c"] = np.ascontiguousarray(np.asarray(su, np.float32).reshape(64, 128).T)
    return sh


def _host_xpos():
    i = np.arange(C // 2, dtype=np.float64)
    zeta = (2.0 * i + 0.4 * C) / (1.4 * C)
    invf = 1.0 / (10000.0 ** (2.0 * i / C))
    t = np.arange(T, dtype=np.float64)
    scale = zeta[None, :] ** (t[:, None] / 512.0)
    ang = t[:, None] * invf[None, :]
    cos = np.repeat(np.cos(ang) * scale, 2, axis=1).astype(np.float32)
    sin = np.repeat(np.sin(ang) * scale, 2, axis=1).astype(np.float32)
    return cos, sin


def _host_percore(core, x_full, cos, sin):
    g = _gammas()
    pc = {}
    pc["x"] = np.ascontiguousarray(x_full[core * TPC:(core + 1) * TPC])
    cs = np.empty((16, 128, 2048), dtype=np.float32)
    csl = cos[core * TPC:(core + 1) * TPC]
    ssl = sin[core * TPC:(core + 1) * TPC]
    for fc in range(16):
        cs[fc, :, 0:1024] = csl[:, fc * 128:(fc + 1) * 128].T
        cs[fc, :, 1024:2048] = ssl[:, fc * 128:(fc + 1) * 128].T
    pc["cs_img"] = cs
    wt = np.zeros((NCORES, 128, H * 128), dtype=np.float32)
    for b in range(core):
        row = np.concatenate(
            [np.full(128, g[h] ** (1024.0 * (core - 1 - b))) for h in range(H)])
        wt[b] = row.astype(np.float32)[None, :]
    pc["wtab"] = wt
    return pc


def build_nc(taps=(), stop_after="ffn"):
    PH = ["a1", "a2p0", "a2p1", "a2p2", "a2p3", "a2p", "a2c0", "a2", "p5", "post", "ffn"]
    lim = PH.index(stop_after)

    def ph(name):
        return PH.index(name) <= lim

    nc = bass.Bass("TRN2", target_bir_lowering=False)
    g64 = _gammas()
    g128 = [float(np.float32(g64[h] ** 128.0)) for h in range(H)]

    x = nc.dram_tensor("x", [TPC, C], F32, kind="ExternalInput").ap()
    out = nc.dram_tensor("out", [TPC, C], F32, kind="ExternalOutput").ap()
    wqkv_img = nc.dram_tensor("wqkv_img", [12, 128, 8192], F32, kind="ExternalInput").ap()
    wg_img = nc.dram_tensor("wg_img", [4, 128, 8192], F32, kind="ExternalInput").ap()
    wp_img = nc.dram_tensor("wp_img", [4, 128, 8192], F32, kind="ExternalInput").ap()
    wgate_img = nc.dram_tensor("wgate_img", [64, 128, 2048], I8, kind="ExternalInput").ap()
    wup_img = nc.dram_tensor("wup_img", [64, 128, 2048], I8, kind="ExternalInput").ap()
    wdown_img = nc.dram_tensor("wdown_img", [16, 128, 8192], I8, kind="ExternalInput").ap()
    maskT_d = nc.dram_tensor("maskT", [128, 2048], F32, kind="ExternalInput").ap()
    qdecb_d = nc.dram_tensor("qdecb", [128, 2048], F32, kind="ExternalInput").ap()
    ghat_d = nc.dram_tensor("ghat", [128, 2048], F32, kind="ExternalInput").ap()
    pdec_d = nc.dram_tensor("pdec", [NCH, 128, 2048], F32, kind="ExternalInput").ap()
    pmat_d = nc.dram_tensor("pmat", [128, 128], F32, kind="ExternalInput").ap()
    gnw_d = nc.dram_tensor("gnw_b", [128, 2048], F32, kind="ExternalInput").ap()
    gnb_d = nc.dram_tensor("gnb_b", [128, 2048], F32, kind="ExternalInput").ap()
    sdown_d = nc.dram_tensor("sdown_b", [128, 2048], F32, kind="ExternalInput").ap()
    sg_d = nc.dram_tensor("sg_c", [128, 64], F32, kind="ExternalInput").ap()
    su_d = nc.dram_tensor("su_c", [128, 64], F32, kind="ExternalInput").ap()
    cs_d = nc.dram_tensor("cs_img", [16, 128, 2048], F32, kind="ExternalInput").ap()
    wtab_d = nc.dram_tensor("wtab", [NCORES, 128, 2048], F32, kind="ExternalInput").ap()

    d_ln = nc.dram_tensor("d_ln", [TPC, C], BF16)
    d_q = nc.dram_tensor("d_q", [C, TPC], BF16)
    d_k = nc.dram_tensor("d_k", [C, TPC], BF16)
    d_v = nc.dram_tensor("d_v", [C, TPC], BF16)
    d_g = nc.dram_tensor("d_g", [TPC, C], BF16)
    d_x2 = nc.dram_tensor("d_x2", [TPC, C], F32)
    d_ln2 = nc.dram_tensor("d_ln2", [TPC, C], BF16)
    d_wgp = nc.dram_tensor("d_wgp", [8, 128, 8192], BF16)
    cc_in = nc.dram_tensor("cc_in", [H * 128, 128], F32)
    cc_out = nc.dram_tensor("cc_out", [NCORES * H * 128, 128], F32)

    tap_outs = {}
    for tname in taps:
        src = {"d_ln": d_ln, "d_q": d_q, "d_k": d_k, "d_v": d_v, "d_g": d_g,
               "d_x2": d_x2, "d_ln2": d_ln2, "cc_in": cc_in,
               "cc_out": cc_out}.get(tname)
        if src is not None:
            tap_outs[tname] = (
                nc.dram_tensor("tap_" + tname, list(src.shape), src.dtype,
                               kind="ExternalOutput").ap(), src.ap())

    ARENA_KB = 204
    ar = nc.alloc_sbuf_tensor("arena", [128, ARENA_KB * KBf], F32)
    arb = ar.bitcast(BF16)

    def f32(kb, n):
        o = int(kb * KBf)
        return ar[:, o:o + n]

    def bf(kb, n):
        o = int(kb * KBf) * 2
        return arb[:, o:o + n]

    lnf = lambda cc: bf(0, 16 * 1024)[:, cc * 1024:(cc + 1) * 1024]
    ghat_s = bf(0, 2048)
    ZmS = [bf(48, 512), bf(49, 512)]
    lnh = lambda fc: bf(0, 16 * 512)[:, fc * 512:(fc + 1) * 512]
    sgh = lambda tc: bf(16, 4 * 2048)[:, tc * 2048:(tc + 1) * 2048]
    ln2f = lambda cc: bf(0, 16 * 1024)[:, cc * 1024:(cc + 1) * 1024]
    qch = [bf(32, 2048), bf(36, 2048)]
    kch = [bf(40, 2048), bf(44, 2048)]
    ktch = [bf(48, 2048), bf(52, 2048)]
    vch = [bf(56, 2048), bf(60, 2048)]
    pdec_s = f32(40, 2048)
    gfh = lambda fc: bf(48, 16 * 512)[:, fc * 512:(fc + 1) * 512]
    fuse = lambda oc: bf(32, 16 * 1024)[:, oc * 1024:(oc + 1) * 1024]
    xs = [f32(64, 2048), f32(72, 2048)]
    cs_s = [f32(80, 2048), f32(88, 2048)]
    gall = lambda b: f32(64, 8 * 2048)[:, b * 2048:(b + 1) * 2048]
    gnw_s = f32(64, 2048)
    gnb_s = f32(72, 2048)
    x0s = [f32(80, 512), f32(82, 512)]
    x2o = [f32(84, 512), f32(86, 512)]
    yn_s = f32(88, 2048)
    acc = lambda t8: f32(64, 8 * 2048)[:, t8 * 2048:(t8 + 1) * 2048]
    wqb = [bf(128, 8192), bf(144, 8192)]
    ypart = lambda c8: bf(128, 8 * 2048)[:, c8 * 2048:(c8 + 1) * 2048]
    wga = [bf(128, 2048), bf(132, 2048)]
    wua = [bf(136, 2048), bf(140, 2048)]
    wdns = [bf(144, 8192), bf(160, 8192)]
    x2f = [f32(184, 2048), f32(192, 2048)]
    maskT_s = bf(160, 2048)
    qdecb_s = bf(164, 2048)
    wpq = [bf(168, 2048), bf(172, 2048)]
    SP = f32(176, 2048)
    sdown_s = f32(176, 2048)
    PT = bf(184, 2048)
    wtb_s = f32(184, 2048)
    Qhat = bf(188, 2048)
    Tb = bf(192, 2048)
    TbA1 = [bf(192, 512), bf(193, 512)]
    Qp = [bf(194, 512), bf(195, 512)]
    gss = [bf(192, 512), bf(193, 512)]
    T1 = f32(196, 1024)
    st_s1 = f32(200, 64)
    st_s2 = f32(200.25, 64)
    st_mu = f32(200.5, 64)
    st_msq = f32(200.75, 64)
    st_var = f32(201.0, 64)
    st_sq = f32(201.25, 64)
    st_rstd = f32(201.5, 64)
    sgc_s = f32(201.75, 64)
    suc_s = f32(202.0, 64)
    pmat_s = bf(202.25, 128)

    PS = [nc.alloc_psum_tensor(f"ps{i}", [128, 512], F32) for i in range(8)]

    s = Sched()
    K = s.key

    psb = [Buf() for _ in range(8)]

    x_t = x.rearrange("(n p) c -> n p c", p=128)
    out_t = out.rearrange("(n p) c -> n p c", p=128)
    dln_t = d_ln.ap().rearrange("(n p) c -> n p c", p=128)
    dg_t = d_g.ap().rearrange("(n p) c -> n p c", p=128)
    dx2_t = d_x2.ap().rearrange("(n p) c -> n p c", p=128)
    dln2_t = d_ln2.ap().rearrange("(n p) c -> n p c", p=128)
    dq_rows = d_q.ap().rearrange("(n p) t -> n p t", p=128)
    dk_rows = d_k.ap().rearrange("(n p) t -> n p t", p=128)
    dv_rows = d_v.ap().rearrange("(n p) t -> n p t", p=128)

    def dq_col(dram, c8):
        return dram.ap().rearrange("(h d) t -> d h t", d=128)[:, :, c8 * 128:(c8 + 1) * 128]

    k_maskT = K("maskT"); s.dma("gp", "tb", maskT_s, maskT_d, key=k_maskT, max_dma_last_dim=8192)
    k_qdecb = K("qdecb"); s.dma("gp", "tb2", qdecb_s, qdecb_d, key=k_qdecb, max_dma_last_dim=8192)
    k_pmat = K("pmat"); s.dma("gp", "tb3", pmat_s, pmat_d, key=k_pmat, max_dma_last_dim=8192)
    k_sgc = K("sgc"); s.dma("sy", "tb4", sgc_s, sg_d, key=k_sgc)
    k_suc = K("suc"); s.dma("sy", "tb5", suc_s, su_d, key=k_suc)

    wq_keys = {}
    for half in range(2):
        k = K("wq"); wq_keys[(0, half)] = k
        s.dma("gp", f"wq{0}{half}", wqb[0][:, half * 4096:(half + 1) * 4096],
              wqkv_img[0][:, half * 4096:(half + 1) * 4096], key=k,
              max_dma_last_dim=8192)
    b_wqb = [Buf(), Buf()]
    b_wqb[0].writer = wq_keys[(0, 1)]

    cs_keys = {}
    b_cs = [Buf(), Buf()]
    for fcg in range(2):
        k = K("cs"); cs_keys[fcg] = k
        s.dma("sy", f"cs{fcg}", cs_s[fcg], cs_d[fcg], key=k)
        b_cs[fcg].writer = k

    b_xs = [Buf(), Buf()]
    b_Tb = Buf()

    def ln_pass(tag, src_t, dst_t, src_pre, first_extra, tiles=range(8),
                bufs=None, ldq=("sy", "sy")):
        if bufs is None:
            xs_l, Tb_l, b_xs_l, b_Tb_l = xs, Tb, b_xs, b_Tb
        else:
            xs_l, Tb_l, b_xs_l, b_Tb_l = bufs
        wkeys = []
        for tc in tiles:
            sl = tc % 2
            kx = K(tag + "x")
            aft = b_xs_l[sl].wr(kx) + (src_pre(tc) or []) + \
                (first_extra if tc < min(tiles) + 2 else [])
            s.dma(ldq[sl], f"{tag}x{sl}", xs_l[sl], src_t[tc], key=kx, after=aft)
            k1 = K(); s.dv.reduce_sum(st_s1[:, tc:tc + 1], xs_l[sl], AX.X).after(
                *b_xs_l[sl].rd(k1)).inc("dv", key=k1)
            k2 = K(); r = s.dv.scalar_tensor_tensor(
                Tb_l[:, :], xs_l[sl], 1.0, xs_l[sl], ALU.mult, ALU.mult,
                accum_out=st_s2[:, tc:tc + 1])
            r.after(*(b_xs_l[sl].rd(k2) + b_Tb_l.wr(k2))).inc("dv", key=k2)
            kmu = K(); s.dv.tensor_scalar(st_mu[:, tc:tc + 1], st_s1[:, tc:tc + 1],
                                          1.0 / C, None, ALU.mult).after(k1).inc(
                "dv", key=kmu)
            kms = K(); s.dv.scalar_tensor_tensor(
                st_msq[:, tc:tc + 1], st_mu[:, tc:tc + 1], 1.0,
                st_mu[:, tc:tc + 1], ALU.mult, ALU.mult).after(kmu).inc("dv", key=kms)
            kv = K(); s.dv.scalar_tensor_tensor(
                st_var[:, tc:tc + 1], st_s2[:, tc:tc + 1], 1.0 / C,
                st_msq[:, tc:tc + 1], ALU.mult, ALU.subtract).after(
                k2, kms).inc("dv", key=kv)
            kv2 = K(); s.dv.tensor_scalar(st_var[:, tc:tc + 1], st_var[:, tc:tc + 1],
                                          EPS, None, ALU.add).after(kv).inc("dv", key=kv2)
            kq = K(); s.ac.activation(st_sq[:, tc:tc + 1], st_var[:, tc:tc + 1],
                                      AF.Sqrt).after(kv2).inc("ac", key=kq)
            krc = K(); s.dv.reciprocal(st_rstd[:, tc:tc + 1],
                                       st_sq[:, tc:tc + 1]).after(kq).inc("dv", key=krc)
            kn = K(tag + "N")
            r = s.dv.tensor_scalar(Tb_l[:, :], xs_l[sl], st_mu[:, tc:tc + 1],
                                   st_rstd[:, tc:tc + 1], ALU.subtract, ALU.mult)
            r.after(*(b_xs_l[sl].rd(kn) + b_Tb_l.wr(kn) + [krc, kmu])).inc("dv", key=kn)
            kw = K(tag + "w")
            s.dma("sy", tag + "w", dst_t[tc], Tb_l[:, :], key=kw,
                  after=b_Tb_l.rd(kw) + [kn])
            wkeys.append(kw)
        return wkeys

    ln1_w = ln_pass("L1", x_t, dln_t, lambda tc: [], [], ldq=("sy", "ac"))

    b_lnf = Buf()
    klnf = []
    for th in range(2):
        for fc in range(16):
            k = K("lnfld"); klnf.append(k)
            s.dma("ac", "lnfld", lnf(fc)[:, th * 512:(th + 1) * 512],
                  d_ln.ap()[th * 512:(th + 1) * 512, fc * 128:(fc + 1) * 128],
                  transpose=True, key=k, after=[ln1_w[th * 4 + 3]])
    b_lnf.writer = klnf[-1]
    lnf_all = klnf
    lnf_half = [klnf[0:16], klnf[16:32]]

    b_tb1 = [Buf(), Buf()]
    b_qp = [Buf(), Buf()]
    b_T1 = [Buf(), Buf()]
    for _b in b_tb1 + b_qp:
        _b.readers += b_Tb.readers + [b_Tb.writer]
    last_dqw = {0: None, 1: None}
    last_dvw = {0: None, 1: None}

    for f4 in range(12):
        wpar = f4 % 2
        if f4 + 1 < 12:
            npar = (f4 + 1) % 2
            for half in range(2):
                k = K("wq"); wq_keys[(f4 + 1, half)] = k
                aft = b_wqb[npar].wr(k) if half == 0 else []
                s.dma("gp", f"wq{npar}{half}",
                      wqb[npar][:, half * 4096:(half + 1) * 4096],
                      wqkv_img[f4 + 1][:, half * 4096:(half + 1) * 4096],
                      key=k, after=aft, max_dma_last_dim=8192)
        for fc in range(4):
            fcg = f4 * 4 + fc
            h = fcg % 16
            kind = "q" if fcg < 16 else ("k" if fcg < 32 else "v")
            kmm = {0: K("a1mm"), 1: K("a1mm")}
            kraw = {}
            pbank = {0: (fc % 2) * 2, 1: (fc % 2) * 2 + 1}
            for tt in range(2):
                for cc in range(16):
                    half = 0 if cc < 8 else 1
                    aft = [wq_keys[(f4, half)]]
                    if f4 == 0 and fc == 0 and cc == 0:
                        aft += lnf_half[tt]
                    if cc == 0:
                        aft += psb[pbank[tt]].wr(kmm[tt])
                    r = s.pe.matmul(PS[pbank[tt]][:, :],
                                    wqb[wpar][:, cc * 512 + fc * 128:cc * 512 + (fc + 1) * 128],
                                    lnf(cc)[:, tt * 512:(tt + 1) * 512],
                                    start=(cc == 0), stop=(cc == 15)).after(*aft)
                    b_lnf.readers.append(kmm[tt])
                    if cc == 15:
                        b_wqb[wpar].rd(kmm[tt])
                        r.inc("pe", key=kmm[tt])
                        last_mm = kmm[tt]
            for tt in range(2):
                kraw[tt] = K("a1raw")
                aft = [kmm[tt]] + b_tb1[tt].wr(kraw[tt]) + psb[pbank[tt]].rd(kraw[tt])
                s.ac.activation(TbA1[tt], PS[pbank[tt]][:, :], AF.Copy,
                                scale=(RD if kind == "q" else 1.0)).after(*aft).inc(
                    "ac", key=kraw[tt])
            for tt in range(2):
                if kind == "v":
                    kd = K("dvw")
                    s.dma("sy", f"dvw{tt}", dv_rows[h][:, tt * 512:(tt + 1) * 512],
                          TbA1[tt], key=kd, after=b_tb1[tt].rd(kd) + [kraw[tt]])
                    last_dvw[tt] = kd
                else:
                    krot = K("a1rot")
                    aft = [kraw[tt], k_pmat] + psb[4 + tt].wr(krot) + b_tb1[tt].rd(krot)
                    s.pe.matmul(PS[4 + tt][:, :], pmat_s, TbA1[tt],
                                start=True, stop=True).after(*aft).inc("pe", key=krot)
                    cpar = fcg % 2
                    kt1 = K("a1t1")
                    aft = [kraw[tt], cs_keys[fcg]] + b_cs[cpar].rd(kt1) + \
                        b_tb1[tt].rd(kt1) + b_T1[0].wr(kt1)
                    s.dv.tensor_tensor(T1[:, 0:512], TbA1[tt],
                                       cs_s[cpar][:, tt * 512:(tt + 1) * 512],
                                       ALU.mult).after(*aft).inc("dv", key=kt1)
                    kt2 = K("a1t2")
                    aft = [krot, cs_keys[fcg]] + b_cs[cpar].rd(kt2) + \
                        psb[4 + tt].rd(kt2) + b_T1[1].wr(kt2)
                    s.dv.tensor_tensor(T1[:, 512:1024], PS[4 + tt][:, :],
                                       cs_s[cpar][:, 1024 + tt * 512:1024 + (tt + 1) * 512],
                                       ALU.mult).after(*aft).inc("dv", key=kt2)
                    kqp = K("a1qp")
                    s.dv.tensor_tensor(Qp[tt], T1[:, 0:512], T1[:, 512:1024],
                                       ALU.add).after(kt1, kt2, *(b_qp[tt].wr(kqp) +
                                       b_T1[0].rd(kqp) + b_T1[1].rd(kqp))).inc("dv", key=kqp)
                    dst = dq_rows[h] if kind == "q" else dk_rows[h]
                    kd = K("dqw")
                    s.dma("sy", f"dqw{tt}", dst[:, tt * 512:(tt + 1) * 512],
                          Qp[tt], key=kd, after=b_qp[tt].rd(kd) + [kqp])
                    last_dqw[tt] = kd
            nxt = fcg + 2
            if 2 <= nxt < 32:
                k = K("cs"); cs_keys[nxt] = k
                s.dma("sy", f"cs{nxt % 2}", cs_s[nxt % 2], cs_d[nxt % 16], key=k,
                      after=b_cs[nxt % 2].wr(k))

    tap_ready = {"d_ln": [ln1_w[7]],
                 "d_q": [last_dqw[0], last_dqw[1]],
                 "d_k": [last_dqw[0], last_dqw[1]],
                 "d_v": [last_dvw[0], last_dvw[1]]}
    def _finish(ow_keys):
        fin_keys = list(ow_keys)
        for tname, (tdst, tsrc) in tap_outs.items():
            rdy = tap_ready.get(tname)
            if rdy is None:
                continue
            kt = K("tap")
            s.dma("sy", "tap" + tname, tdst, tsrc, key=kt,
                  after=rdy + list(ow_keys))
            fin_keys.append(kt)
        r = s.sy.nop().after(*fin_keys)
        for name, v in s.cnt.items():
            if name.startswith("D"):
                if r.op.waits.get(name, -1) < v:
                    r.op.waits[name] = v
        build_nc._last_sched = s
        sems = {}
        for name in s.cnt:
            sems[name] = nc.alloc_semaphore("s_" + name.replace(":", "_"))
        with nc.Block("main") as blk:
            def mk(engname):
                def f(e):
                    seen = {}
                    for op in s.q[engname]:
                        for (sem, v) in sorted(op.waits.items()):
                            if seen.get(sem, -1) < v:
                                e.wait_ge(sems[sem], v)
                                seen[sem] = v
                        inst = getattr(e, op.method)(*op.args, **op.kwargs)
                        for (sem, st) in op.incs:
                            inst.then_inc(sems[sem], st)
                return f
            blk.tensor(mk("pe"))
            blk.vector(mk("dv"))
            blk.scalar(mk("ac"))
            blk.sync(mk("sy"))
            blk.gpsimd(mk("gp"))
        return nc

    if not ph("a2p"):
        return _finish([])

    kbar = last_mm
    k_ghat = K("ghat")
    s.dma("gp", "ghld", ghat_s, ghat_d, key=k_ghat,
          after=list(b_lnf.readers) + [kbar], max_dma_last_dim=8192)

    b_wpq = [Buf(), Buf()]
    wgp_keys = {}

    def precast_round(i):
        idx, p = divmod(i, 4)
        srcimg = wg_img[idx] if idx < 4 else wp_img[idx - 4]
        kc = K("wgc")
        s.dma("gp", f"wgc{i % 2}", wpq[i % 2],
              srcimg[:, p * 2048:(p + 1) * 2048], key=kc,
              after=b_wpq[i % 2].wr(kc), max_dma_last_dim=8192)
        kw = K("wgw")
        s.dma("sy", f"wgw{i % 2}",
              d_wgp.ap()[idx][:, p * 2048:(p + 1) * 2048], wpq[i % 2],
              key=kw, after=b_wpq[i % 2].rd(kw) + [kc])
        wgp_keys[(idx, p)] = kw

    b_SP = Buf()
    ksr = K("srun0")
    s.dv.memset(SP[:, :], 0.0).inc("dv", key=ksr)
    b_SP.writer = ksr

    b_qch = [Buf(), Buf()]
    b_kch = [Buf(), Buf()]
    b_ktch = [Buf(), Buf()]
    b_vch = [Buf(), Buf()]
    b_Qhat = Buf()
    b_Zm = [Buf(), Buf()]
    b_yp = [Buf() for _ in range(NCH)]

    chunk_keys = {}

    dqk_done = [last_dqw[0], last_dqw[1]]
    dv_done = [last_dvw[0], last_dvw[1]]

    def ld_chunk(c8, parts="qktv"):
        par = c8 % 2
        kq = K("qch"); kk = K("kch"); kt = K("ktch"); kv = K("vch")
        chunk_keys[c8] = dict(q=kq, k=kk, kt=kt, v=kv)
        if "q" in parts:
            s.dma("sy", f"qch{par}", qch[par].rearrange("p (h t) -> p h t", h=16),
                  dq_col(d_q, c8), key=kq,
                  after=b_qch[par].wr(kq) + dqk_done)
        if "k" in parts:
            s.dma("sy", f"kch{par}", kch[par].rearrange("p (h t) -> p h t", h=16),
                  dq_col(d_k, c8), key=kk,
                  after=b_kch[par].wr(kk) + dqk_done)
        if "t" in parts:
            s.dma("ac", f"ktch{par}", ktch[par],
                  d_k.ap()[:, c8 * 128:(c8 + 1) * 128], transpose=True,
                  key=kt, after=b_ktch[par].wr(kt) + dqk_done)
        if "v" in parts:
            s.dma("ac", f"vch{par}", vch[par],
                  d_v.ap()[:, c8 * 128:(c8 + 1) * 128], transpose=True,
                  key=kv, after=b_vch[par].wr(kv) + dv_done)

    if stop_after == "a2p0":
        return _finish([])
    ld_chunk(0, parts="tv")
    if stop_after == "a2p1":
        return _finish([])
    if stop_after == "a2p2":
        ld_chunk(1, parts="qk")
        return _finish([])
    if stop_after == "a2p3":
        ld_chunk(1, parts="tv")
        return _finish([])
    ld_chunk(1, parts="tv")
    if stop_after == "a2p":
        return _finish([])

    sbf_of = {}
    ye_of = {}
    for c8 in range(NCH):
        par = c8 % 2
        ck = chunk_keys[c8]
        kkh = K("a2kh")
        aft = [ck["kt"], k_ghat] + b_ktch[par].rd(kkh)
        b_ktch[par].writer = kkh
        b_ktch[par].readers = []
        s.gp.tensor_tensor(ktch[par], ktch[par], ghat_s, ALU.mult).after(
            *aft).inc("gp", key=kkh)
        if c8 >= 1:
            for grp in range(4):
                gs4 = slice(grp * 512, (grp + 1) * 512)
                ksb = K("a2sb")
                aft = b_SP.rd(ksb)
                if grp == 0:
                    aft += b_yp[c8].wr(ksb)
                s.ac.activation(ypart(c8)[:, gs4], SP[:, gs4], AF.Copy).after(
                    *aft).inc("ac", key=ksb)
                b_yp[c8].writer = ksb
                sbf_of[(c8, grp)] = ksb
        for grp in range(4):
            pz = (c8 * 4 + grp) % 2
            ksc = K("a2sc")
            for q in range(4):
                h = grp * 4 + q
                hs = slice(h * 128, (h + 1) * 128)
                aft = [kkh, ck["v"]] + b_ktch[par].rd(ksc) + b_vch[par].rd(ksc)
                if q == 0:
                    aft += psb[4 + pz].wr(ksc)
                r = s.pe.matmul(PS[4 + pz][:, q * 128:(q + 1) * 128],
                                ktch[par][:, hs], vch[par][:, hs],
                                start=True, stop=True).after(*aft)
                if q == 3:
                    r.inc("pe", key=ksc)
            for q in range(4):
                h = grp * 4 + q
                hs = slice(h * 128, (h + 1) * 128)
                ksu = K("a2su")
                aft = [ksc] + psb[4 + pz].rd(ksu)
                if c8 == 0 and grp == 0 and q == 0:
                    aft.append(ksr)
                if c8 >= 1:
                    aft.append(sbf_of[(c8, grp)])
                b_SP.readers.append(ksu)
                s.dv.scalar_tensor_tensor(SP[:, hs], SP[:, hs], g128[h],
                                          PS[4 + pz][:, q * 128:(q + 1) * 128],
                                          ALU.mult, ALU.add).after(*aft).inc(
                    "dv", key=ksu)
                if h == 15:
                    b_SP.writer = ksu
                    b_SP.readers = []
        if c8 + 2 < NCH:
            ld_chunk(c8 + 2, parts="tv")
        for i in range(c8 * 2, c8 * 2 + 2):
            precast_round(i)
        if stop_after == "a2c0":
            break

    kccw = K("ccw")
    s.dma("sy", "ccw", cc_in.ap().rearrange("(h d) e -> d h e", d=128),
          SP[:, :].rearrange("p (h e) -> p h e", h=16), key=kccw,
          after=b_SP.rd(kccw))
    tap_ready["cc_in"] = [kccw]
    kcc = K("cc")
    s.gp.collective_compute(
        "AllGather", ALU.bypass, replica_groups=[list(range(NCORES))],
        ins=[cc_in.ap().opt()], outs=[cc_out.ap().opt()]).after(kccw).inc(
        "cc", key=kcc)

    b_gall = Buf()
    gall_keys = []
    cc_out_r = cc_out.ap().rearrange("(b h d) e -> b d h e", d=128, h=16)
    for c8 in range(NCH):
        par = c8 % 2
        kq = K("qch"); kk = K("kch"); kv = K("vch2")
        s.dma("sy", f"qch{par}", qch[par].rearrange("p (h t) -> p h t", h=16),
              dq_col(d_q, c8), key=kq,
              after=b_qch[par].wr(kq) + dqk_done)
        s.dma("ac", f"kch{par}", kch[par].rearrange("p (h t) -> p h t", h=16),
              dq_col(d_k, c8), key=kk,
              after=b_kch[par].wr(kk) + dqk_done)
        s.dma("ac", f"vch{par}", vch[par],
              d_v.ap()[:, c8 * 128:(c8 + 1) * 128], transpose=True,
              key=kv, after=b_vch[par].wr(kv))
        kqh = K("a2qh")
        aft = [kq, k_qdecb] + b_Qhat.wr(kqh) + b_qch[par].rd(kqh)
        s.dv.tensor_tensor(Qhat, qch[par], qdecb_s, ALU.mult).after(*aft).inc(
            "dv", key=kqh)
        for grp in range(4):
            pz = (c8 * 4 + grp) % 2
            gs4 = slice(grp * 512, (grp + 1) * 512)
            kz = K("a2z")
            for q in range(4):
                h = grp * 4 + q
                hs = slice(h * 128, (h + 1) * 128)
                aft = [kk, kq] + b_kch[par].rd(kz) + b_qch[par].rd(kz)
                if q == 0:
                    aft += psb[pz].wr(kz)
                r = s.pe.matmul(PS[pz][:, q * 128:(q + 1) * 128],
                                kch[par][:, hs], qch[par][:, hs],
                                start=True, stop=True).after(*aft)
                if q == 3:
                    r.inc("pe", key=kz)
            kzm = K("a2zm")
            aft = [kz, k_maskT] + b_Zm[pz].wr(kzm) + psb[pz].rd(kzm)
            if c8 == 0 and grp < 2:
                aft += b_ktch[0].readers + [b_ktch[0].writer]
            s.dv.tensor_tensor(ZmS[pz], PS[pz][:, 0:512], maskT_s[:, gs4],
                               ALU.mult).after(*aft).inc("dv", key=kzm)
            kyg = K("a2y2")
            for q in range(4):
                h = grp * 4 + q
                hs = slice(h * 128, (h + 1) * 128)
                qsl = slice(q * 128, (q + 1) * 128)
                aft = [kzm, kv] + b_Zm[pz].rd(kyg) + b_vch[par].rd(kyg)
                if q == 0:
                    aft += psb[2 + pz].wr(kyg)
                    b_yp[c8].readers.append(kyg)
                r = s.pe.matmul(PS[2 + pz][:, qsl], ZmS[pz][:, qsl],
                                vch[par][:, hs], start=True,
                                stop=(c8 == 0)).after(*aft)
                if c8 == 0 and q == 3:
                    r.inc("pe", key=kyg)
                if c8 >= 1:
                    aft2 = [kqh, sbf_of[(c8, grp)]] + b_Qhat.rd(kyg)
                    r = s.pe.matmul(PS[2 + pz][:, qsl], Qhat[:, hs],
                                    ypart(c8)[:, hs], start=False,
                                    stop=True).after(*aft2)
                    if q == 3:
                        r.inc("pe", key=kyg)
            kye = K("a2ye")
            aft = [kyg] + psb[2 + pz].rd(kye)
            if grp == 0:
                aft += b_yp[c8].wr(kye)
            s.ac.activation(ypart(c8)[:, gs4], PS[2 + pz][:, 0:512],
                            AF.Copy).after(*aft).inc("ac", key=kye)
            b_yp[c8].writer = kye
            ye_of[(c8, grp)] = kye
        if c8 == 1:
            for b in range(NCORES):
                kg_ = K("gall"); gall_keys.append(kg_)
                aft = [kcc]
                if b == 0:
                    aft += b_xs[0].readers + b_xs[1].readers + \
                        b_cs[0].readers + b_cs[1].readers
                s.dma("gp", "gall", gall(b).rearrange("p (h e) -> p h e", h=16),
                      cc_out_r[b], key=kg_, after=aft)
            b_gall.writer = gall_keys[-1]
        if stop_after == "a2c0":
            break

    if not ph("p5"):
        return _finish([])

    wgq_keys = {}
    for qb in range(2):
        k = K("wg"); wgq_keys[(0, 0, qb)] = k
        s.dma("sy", f"wpq{qb}", wpq[qb], d_wgp.ap()[0][:, qb * 2048:(qb + 1) * 2048],
              key=k, after=b_wpq[qb].wr(k) + [wgp_keys[(0, qb)]])
    b_wtb = Buf()
    b_yn = Buf()
    ptmp_s = f32(48, 2048)
    b_ptmp = Buf()
    pcm = None
    for b in range(NCORES):
        kw = K("wtb")
        aft = b_wtb.wr(kw)
        if b == 0:
            aft += b_Qhat.readers + [b_Qhat.writer]
        s.dma("gp", "wtb", wtb_s, wtab_d[b], key=kw, after=aft)
        if b == 0:
            kp = K("pcm")
            aft = [kccw, kw] + gall_keys + b_SP.wr(kp) + b_wtb.rd(kp) + b_gall.rd(kp)
            s.dv.tensor_tensor(SP[:, :], gall(0), wtb_s, ALU.mult).after(
                *aft).inc("dv", key=kp)
            pcm = kp
        else:
            kt = K()
            aft = [kw] + b_ptmp.wr(kt) + b_wtb.rd(kt) + b_gall.rd(kt)
            if b == 1:
                aft += b_ktch[0].readers + b_ktch[1].readers + \
                    b_vch[0].readers + b_vch[1].readers + \
                    b_Zm[0].readers + b_Zm[1].readers + \
                    [b_ktch[0].writer, b_ktch[1].writer,
                     b_vch[0].writer, b_vch[1].writer,
                     b_Zm[0].writer, b_Zm[1].writer]
            s.dv.tensor_tensor(ptmp_s, gall(b), wtb_s, ALU.mult).after(*aft).inc(
                "dv", key=kt)
            kp = K("pcm")
            b_SP.readers.append(kp)
            s.dv.tensor_tensor(SP[:, :], SP[:, :], ptmp_s, ALU.add).after(
                kt, *b_ptmp.rd(kp)).inc("dv", key=kp)
            b_SP.writer = kp
            b_SP.readers = []
            pcm = kp

    kgnw = K("gnw")
    s.dma("gp", "gnw", gnw_s, gnw_d, key=kgnw, after=[pcm] + b_gall.readers)
    kgnb = K("gnb")
    s.dma("gp", "gnb", gnb_s, gnb_d, key=kgnb, after=[pcm])
    b_gnw = Buf(); b_gnw.writer = kgnw
    b_gnb = Buf(); b_gnb.writer = kgnb
    if not ph("post"):
        return _finish([])

    b_lnh = Buf()
    b_sgh = [Buf() for _ in range(4)]
    b_gfh = Buf()
    b_gfh.readers += b_ptmp.readers + [b_ptmp.writer]
    xs2 = [f32(96, 2048), f32(104, 2048)]
    Tb2 = bf(112, 2048)
    b_xs2 = [Buf(), Buf()]
    b_Tb2 = Buf()
    ln2_w = []

    def l2_pre2(tc8):
        hf_, tc_ = divmod(tc8, 4)
        return [x2w_keys[(hf_, c4_, tc_)] for c4_ in range(4)]
    b_PT = Buf()
    b_pdec = Buf()
    b_x0 = [Buf(), Buf()]
    b_x2o = [Buf(), Buf()]
    b_stats = Buf()
    x2w_keys = {}

    for hf in range(2):
        kl = []
        for fc in range(16):
            k = K("lnh"); kl.append(k)
            aft = [ln1_w[7]]
            if fc == 0:
                aft += b_lnh.wr(k)
                if hf == 0:
                    aft += ([k_ghat, ye_of[(7, 3)]] +
                            b_ktch[0].readers + b_ktch[1].readers +
                            [b_ktch[0].writer, b_ktch[1].writer])
            s.dma("ac", "lnh", lnh(fc),
                  d_ln.ap()[hf * 512:(hf + 1) * 512, fc * 128:(fc + 1) * 128],
                  transpose=True, key=k, after=aft)
        b_lnh.writer = kl[-1]
        lnh_all = kl
        gse = {}
        for c4 in range(4):
            for qb in range(2):
                if hf == 0 and c4 == 0 and qb < 2:
                    continue
                k = K("wg"); wgq_keys[(hf, c4, qb)] = k
                s.dma("sy", f"wpq{qb % 2}", wpq[qb % 2],
                      d_wgp.ap()[c4][:, qb * 2048:(qb + 1) * 2048],
                      key=k, after=b_wpq[qb % 2].wr(k) + [wgp_keys[(c4, qb)]])
            kg = {}
            for cc in range(16):
                qb = cc // 4
                for tc in range(4):
                    if cc == 0:
                        kg[tc] = K("gmm")
                    aft = [wgq_keys[(hf, c4, qb)]]
                    if c4 == 0 and cc == 0:
                        aft += lnh_all
                    if cc == 0:
                        aft += psb[tc].wr(kg[tc])
                    r = s.pe.matmul(PS[tc][:, :],
                                    lnh(cc)[:, tc * 128:(tc + 1) * 128],
                                    wpq[qb % 2][:, (cc % 4) * 512:(cc % 4 + 1) * 512],
                                    start=(cc == 0), stop=(cc == 15)).after(*aft)
                    b_lnh.readers.append(kg[tc])
                    if cc % 4 == 3 and tc == 3:
                        if cc == 15:
                            b_wpq[qb % 2].rd(kg[tc])
                        else:
                            kqr = K("gqr")
                            r.inc("pe", key=kqr)
                            b_wpq[qb % 2].rd(kqr)
                            nqb = qb + 2
                            if nqb < 4:
                                k2 = K("wg"); wgq_keys[(hf, c4, nqb)] = k2
                                s.dma("sy", f"wpq{nqb % 2}", wpq[nqb % 2],
                                      d_wgp.ap()[c4][:, nqb * 2048:(nqb + 1) * 2048],
                                      key=k2, after=b_wpq[nqb % 2].wr(k2) +
                                      [wgp_keys[(c4, nqb)]])
                    if cc == 15:
                        r.inc("pe", key=kg[tc])
            for tc in range(4):
                ks = K("gse")
                aft = [kg[tc]] + b_sgh[tc].wr(ks) + psb[tc].rd(ks)
                s.ac.activation(sgh(tc)[:, c4 * 512:(c4 + 1) * 512], PS[tc][:, :],
                                AF.Silu).after(*aft).inc("ac", key=ks)
                gse[(c4, tc)] = ks
        for tc in range(4):
            c8g = hf * 4 + tc
            par = tc % 2
            kq = K("qch2")
            s.dma("sy", f"qch{par}", qch[par].rearrange("p (h t) -> p h t", h=16),
                  dq_col(d_q, c8g), key=kq, after=b_qch[par].wr(kq))
            kp = K("pdec")
            aft = b_pdec.wr(kp)
            if hf == 0 and tc == 0:
                aft += b_kch[0].readers + b_kch[1].readers + \
                    [b_kch[0].writer, b_kch[1].writer]
            s.dma("sy", "pdec", pdec_s, pdec_d[c8g], key=kp, after=aft)
            kqh = K("qh2")
            aft = [kq, k_qdecb] + b_Qhat.wr(kqh) + b_qch[par].rd(kqh)
            s.dv.tensor_tensor(Qhat, qch[par], qdecb_s, ALU.mult).after(
                *aft).inc("dv", key=kqh)
            kpt = K("ptl")
            aft = [kp, pcm] + b_PT.wr(kpt) + b_pdec.rd(kpt) + b_SP.rd(kpt)
            s.dv.tensor_tensor(PT, SP[:, :], pdec_s, ALU.mult).after(*aft).inc(
                "dv", key=kpt)
            for h in range(16):
                hs = slice(h * 128, (h + 1) * 128)
                kmm = K("pth")
                aft = [kqh, kpt] + psb[6 + h % 2].wr(kmm) + \
                    b_Qhat.rd(kmm) + b_PT.rd(kmm)
                s.pe.matmul(PS[6 + h % 2][:, 0:128], Qhat[:, hs], PT[:, hs],
                            start=True, stop=True).after(*aft).inc("pe", key=kmm)
                ky = K("ysl")
                aft = [kmm, ye_of[(c8g, h // 4)]] + psb[6 + h % 2].rd(ky)
                b_yp[c8g].readers.append(ky)
                s.dv.tensor_tensor(ypart(c8g)[:, hs], ypart(c8g)[:, hs],
                                   PS[6 + h % 2][:, 0:128], ALU.add).after(
                    *aft).inc("dv", key=ky)
                b_yp[c8g].writer = ky
                j = tc * 16 + h
                krd = K()
                s.dv.reduce_sum(st_s1[:, j:j + 1], ypart(c8g)[:, hs],
                                AX.X).after(ky).inc("dv", key=krd)
                kst = K("yst")
                r = s.dv.scalar_tensor_tensor(
                    Tb[:, 0:128], ypart(c8g)[:, hs], 1.0, ypart(c8g)[:, hs],
                    ALU.mult, ALU.mult, accum_out=st_s2[:, j:j + 1])
                r.after(*(b_Tb.wr(kst) + [ky])).inc("dv", key=kst)
                last_sweep = (krd, kst)
        kmu_b = K()
        s.dv.tensor_scalar(st_mu[:, :], st_s1[:, :], 1.0 / D, None,
                           ALU.mult).after(*last_sweep).inc("dv", key=kmu_b)
        kms_b = K()
        s.dv.scalar_tensor_tensor(st_msq[:, :], st_mu[:, :], 1.0, st_mu[:, :],
                                  ALU.mult, ALU.mult).after(kmu_b).inc("dv", key=kms_b)
        kv = K("gnv")
        s.dv.scalar_tensor_tensor(st_var[:, :], st_s2[:, :], 1.0 / D,
                                  st_msq[:, :], ALU.mult, ALU.subtract).after(
            kms_b).inc("dv", key=kv)
        kv2 = K()
        s.dv.tensor_scalar(st_var[:, :], st_var[:, :], EPS, None,
                           ALU.add).after(kv).inc("dv", key=kv2)
        kq_ = K("gnq")
        s.ac.activation(st_sq[:, :], st_var[:, :], AF.Sqrt).after(
            kv2).inc("ac", key=kq_)
        kr = K("gnr")
        s.dv.reciprocal(st_rstd[:, :], st_sq[:, :]).after(kq_).inc("dv", key=kr)
        grow_keys = []
        for tc in range(4):
            c8g = hf * 4 + tc
            for h in range(16):
                hs = slice(h * 128, (h + 1) * 128)
                j = tc * 16 + h
                kn = K()
                aft = [kr, kmu_b] + (b_yn.wr(kn) if h == 0 else [])
                b_yp[c8g].readers.append(kn)
                s.dv.tensor_scalar(yn_s[:, hs], ypart(c8g)[:, hs],
                                   st_mu[:, j:j + 1], st_rstd[:, j:j + 1],
                                   ALU.subtract, ALU.mult).after(*aft).inc("dv", key=kn)
                kn_last = kn
            ka = K()
            s.dv.tensor_tensor(yn_s, yn_s, gnw_s, ALU.mult).after(
                kgnw, kn_last).inc("dv", key=ka)
            b_gnw.readers.append(ka)
            kb = K()
            s.dv.tensor_tensor(yn_s, yn_s, gnb_s, ALU.add).after(
                kgnb, ka).inc("dv", key=kb)
            b_gnb.readers.append(kb)
            kgr = K("grow")
            aft = [gse[(3, tc)], kb] + b_Tb.wr(kgr) + b_sgh[tc].rd(kgr)
            s.dv.tensor_tensor(Tb[:, :], yn_s, sgh(tc), ALU.mult).after(
                *aft).inc("dv", key=kgr)
            b_yn.readers.append(kgr)
            kw = K("dgw")
            s.dma("sy", "dgw", dg_t[c8g], Tb[:, :], key=kw,
                  after=b_Tb.rd(kw) + [kgr])
            grow_keys.append(kw)
        kgf = []
        for fc in range(16):
            k = K("gfh"); kgf.append(k)
            aft = [grow_keys[3]]
            if fc == 0:
                aft += b_gfh.wr(k)
                if hf == 0:
                    aft += b_ktch[0].readers + b_ktch[1].readers + \
                        b_vch[0].readers + b_vch[1].readers + \
                        b_Zm[0].readers + b_Zm[1].readers + \
                        [b_ktch[0].writer, b_ktch[1].writer,
                         b_vch[0].writer, b_vch[1].writer,
                         b_Zm[0].writer, b_Zm[1].writer]
            s.dma("ac", "gfh", gfh(fc),
                  d_g.ap()[hf * 512:(hf + 1) * 512, fc * 128:(fc + 1) * 128],
                  transpose=True, key=k, after=aft)
        b_gfh.writer = kgf[-1]
        gfh_all = kgf
        for c4 in range(4):
            for qb in range(2):
                k = K("wp"); wgq_keys[("p", hf, c4, qb)] = k
                s.dma("gp", f"wpc{qb % 2}", wpq[qb % 2],
                      wp_img[c4][:, qb * 2048:(qb + 1) * 2048],
                      key=k, after=b_wpq[qb % 2].wr(k), max_dma_last_dim=8192)
            kpj = {}
            for tc in range(2):
                sl = tc % 2
                kx = K("x0")
                s.dma("sy", f"x0{sl}", x0s[sl],
                      bass.AP(x.tensor, (hf * 512 + tc * 128) * C + c4 * 512,
                              [[C, 128], [1, 512]]),
                      key=kx, after=b_x0[sl].wr(kx))
                wgq_keys[("x0", hf, c4, tc)] = kx
            for cc in range(16):
                qb = cc // 4
                for tc in range(4):
                    if cc == 0:
                        kpj[tc] = K("pjmm")
                    aft = [wgq_keys[("p", hf, c4, qb)]]
                    if c4 == 0 and cc == 0:
                        aft += gfh_all
                    if cc == 0:
                        aft += psb[tc].wr(kpj[tc])
                    r = s.pe.matmul(PS[tc][:, :],
                                    gfh(cc)[:, tc * 128:(tc + 1) * 128],
                                    wpq[qb % 2][:, (cc % 4) * 512:(cc % 4 + 1) * 512],
                                    start=(cc == 0), stop=(cc == 15)).after(*aft)
                    b_gfh.readers.append(kpj[tc])
                    if cc % 4 == 3 and tc == 3:
                        if cc == 15:
                            b_wpq[qb % 2].rd(kpj[tc])
                        else:
                            kqr = K("pqr")
                            r.inc("pe", key=kqr)
                            b_wpq[qb % 2].rd(kqr)
                            nqb = qb + 2
                            if nqb < 4:
                                k2 = K("wp"); wgq_keys[("p", hf, c4, nqb)] = k2
                                s.dma("gp", f"wpc{nqb % 2}", wpq[nqb % 2],
                                      wp_img[c4][:, nqb * 2048:(nqb + 1) * 2048],
                                      key=k2, after=b_wpq[nqb % 2].wr(k2),
                                      max_dma_last_dim=8192)
                    if cc == 15:
                        r.inc("pe", key=kpj[tc])
            for tc in range(4):
                sl = tc % 2
                ke = K("x2e")
                aft = [kpj[tc], wgq_keys[("x0", hf, c4, tc)]] + \
                    b_x2o[sl].wr(ke) + psb[tc].rd(ke) + b_x0[sl].rd(ke)
                s.dv.tensor_tensor(x2o[sl], PS[tc][:, :], x0s[sl],
                                   ALU.add).after(*aft).inc("dv", key=ke)
                kw = K("x2w")
                s.dma("sy", f"x2w{sl}",
                      bass.AP(d_x2.ap().tensor, (hf * 512 + tc * 128) * C + c4 * 512,
                              [[C, 128], [1, 512]]),
                      x2o[sl], key=kw, after=b_x2o[sl].rd(kw) + [ke])
                x2w_keys[(hf, c4, tc)] = kw
                if tc + 2 < 4:
                    kx = K("x0")
                    s.dma("sy", f"x0{sl}", x0s[sl],
                          bass.AP(x.tensor,
                                  (hf * 512 + (tc + 2) * 128) * C + c4 * 512,
                                  [[C, 128], [1, 512]]),
                          key=kx, after=b_x0[sl].wr(kx))
                    wgq_keys[("x0", hf, c4, tc + 2)] = kx
        ln2_w += ln_pass("L2", dx2_t, dln2_t, l2_pre2,
                         [pcm] if hf == 0 else [],
                         tiles=range(hf * 4, hf * 4 + 4),
                         bufs=(xs2, Tb2, b_xs2, b_Tb2))

    tap_ready["d_g"] = [x2w_keys[(1, 3, 3)]]
    tap_ready["d_x2"] = [x2w_keys[(hf_, c4_, tc_)]
                        for hf_ in range(2) for c4_ in range(4) for tc_ in range(4)]
    if not ph("post") or not ph("ffn"):
        return _finish([])

    l2_pre = l2_pre2
    tap_ready["d_ln2"] = [ln2_w[7]]

    b_ln2f = Buf()
    kl2 = []
    for fc in range(16):
        k = K("ln2ld"); kl2.append(k)
        aft = [ln2_w[7]]
        if fc == 0:
            aft += b_lnh.readers + b_sgh[0].readers + b_sgh[1].readers + \
                b_sgh[2].readers + b_sgh[3].readers + b_ln2f.wr(k)
        s.dma("ac", "ln2ld", ln2f(fc), d_ln2.ap()[:, fc * 128:(fc + 1) * 128],
              transpose=True, key=k, after=aft)
    b_ln2f.writer = kl2[-1]
    ln2_all = kl2

    ksd = K("sdl")
    s.dma("sy", "sdl", sdown_s, sdown_d, key=ksd,
          after=b_SP.readers + [b_SP.writer])
    b_sdown = Buf(); b_sdown.writer = ksd

    b_wga = [Buf(), Buf()]
    b_wua = [Buf(), Buf()]
    b_wdn = [Buf(), Buf()]
    b_fuse = [Buf() for _ in range(16)]
    b_gs = [Buf(), Buf()]
    b_acc = [Buf() for _ in range(8)]
    for oc in range(2):
        for t in range(4):
            b_wga[oc].readers += b_yp[t].readers + [b_yp[t].writer]
            b_wua[oc].readers += b_yp[t].readers + [b_yp[t].writer]
    for t in range(4, 8):
        b_wdn[0].readers += b_yp[t].readers + [b_yp[t].writer]
    b_wdn[1].readers += b_Qhat.readers + [b_Qhat.writer] + \
        b_wpq[0].readers + b_wpq[1].readers + \
        [b_wpq[0].writer, b_wpq[1].writer]
    for tt in range(2):
        b_gs[tt].readers += b_Tb.readers + [b_Tb.writer]
    fuse_prime = b_qch[0].readers + b_qch[1].readers + \
        [b_qch[0].writer, b_qch[1].writer] + b_pdec.readers + \
        [b_pdec.writer] + b_gfh.readers + [b_gfh.writer]
    for oc in range(16):
        b_fuse[oc].readers += fuse_prime
    acc_prime = b_gnw.readers + b_gnb.readers + b_yn.readers + \
        b_x0[0].readers + b_x0[1].readers + b_x2o[0].readers + \
        b_x2o[1].readers + b_xs[0].readers + b_xs[1].readers + \
        b_xs2[0].readers + b_xs2[1].readers + b_Tb2.readers + [b_Tb2.writer]
    for t8 in range(8):
        b_acc[t8].readers += acc_prime

    fgs_keys = {}
    ffu_keys = {}
    fdm_keys = {}
    fde_keys = {}
    for og in range(4):
        for oc in range(16):
            ocg = og * 16 + oc
            kwg = K("wga")
            s.dma("gp", f"wga{oc % 2}", wga[oc % 2], wgate_img[ocg], key=kwg,
                  after=b_wga[oc % 2].wr(kwg), max_dma_last_dim=2048)
            kwu = K("wua")
            s.dma("gp", f"wua{oc % 2}", wua[oc % 2], wup_img[ocg], key=kwu,
                  after=b_wua[oc % 2].wr(kwu), max_dma_last_dim=2048)
            kgm = {0: K("fgm"), 1: K("fgm")}
            kum = {0: K("fum"), 1: K("fum")}
            for cc in range(16):
                for tt in range(2):
                    aft = [kwg]
                    if ocg == 0 and tt == 0 and cc == 0:
                        aft += ln2_all
                    if cc == 0:
                        aft += psb[tt].wr(kgm[tt])
                    r = s.pe.matmul(PS[tt][:, :],
                                    wga[oc % 2][:, cc * 128:(cc + 1) * 128],
                                    ln2f(cc)[:, tt * 512:(tt + 1) * 512],
                                    start=(cc == 0), stop=(cc == 15)).after(*aft)
                    b_ln2f.readers.append(kgm[tt])
                    if cc == 15:
                        b_wga[oc % 2].rd(kgm[tt])
                        r.inc("pe", key=kgm[tt])
            for cc in range(16):
                for tt in range(2):
                    aft = [kwu]
                    if cc == 0:
                        aft += psb[2 + tt].wr(kum[tt])
                    r = s.pe.matmul(PS[2 + tt][:, :],
                                    wua[oc % 2][:, cc * 128:(cc + 1) * 128],
                                    ln2f(cc)[:, tt * 512:(tt + 1) * 512],
                                    start=(cc == 0), stop=(cc == 15)).after(*aft)
                    b_ln2f.readers.append(kum[tt])
                    if cc == 15:
                        b_wua[oc % 2].rd(kum[tt])
                        r.inc("pe", key=kum[tt])
            for tt in range(2):
                kgs = K("fgs")
                aft = [kgm[tt], k_sgc] + b_gs[tt].wr(kgs) + psb[tt].rd(kgs)
                s.ac.activation(gss[tt], PS[tt][:, :], AF.Silu,
                                scale=sgc_s[:, ocg:ocg + 1]).after(*aft).inc(
                    "ac", key=kgs)
                fgs_keys[(ocg, tt)] = kgs
                kfu = K("ffu")
                aft = [kum[tt], kgs, k_suc]
                if tt == 0:
                    aft += b_fuse[oc].wr(kfu)
                aft += psb[2 + tt].rd(kfu) + b_gs[tt].rd(kfu)
                s.dv.scalar_tensor_tensor(
                    fuse(oc)[:, tt * 512:(tt + 1) * 512], PS[2 + tt][:, :],
                    suc_s[:, ocg:ocg + 1], gss[tt], ALU.mult, ALU.mult).after(
                    *aft).inc("dv", key=kfu)
                ffu_keys[(ocg, tt)] = kfu
        if og < 3:
            dseq = [(ht, hf2, ht % 2, hf2 == 0)
                    for ht in range(4) for hf2 in range(2)]
        else:
            dseq = [(ht, hf2, (hf2 * 4 + ht) % 2, True)
                    for hf2 in range(2) for ht in range(4)]
        for (ht, hf2, dpar, fresh) in dseq:
            if fresh:
                kwd = K("wdn")
                s.dma("gp", f"wdn{dpar}", wdns[dpar], wdown_img[og * 4 + ht],
                      key=kwd, after=b_wdn[dpar].wr(kwd), max_dma_last_dim=2048)
            if True:
                kdm = {}
                for tc in range(4):
                    kdm[tc] = K("fdm")
                for oc in range(16):
                    for tc in range(4):
                        aft = [kwd, ffu_keys[(og * 16 + oc, hf2)]]
                        if oc == 0:
                            aft += psb[4 + tc].wr(kdm[tc])
                        r = s.pe.matmul(
                            PS[4 + tc][:, :],
                            fuse(oc)[:, hf2 * 512 + tc * 128:hf2 * 512 + (tc + 1) * 128],
                            wdns[dpar][:, oc * 512:(oc + 1) * 512],
                            start=(oc == 0), stop=(oc == 15)).after(*aft)
                        b_fuse[oc].readers.append(kdm[tc])
                        if oc == 15:
                            b_wdn[dpar].rd(kdm[tc])
                            r.inc("pe", key=kdm[tc])
                for tc in range(4):
                    t8 = hf2 * 4 + tc
                    accsl = acc(t8)[:, ht * 512:(ht + 1) * 512]
                    ke = K("fde")
                    aft = [kdm[tc]] + psb[4 + tc].rd(ke)
                    if og == 0:
                        aft += b_acc[t8].wr(ke)
                        r = s.dv.tensor_copy(accsl, PS[4 + tc][:, :]).after(*aft)
                    else:
                        b_acc[t8].readers.append(ke)
                        r = s.dv.tensor_tensor(accsl, accsl, PS[4 + tc][:, :],
                                               ALU.add).after(*aft)
                        b_acc[t8].writer = ke
                        b_acc[t8].readers = []
                    r.inc("dv", key=ke)
                    fde_keys[(og, ht, hf2, tc)] = ke
    b_x2f = [Buf(), Buf()]
    b_x2f[0].readers += b_PT.readers + [b_PT.writer] + \
        b_Qhat.readers + [b_Qhat.writer] + b_wtb.readers + [b_wtb.writer]
    b_x2f[1].readers += b_Tb.readers + [b_Tb.writer] + \
        b_gs[0].readers + b_gs[1].readers + \
        b_T1[0].readers + b_T1[1].readers + \
        b_tb1[0].readers + b_tb1[1].readers + \
        b_qp[0].readers + b_qp[1].readers
    ow_keys = []
    for t8 in range(8):
        par = t8 % 2
        kx = K("x2f")
        aft = l2_pre(t8) + b_x2f[par].wr(kx)
        s.dma("ac", f"x2f{par}", x2f[par], dx2_t[t8], key=kx, after=aft)
        km = K("finm")
        aft = [fde_keys[(3, ht, t8 // 4, t8 % 4)] for ht in range(4)] + \
            [ksd] + b_acc[t8].wr(km) + b_sdown.rd(km)
        s.dv.tensor_tensor(acc(t8), acc(t8), sdown_s, ALU.mult).after(
            *aft).inc("dv", key=km)
        kf = K("fin")
        s.dv.tensor_tensor(acc(t8), acc(t8), x2f[par], ALU.add).after(
            kx, km, *(b_acc[t8].wr(kf) + b_x2f[par].rd(kf))).inc("dv", key=kf)
        ko = K("ow")
        s.dma("sy" if par == 0 else "ac", f"ow{par}", out_t[t8], acc(t8),
              key=ko, after=b_acc[t8].rd(ko) + [kf])
        ow_keys.append(ko)

    return _finish(ow_keys)


def kernel(L0_attn_x, L0_attn_w_qkv, L0_attn_w_g, L0_attn_w_proj,
           L0_attn_gn_weight, L0_attn_gn_bias, L0_ffn_w_gate, L0_ffn_w_up,
           L0_ffn_w_down, L0_ffn_s_gate, L0_ffn_s_up, L0_ffn_s_down):
    x_full = np.ascontiguousarray(np.asarray(L0_attn_x, np.float32).reshape(T, C))
    sh = _host_shared(L0_attn_w_qkv, L0_attn_w_g, L0_attn_w_proj,
                      L0_ffn_w_gate, L0_ffn_w_up, L0_ffn_w_down,
                      L0_attn_gn_weight, L0_attn_gn_bias,
                      L0_ffn_s_gate, L0_ffn_s_up, L0_ffn_s_down)
    cos, sin = _host_xpos()

    nc = build_nc(taps=tuple(_DEBUG_TAPS))

    in_maps = []
    for core in range(NCORES):
        pc = _host_percore(core, x_full, cos, sin)
        m = dict(x=pc["x"], cs_img=pc["cs_img"], wtab=pc["wtab"],
                 wqkv_img=sh["wqkv_img"], wg_img=sh["wg_img"], wp_img=sh["wp_img"],
                 wgate_img=sh["wgate_img"], wup_img=sh["wup_img"],
                 wdown_img=sh["wdown_img"], maskT=sh["maskT"], qdecb=sh["qdecb"],
                 ghat=sh["ghat"], pdec=sh["pdec"], pmat=sh["pmat"],
                 gnw_b=sh["gnw_b"], gnb_b=sh["gnb_b"], sdown_b=sh["sdown_b"],
                 sg_c=sh["sg_c"], su_c=sh["su_c"])
        in_maps.append(m)

    res = run_bass_kernel_spmd(nc, in_maps, core_ids=list(range(NCORES)))
    outs = [res.results[c]["out"] for c in range(NCORES)]
    full = np.concatenate(outs, axis=0).reshape(1, T, C).astype(np.float32)
    if _DEBUG_TAPS:
        kernel._tap_results = [
            {k: v for k, v in res.results[c].items() if k.startswith("tap_")}
            for c in range(NCORES)]
    return full


# version 14
# speedup vs baseline: 1.1115x; 1.0217x over previous
import numpy as np
import concourse.bass as bass
import concourse.mybir as mybir
from concourse.bass_utils import run_bass_kernel_spmd

F32 = mybir.dt.float32
BF16 = mybir.dt.bfloat16
I8 = mybir.dt.int8
AF = mybir.ActivationFunctionType
ALU = mybir.AluOpType
AX = mybir.AxisListType

NCORES = 8
T = 8192
C = 2048
H = 16
D = 128
TPC = T // NCORES
NCH = TPC // 128
INTER = 8192
EPS = 1e-5
RD = float(D) ** -0.5

_DEBUG_TAPS = []

KBf = 256


ENGS = ("pe", "dv", "ac", "sy", "gp")


class _Op:
    __slots__ = ("method", "args", "kwargs", "waits", "incs")

    def __init__(self, method, args, kwargs):
        self.method = method
        self.args = args
        self.kwargs = kwargs
        self.waits = {}
        self.incs = []


class _OpRef:
    def __init__(self, s, op):
        self.s = s
        self.op = op

    def after(self, *keys):
        for k in keys:
            if k is None:
                continue
            if isinstance(k, list):
                self.after(*k)
                continue
            sem, v = self.s.tick[k]
            if self.op.waits.get(sem, -1) < v:
                self.op.waits[sem] = v
        return self

    def inc(self, sem, key=None, dma=False):
        step = 16 if dma else 1
        self.s.cnt[sem] = self.s.cnt.get(sem, 0) + step
        v = self.s.cnt[sem]
        self.op.incs.append((sem, step))
        if key is not None:
            keys = key if isinstance(key, list) else [key]
            for k in keys:
                assert k not in self.s.tick, f"dup key {k}"
                self.s.tick[k] = (sem, v)
        return self


class _EngProxy:
    def __init__(self, s, eng):
        self._s = s
        self._e = eng

    def __getattr__(self, m):
        def call(*a, **kw):
            op = _Op(m, a, kw)
            self._s.q[self._e].append(op)
            return _OpRef(self._s, op)
        return call


class Sched:
    def __init__(self):
        self.q = {e: [] for e in ENGS}
        self.cnt = {}
        self.tick = {}
        self.nkey = 0
        for e in ENGS:
            setattr(self, e, _EngProxy(self, e))

    def key(self, tag=""):
        self.nkey += 1
        return (tag, self.nkey)

    def dma(self, eng, slot, out, in_, key=None, after=(), **kw):
        r = getattr(self, eng).dma_start(out=out, in_=in_, **kw)
        r.after(*after)
        r.inc("D" + slot, key=key, dma=True)
        return r


class Buf:

    def __init__(self):
        self.writer = None
        self.readers = []

    def wr(self, key):
        aft = list(self.readers) if self.readers else (
            [self.writer] if self.writer is not None else [])
        self.writer = key
        self.readers = []
        return aft

    def rd(self, key):
        self.readers.append(key)
        return [self.writer] if self.writer is not None else []


def _gammas():
    return np.array([1.0 - 2.0 ** (-5.0 - h) for h in range(H)], dtype=np.float64)


def _host_shared(wqkv, wg, wp, wgate, wup, wdown, gnw, gnb, sg, su, sd):
    g = _gammas()
    sh = {}
    sh["wqkv_img"] = np.ascontiguousarray(
        np.asarray(wqkv, np.float32).reshape(16, 128, 12, 512)
        .transpose(2, 1, 0, 3).reshape(12, 128, 8192))
    sh["wg_img"] = np.ascontiguousarray(
        np.asarray(wg, np.float32).reshape(16, 128, 4, 512)
        .transpose(2, 1, 0, 3).reshape(4, 128, 8192))
    sh["wp_img"] = np.ascontiguousarray(
        np.asarray(wp, np.float32).reshape(16, 128, 4, 512)
        .transpose(2, 1, 0, 3).reshape(4, 128, 8192))
    sh["wgate_img"] = np.ascontiguousarray(
        np.asarray(wgate).reshape(16, 128, 64, 128)
        .transpose(2, 1, 0, 3).reshape(64, 128, 2048))
    sh["wup_img"] = np.ascontiguousarray(
        np.asarray(wup).reshape(16, 128, 64, 128)
        .transpose(2, 1, 0, 3).reshape(64, 128, 2048))
    sh["wdown_img"] = np.ascontiguousarray(
        np.asarray(wdown).reshape(4, 16, 128, 4, 512)
        .transpose(0, 3, 2, 1, 4).reshape(16, 128, 8192))
    j = np.arange(128, dtype=np.float64)
    maskT = np.zeros((128, H * 128), dtype=np.float64)
    for h in range(H):
        e = j[None, :] - j[:, None]
        maskT[:, h * 128:(h + 1) * 128] = np.where(e >= 0, g[h] ** np.maximum(e, 0), 0.0)
    sh["maskT"] = maskT.astype(np.float32)
    qd = np.concatenate([g[h] ** (j + 1.0) for h in range(H)])
    sh["qdecb"] = np.broadcast_to(qd.astype(np.float32)[None, :], (128, H * 128)).copy()
    gh = np.zeros((128, H * 128), dtype=np.float64)
    for h in range(H):
        gh[:, h * 128:(h + 1) * 128] = (g[h] ** (127.0 - j))[:, None]
    sh["ghat"] = gh.astype(np.float32)
    pdec = np.zeros((NCH, 128, H * 128), dtype=np.float32)
    for c8 in range(NCH):
        row = np.concatenate([np.full(128, g[h] ** (128.0 * c8)) for h in range(H)])
        pdec[c8] = row.astype(np.float32)[None, :]
    sh["pdec"] = pdec
    P = np.zeros((128, 128), dtype=np.float32)
    ii = np.arange(0, 128, 2)
    P[ii + 1, ii] = -1.0
    P[ii, ii + 1] = 1.0
    sh["pmat"] = P
    sh["gnw_b"] = np.broadcast_to(np.asarray(gnw, np.float32).reshape(1, C), (128, C)).copy()
    sh["gnb_b"] = np.broadcast_to(np.asarray(gnb, np.float32).reshape(1, C), (128, C)).copy()
    sh["sdown_b"] = np.broadcast_to(np.asarray(sd, np.float32).reshape(1, C), (128, C)).copy()
    sh["sg_c"] = np.ascontiguousarray(np.asarray(sg, np.float32).reshape(64, 128).T)
    sh["su_c"] = np.ascontiguousarray(np.asarray(su, np.float32).reshape(64, 128).T)
    return sh


def _host_xpos():
    i = np.arange(C // 2, dtype=np.float64)
    zeta = (2.0 * i + 0.4 * C) / (1.4 * C)
    invf = 1.0 / (10000.0 ** (2.0 * i / C))
    t = np.arange(T, dtype=np.float64)
    scale = zeta[None, :] ** (t[:, None] / 512.0)
    ang = t[:, None] * invf[None, :]
    cos = np.repeat(np.cos(ang) * scale, 2, axis=1).astype(np.float32)
    sin = np.repeat(np.sin(ang) * scale, 2, axis=1).astype(np.float32)
    return cos, sin


def _host_percore(core, x_full, cos, sin):
    g = _gammas()
    pc = {}
    pc["x"] = np.ascontiguousarray(x_full[core * TPC:(core + 1) * TPC])
    cs = np.empty((16, 128, 2048), dtype=np.float32)
    csl = cos[core * TPC:(core + 1) * TPC]
    ssl = sin[core * TPC:(core + 1) * TPC]
    for fc in range(16):
        cs[fc, :, 0:1024] = csl[:, fc * 128:(fc + 1) * 128].T
        cs[fc, :, 1024:2048] = ssl[:, fc * 128:(fc + 1) * 128].T
    pc["cs_img"] = cs
    wt = np.zeros((128, NCORES * H), dtype=np.float32)
    for b in range(core):
        for h in range(H):
            wt[:, b * H + h] = g[h] ** (1024.0 * (core - 1 - b))
    pc["wtab"] = wt
    return pc


def build_nc(taps=(), stop_after="ffn"):
    PH = ["a1", "a2p0", "a2p1", "a2p2", "a2p3", "a2p", "a2c0", "a2", "p5", "post", "ffn"]
    lim = PH.index(stop_after)

    def ph(name):
        return PH.index(name) <= lim

    nc = bass.Bass("TRN2", target_bir_lowering=False)
    g64 = _gammas()
    g128 = [float(np.float32(g64[h] ** 128.0)) for h in range(H)]

    x = nc.dram_tensor("x", [TPC, C], F32, kind="ExternalInput").ap()
    out = nc.dram_tensor("out", [TPC, C], F32, kind="ExternalOutput").ap()
    wqkv_img = nc.dram_tensor("wqkv_img", [12, 128, 8192], F32, kind="ExternalInput").ap()
    wg_img = nc.dram_tensor("wg_img", [4, 128, 8192], F32, kind="ExternalInput").ap()
    wp_img = nc.dram_tensor("wp_img", [4, 128, 8192], F32, kind="ExternalInput").ap()
    wgate_img = nc.dram_tensor("wgate_img", [64, 128, 2048], I8, kind="ExternalInput").ap()
    wup_img = nc.dram_tensor("wup_img", [64, 128, 2048], I8, kind="ExternalInput").ap()
    wdown_img = nc.dram_tensor("wdown_img", [16, 128, 8192], I8, kind="ExternalInput").ap()
    maskT_d = nc.dram_tensor("maskT", [128, 2048], F32, kind="ExternalInput").ap()
    qdecb_d = nc.dram_tensor("qdecb", [128, 2048], F32, kind="ExternalInput").ap()
    ghat_d = nc.dram_tensor("ghat", [128, 2048], F32, kind="ExternalInput").ap()
    pdec_d = nc.dram_tensor("pdec", [NCH, 128, 2048], F32, kind="ExternalInput").ap()
    pmat_d = nc.dram_tensor("pmat", [128, 128], F32, kind="ExternalInput").ap()
    gnw_d = nc.dram_tensor("gnw_b", [128, 2048], F32, kind="ExternalInput").ap()
    gnb_d = nc.dram_tensor("gnb_b", [128, 2048], F32, kind="ExternalInput").ap()
    sdown_d = nc.dram_tensor("sdown_b", [128, 2048], F32, kind="ExternalInput").ap()
    sg_d = nc.dram_tensor("sg_c", [128, 64], F32, kind="ExternalInput").ap()
    su_d = nc.dram_tensor("su_c", [128, 64], F32, kind="ExternalInput").ap()
    cs_d = nc.dram_tensor("cs_img", [16, 128, 2048], F32, kind="ExternalInput").ap()
    wtab_d = nc.dram_tensor("wtab", [128, NCORES * 16], F32, kind="ExternalInput").ap()

    d_ln = nc.dram_tensor("d_ln", [TPC, C], BF16)
    d_q = nc.dram_tensor("d_q", [C, TPC], BF16)
    d_k = nc.dram_tensor("d_k", [C, TPC], BF16)
    d_v = nc.dram_tensor("d_v", [C, TPC], BF16)
    d_g = nc.dram_tensor("d_g", [TPC, C], BF16)
    d_x2 = nc.dram_tensor("d_x2", [TPC, C], F32)
    d_ln2 = nc.dram_tensor("d_ln2", [TPC, C], BF16)
    cc_in = nc.dram_tensor("cc_in", [H * 128, 128], F32)
    cc_out = nc.dram_tensor("cc_out", [NCORES * H * 128, 128], F32)

    tap_outs = {}
    for tname in taps:
        src = {"d_ln": d_ln, "d_q": d_q, "d_k": d_k, "d_v": d_v, "d_g": d_g,
               "d_x2": d_x2, "d_ln2": d_ln2, "cc_in": cc_in,
               "cc_out": cc_out}.get(tname)
        if src is not None:
            tap_outs[tname] = (
                nc.dram_tensor("tap_" + tname, list(src.shape), src.dtype,
                               kind="ExternalOutput").ap(), src.ap())

    ARENA_KB = 204
    ar = nc.alloc_sbuf_tensor("arena", [128, ARENA_KB * KBf], F32)
    arb = ar.bitcast(BF16)

    def f32(kb, n):
        o = int(kb * KBf)
        return ar[:, o:o + n]

    def bf(kb, n):
        o = int(kb * KBf) * 2
        return arb[:, o:o + n]

    lnf = lambda cc: bf(0, 16 * 1024)[:, cc * 1024:(cc + 1) * 1024]
    ghat_s = bf(32, 2048)
    ZmS = [bf(48, 512), bf(49, 512)]
    ln2f = lambda cc: bf(0, 16 * 1024)[:, cc * 1024:(cc + 1) * 1024]
    gnw_s = f32(0, 2048)
    gnb_s = f32(8, 2048)
    yn_s = f32(16, 2048)
    x0s = [f32(24, 512), f32(26, 512)]
    x2o = [f32(28, 512), f32(30, 512)]
    qch = [bf(32, 2048), bf(36, 2048)]
    kch = [bf(40, 2048), bf(44, 2048)]
    ktch = [bf(48, 2048), bf(52, 2048)]
    vch = [bf(56, 2048), bf(60, 2048)]
    pdec_s = f32(40, 2048)
    gfh = lambda fc: bf(48, 16 * 512)[:, fc * 512:(fc + 1) * 512]
    fuse = lambda oc: bf(32, 16 * 1024)[:, oc * 1024:(oc + 1) * 1024]
    xs = [f32(64, 2048), f32(72, 2048)]
    cs_s = [f32(80, 2048), f32(88, 2048)]
    gtmp = [f32(64, 2048), f32(72, 2048), f32(80, 2048), f32(88, 2048)]
    sgh8 = lambda t8: bf(96, 16 * 1024)[:, t8 * 2048:(t8 + 1) * 2048]
    xs2 = [f32(64, 2048), f32(72, 2048)]
    Tb2 = bf(80, 2048)
    acc = lambda t8: f32(64, 8 * 2048)[:, t8 * 2048:(t8 + 1) * 2048]
    wqb = [bf(128, 8192), bf(144, 8192)]
    ypart = lambda c8: bf(128, 8 * 2048)[:, c8 * 2048:(c8 + 1) * 2048]
    wga = [bf(128, 2048), bf(132, 2048)]
    wua = [bf(136, 2048), bf(140, 2048)]
    wdns = [bf(144, 8192), bf(160, 8192)]
    x2f = [f32(184, 2048), f32(192, 2048)]
    maskT_s = bf(160, 2048)
    qdecb_s = bf(164, 2048)
    wpq = [bf(168, 2048), bf(172, 2048)]
    SP = f32(176, 2048)
    sdown_s = f32(176, 2048)
    PT = bf(184, 2048)
    Qhat = bf(188, 2048)
    Tb = bf(192, 2048)
    TbA1 = [bf(192, 512), bf(193, 512)]
    Qp = [bf(194, 512), bf(195, 512)]
    gss = [bf(192, 512), bf(193, 512)]
    T1 = f32(196, 1024)
    st_s1 = f32(200, 64)
    st_s2 = f32(200.25, 64)
    st_mu = f32(200.5, 64)
    st_msq = f32(200.75, 64)
    st_var = f32(201.0, 64)
    st_sq = f32(201.25, 64)
    st_rstd = f32(201.5, 64)
    sgc_s = f32(201.75, 64)
    suc_s = f32(202.0, 64)
    pmat_s = bf(202.25, 128)
    wtab_sb = f32(202.5, 128)

    PS = [nc.alloc_psum_tensor(f"ps{i}", [128, 512], F32) for i in range(8)]

    s = Sched()
    K = s.key

    psb = [Buf() for _ in range(8)]

    x_t = x.rearrange("(n p) c -> n p c", p=128)
    out_t = out.rearrange("(n p) c -> n p c", p=128)
    dln_t = d_ln.ap().rearrange("(n p) c -> n p c", p=128)
    dg_t = d_g.ap().rearrange("(n p) c -> n p c", p=128)
    dx2_t = d_x2.ap().rearrange("(n p) c -> n p c", p=128)
    dln2_t = d_ln2.ap().rearrange("(n p) c -> n p c", p=128)
    dq_rows = d_q.ap().rearrange("(n p) t -> n p t", p=128)
    dk_rows = d_k.ap().rearrange("(n p) t -> n p t", p=128)
    dv_rows = d_v.ap().rearrange("(n p) t -> n p t", p=128)

    def dq_col(dram, c8):
        return dram.ap().rearrange("(h d) t -> d h t", d=128)[:, :, c8 * 128:(c8 + 1) * 128]

    k_maskT = K("maskT"); s.dma("gp", "tb", maskT_s, maskT_d, key=k_maskT, max_dma_last_dim=8192)
    k_qdecb = K("qdecb"); s.dma("gp", "tb2", qdecb_s, qdecb_d, key=k_qdecb, max_dma_last_dim=8192)
    k_pmat = K("pmat"); s.dma("gp", "tb3", pmat_s, pmat_d, key=k_pmat, max_dma_last_dim=8192)
    k_sgc = K("sgc"); s.dma("sy", "tb4", sgc_s, sg_d, key=k_sgc)
    k_suc = K("suc"); s.dma("sy", "tb5", suc_s, su_d, key=k_suc)
    k_wtab = K("wtab"); s.dma("sy", "tb6", wtab_sb, wtab_d, key=k_wtab)

    wq_keys = {}
    for half in range(2):
        k = K("wq"); wq_keys[(0, half)] = k
        s.dma("gp", f"wq{0}{half}", wqb[0][:, half * 4096:(half + 1) * 4096],
              wqkv_img[0][:, half * 4096:(half + 1) * 4096], key=k,
              max_dma_last_dim=8192)
    b_wqb = [Buf(), Buf()]
    b_wqb[0].writer = wq_keys[(0, 1)]

    cs_keys = {}
    b_cs = [Buf(), Buf()]
    for fcg in range(2):
        k = K("cs"); cs_keys[fcg] = k
        s.dma("sy", f"cs{fcg}", cs_s[fcg], cs_d[fcg], key=k)
        b_cs[fcg].writer = k

    b_xs = [Buf(), Buf()]
    b_Tb = Buf()

    def ln_pass(tag, src_t, dst_t, src_pre, first_extra, tiles=range(8),
                bufs=None, ldq=("sy", "sy")):
        if bufs is None:
            xs_l, Tb_l, b_xs_l, b_Tb_l = xs, Tb, b_xs, b_Tb
        else:
            xs_l, Tb_l, b_xs_l, b_Tb_l = bufs
        wkeys = []
        for tc in tiles:
            sl = tc % 2
            kx = K(tag + "x")
            aft = b_xs_l[sl].wr(kx) + (src_pre(tc) or []) + \
                (first_extra if tc < min(tiles) + 2 else [])
            s.dma(ldq[sl], f"{tag}x{sl}", xs_l[sl], src_t[tc], key=kx, after=aft)
            k1 = K(); s.dv.reduce_sum(st_s1[:, tc:tc + 1], xs_l[sl], AX.X).after(
                *b_xs_l[sl].rd(k1)).inc("dv", key=k1)
            k2 = K(); r = s.dv.scalar_tensor_tensor(
                Tb_l[:, :], xs_l[sl], 1.0, xs_l[sl], ALU.mult, ALU.mult,
                accum_out=st_s2[:, tc:tc + 1])
            r.after(*(b_xs_l[sl].rd(k2) + b_Tb_l.wr(k2))).inc("dv", key=k2)
            kmu = K(); s.dv.tensor_scalar(st_mu[:, tc:tc + 1], st_s1[:, tc:tc + 1],
                                          1.0 / C, None, ALU.mult).after(k1).inc(
                "dv", key=kmu)
            kms = K(); s.dv.scalar_tensor_tensor(
                st_msq[:, tc:tc + 1], st_mu[:, tc:tc + 1], 1.0,
                st_mu[:, tc:tc + 1], ALU.mult, ALU.mult).after(kmu).inc("dv", key=kms)
            kv = K(); s.dv.scalar_tensor_tensor(
                st_var[:, tc:tc + 1], st_s2[:, tc:tc + 1], 1.0 / C,
                st_msq[:, tc:tc + 1], ALU.mult, ALU.subtract).after(
                k2, kms).inc("dv", key=kv)
            kv2 = K(); s.dv.tensor_scalar(st_var[:, tc:tc + 1], st_var[:, tc:tc + 1],
                                          EPS, None, ALU.add).after(kv).inc("dv", key=kv2)
            kq = K(); s.ac.activation(st_sq[:, tc:tc + 1], st_var[:, tc:tc + 1],
                                      AF.Sqrt).after(kv2).inc("ac", key=kq)
            krc = K(); s.dv.reciprocal(st_rstd[:, tc:tc + 1],
                                       st_sq[:, tc:tc + 1]).after(kq).inc("dv", key=krc)
            kn = K(tag + "N")
            r = s.dv.tensor_scalar(Tb_l[:, :], xs_l[sl], st_mu[:, tc:tc + 1],
                                   st_rstd[:, tc:tc + 1], ALU.subtract, ALU.mult)
            r.after(*(b_xs_l[sl].rd(kn) + b_Tb_l.wr(kn) + [krc, kmu])).inc("dv", key=kn)
            kw = K(tag + "w")
            s.dma("sy", tag + "w", dst_t[tc], Tb_l[:, :], key=kw,
                  after=b_Tb_l.rd(kw) + [kn])
            wkeys.append(kw)
        return wkeys

    ln1_w = ln_pass("L1", x_t, dln_t, lambda tc: [], [], ldq=("sy", "ac"))

    b_lnf = Buf()
    klnf = []
    for th in range(2):
        for fc in range(16):
            k = K("lnfld"); klnf.append(k)
            s.dma("ac", "lnfld", lnf(fc)[:, th * 512:(th + 1) * 512],
                  d_ln.ap()[th * 512:(th + 1) * 512, fc * 128:(fc + 1) * 128],
                  transpose=True, key=k, after=[ln1_w[th * 4 + 3]])
    b_lnf.writer = klnf[-1]
    lnf_all = klnf
    lnf_half = [klnf[0:16], klnf[16:32]]

    b_tb1 = [Buf(), Buf()]
    b_qp = [Buf(), Buf()]
    b_T1 = [Buf(), Buf()]
    for _b in b_tb1 + b_qp:
        _b.readers += b_Tb.readers + [b_Tb.writer]
    last_dqw = {0: None, 1: None}
    last_dvw = {0: None, 1: None}

    for f4 in range(12):
        wpar = f4 % 2
        if f4 + 1 < 12:
            npar = (f4 + 1) % 2
            for half in range(2):
                k = K("wq"); wq_keys[(f4 + 1, half)] = k
                aft = b_wqb[npar].wr(k) if half == 0 else []
                s.dma("gp", f"wq{npar}{half}",
                      wqb[npar][:, half * 4096:(half + 1) * 4096],
                      wqkv_img[f4 + 1][:, half * 4096:(half + 1) * 4096],
                      key=k, after=aft, max_dma_last_dim=8192)
        for fc in range(4):
            fcg = f4 * 4 + fc
            h = fcg % 16
            kind = "q" if fcg < 16 else ("k" if fcg < 32 else "v")
            kmm = {0: K("a1mm"), 1: K("a1mm")}
            kraw = {}
            pbank = {0: (fc % 2) * 2, 1: (fc % 2) * 2 + 1}
            for tt in range(2):
                for cc in range(16):
                    half = 0 if cc < 8 else 1
                    aft = [wq_keys[(f4, half)]]
                    if f4 == 0 and fc == 0 and cc == 0:
                        aft += lnf_half[tt]
                    if cc == 0:
                        aft += psb[pbank[tt]].wr(kmm[tt])
                    r = s.pe.matmul(PS[pbank[tt]][:, :],
                                    wqb[wpar][:, cc * 512 + fc * 128:cc * 512 + (fc + 1) * 128],
                                    lnf(cc)[:, tt * 512:(tt + 1) * 512],
                                    start=(cc == 0), stop=(cc == 15)).after(*aft)
                    b_lnf.readers.append(kmm[tt])
                    if cc == 15:
                        b_wqb[wpar].rd(kmm[tt])
                        r.inc("pe", key=kmm[tt])
                        last_mm = kmm[tt]
            for tt in range(2):
                kraw[tt] = K("a1raw")
                aft = [kmm[tt]] + b_tb1[tt].wr(kraw[tt]) + psb[pbank[tt]].rd(kraw[tt])
                s.ac.activation(TbA1[tt], PS[pbank[tt]][:, :], AF.Copy,
                                scale=(RD if kind == "q" else 1.0)).after(*aft).inc(
                    "ac", key=kraw[tt])
            for tt in range(2):
                if kind == "v":
                    kd = K("dvw")
                    s.dma("sy", f"dvw{tt}", dv_rows[h][:, tt * 512:(tt + 1) * 512],
                          TbA1[tt], key=kd, after=b_tb1[tt].rd(kd) + [kraw[tt]])
                    last_dvw[tt] = kd
                else:
                    krot = K("a1rot")
                    aft = [kraw[tt], k_pmat] + psb[4 + tt].wr(krot) + b_tb1[tt].rd(krot)
                    s.pe.matmul(PS[4 + tt][:, :], pmat_s, TbA1[tt],
                                start=True, stop=True).after(*aft).inc("pe", key=krot)
                    cpar = fcg % 2
                    kt1 = K("a1t1")
                    aft = [kraw[tt], cs_keys[fcg]] + b_cs[cpar].rd(kt1) + \
                        b_tb1[tt].rd(kt1) + b_T1[0].wr(kt1)
                    s.dv.tensor_tensor(T1[:, 0:512], TbA1[tt],
                                       cs_s[cpar][:, tt * 512:(tt + 1) * 512],
                                       ALU.mult).after(*aft).inc("dv", key=kt1)
                    kt2 = K("a1t2")
                    aft = [krot, cs_keys[fcg]] + b_cs[cpar].rd(kt2) + \
                        psb[4 + tt].rd(kt2) + b_T1[1].wr(kt2)
                    s.dv.tensor_tensor(T1[:, 512:1024], PS[4 + tt][:, :],
                                       cs_s[cpar][:, 1024 + tt * 512:1024 + (tt + 1) * 512],
                                       ALU.mult).after(*aft).inc("dv", key=kt2)
                    kqp = K("a1qp")
                    s.dv.tensor_tensor(Qp[tt], T1[:, 0:512], T1[:, 512:1024],
                                       ALU.add).after(kt1, kt2, *(b_qp[tt].wr(kqp) +
                                       b_T1[0].rd(kqp) + b_T1[1].rd(kqp))).inc("dv", key=kqp)
                    dst = dq_rows[h] if kind == "q" else dk_rows[h]
                    kd = K("dqw")
                    s.dma("sy", f"dqw{tt}", dst[:, tt * 512:(tt + 1) * 512],
                          Qp[tt], key=kd, after=b_qp[tt].rd(kd) + [kqp])
                    last_dqw[tt] = kd
            nxt = fcg + 2
            if 2 <= nxt < 32:
                k = K("cs"); cs_keys[nxt] = k
                s.dma("sy", f"cs{nxt % 2}", cs_s[nxt % 2], cs_d[nxt % 16], key=k,
                      after=b_cs[nxt % 2].wr(k))

    tap_ready = {"d_ln": [ln1_w[7]],
                 "d_q": [last_dqw[0], last_dqw[1]],
                 "d_k": [last_dqw[0], last_dqw[1]],
                 "d_v": [last_dvw[0], last_dvw[1]]}
    def _finish(ow_keys):
        fin_keys = list(ow_keys)
        for tname, (tdst, tsrc) in tap_outs.items():
            rdy = tap_ready.get(tname)
            if rdy is None:
                continue
            kt = K("tap")
            s.dma("sy", "tap" + tname, tdst, tsrc, key=kt,
                  after=rdy + list(ow_keys))
            fin_keys.append(kt)
        r = s.sy.nop().after(*fin_keys)
        for name, v in s.cnt.items():
            if name.startswith("D"):
                if r.op.waits.get(name, -1) < v:
                    r.op.waits[name] = v
        build_nc._last_sched = s
        sems = {}
        for name in s.cnt:
            sems[name] = nc.alloc_semaphore("s_" + name.replace(":", "_"))
        with nc.Block("main") as blk:
            def mk(engname):
                def f(e):
                    seen = {}
                    for op in s.q[engname]:
                        for (sem, v) in sorted(op.waits.items()):
                            if seen.get(sem, -1) < v:
                                e.wait_ge(sems[sem], v)
                                seen[sem] = v
                        inst = getattr(e, op.method)(*op.args, **op.kwargs)
                        for (sem, st) in op.incs:
                            inst.then_inc(sems[sem], st)
                return f
            blk.tensor(mk("pe"))
            blk.vector(mk("dv"))
            blk.scalar(mk("ac"))
            blk.sync(mk("sy"))
            blk.gpsimd(mk("gp"))
        return nc

    if not ph("a2p"):
        return _finish([])

    kbar = last_mm
    k_ghat = K("ghat")
    s.dma("gp", "ghld", ghat_s, ghat_d, key=k_ghat, max_dma_last_dim=8192)
    ghat_readers = []

    b_wpq = [Buf(), Buf()]
    wpqg = [wpq[0], wpq[1], bf(64, 2048), bf(68, 2048)]
    b_wpqg = [b_wpq[0], b_wpq[1], Buf(), Buf()]
    b_sgh8 = [Buf() for _ in range(8)]
    wgq_keys = {}
    gse8 = {}

    def gmm_round(c4):
        for qb in range(4):
            k = K("wg"); wgq_keys[(c4, qb)] = k
            s.dma("gp", f"wpq{qb}", wpqg[qb],
                  wg_img[c4][:, qb * 2048:(qb + 1) * 2048],
                  key=k, after=b_wpqg[qb].wr(k), max_dma_last_dim=8192)
        for half8 in range(2):
            kg = {}
            for cc in range(16):
                qb = cc // 4
                for tq in range(4):
                    t8 = half8 * 4 + tq
                    if cc == 0:
                        kg[tq] = K("gmm")
                    aft = [wgq_keys[(c4, qb)]]
                    if cc == 0:
                        aft += psb[tq].wr(kg[tq])
                    r = s.pe.matmul(PS[tq][:, :],
                                    lnf(cc)[:, t8 * 128:(t8 + 1) * 128],
                                    wpqg[qb][:, (cc % 4) * 512:(cc % 4 + 1) * 512],
                                    start=(cc == 0), stop=(cc == 15)).after(*aft)
                    b_lnf.readers.append(kg[tq])
                    if cc % 4 == 3 and tq == 3 and cc != 15:
                        kqr = K("gqr")
                        r.inc("pe", key=kqr)
                        b_wpqg[qb].rd(kqr)
                    if cc == 15:
                        if tq == 3:
                            b_wpqg[qb].rd(kg[tq])
                        r.inc("pe", key=kg[tq])
            for tq in range(4):
                t8 = half8 * 4 + tq
                ks = K("gse")
                aft = [kg[tq]] + b_sgh8[t8].wr(ks) + psb[tq].rd(ks)
                s.ac.activation(sgh8(t8)[:, c4 * 512:(c4 + 1) * 512],
                                PS[tq][:, :], AF.Silu).after(*aft).inc(
                    "ac", key=ks)
                gse8[(c4, t8)] = ks

    b_SP = Buf()
    ksr = K("srun0")
    s.dv.memset(SP[:, :], 0.0).inc("dv", key=ksr)
    b_SP.writer = ksr

    b_qch = [Buf(), Buf()]
    b_kch = [Buf(), Buf()]
    b_ktch = [Buf(), Buf()]
    b_vch = [Buf(), Buf()]
    b_Qhat = Buf()
    b_Zm = [Buf(), Buf()]
    b_yp = [Buf() for _ in range(NCH)]

    chunk_keys = {}

    dqk_done = [last_dqw[0], last_dqw[1]]
    dv_done = [last_dvw[0], last_dvw[1]]

    def ld_chunk(c8, parts="qktv"):
        par = c8 % 2
        kq = K("qch"); kk = K("kch"); kt = K("ktch"); kv = K("vch")
        chunk_keys[c8] = dict(q=kq, k=kk, kt=kt, v=kv)
        if "q" in parts:
            s.dma("sy", f"qch{par}", qch[par].rearrange("p (h t) -> p h t", h=16),
                  dq_col(d_q, c8), key=kq,
                  after=b_qch[par].wr(kq) + dqk_done)
        if "k" in parts:
            s.dma("sy", f"kch{par}", kch[par].rearrange("p (h t) -> p h t", h=16),
                  dq_col(d_k, c8), key=kk,
                  after=b_kch[par].wr(kk) + dqk_done)
        if "t" in parts:
            s.dma("ac", f"ktch{par}", ktch[par],
                  d_k.ap()[:, c8 * 128:(c8 + 1) * 128], transpose=True,
                  key=kt, after=b_ktch[par].wr(kt) + dqk_done)
        if "v" in parts:
            s.dma("ac", f"vch{par}", vch[par],
                  d_v.ap()[:, c8 * 128:(c8 + 1) * 128], transpose=True,
                  key=kv, after=b_vch[par].wr(kv) + dv_done)

    if stop_after == "a2p0":
        return _finish([])
    ld_chunk(0, parts="tv")
    if stop_after == "a2p1":
        return _finish([])
    if stop_after == "a2p2":
        ld_chunk(1, parts="qk")
        return _finish([])
    if stop_after == "a2p3":
        ld_chunk(1, parts="tv")
        return _finish([])
    ld_chunk(1, parts="tv")
    if stop_after == "a2p":
        return _finish([])

    sbf_of = {}
    ye_of = {}
    for c8 in range(NCH):
        par = c8 % 2
        ck = chunk_keys[c8]
        kkh = K("a2kh")
        aft = [ck["kt"], k_ghat] + b_ktch[par].rd(kkh)
        b_ktch[par].writer = kkh
        b_ktch[par].readers = []
        ghat_readers.append(kkh)
        s.gp.tensor_tensor(ktch[par], ktch[par], ghat_s, ALU.mult).after(
            *aft).inc("gp", key=kkh)
        if c8 >= 1:
            for grp in range(4):
                gs4 = slice(grp * 512, (grp + 1) * 512)
                ksb = K("a2sb")
                aft = b_SP.rd(ksb)
                if grp == 0:
                    aft += b_yp[c8].wr(ksb)
                s.ac.activation(ypart(c8)[:, gs4], SP[:, gs4], AF.Copy).after(
                    *aft).inc("ac", key=ksb)
                b_yp[c8].writer = ksb
                sbf_of[(c8, grp)] = ksb
        for grp in range(4):
            pz = (c8 * 4 + grp) % 2
            ksc = K("a2sc")
            for q in range(4):
                h = grp * 4 + q
                hs = slice(h * 128, (h + 1) * 128)
                aft = [kkh, ck["v"]] + b_ktch[par].rd(ksc) + b_vch[par].rd(ksc)
                if q == 0:
                    aft += psb[4 + pz].wr(ksc)
                r = s.pe.matmul(PS[4 + pz][:, q * 128:(q + 1) * 128],
                                ktch[par][:, hs], vch[par][:, hs],
                                start=True, stop=True).after(*aft)
                if q == 3:
                    r.inc("pe", key=ksc)
            for q in range(4):
                h = grp * 4 + q
                hs = slice(h * 128, (h + 1) * 128)
                ksu = K("a2su")
                aft = [ksc] + psb[4 + pz].rd(ksu)
                if c8 == 0 and grp == 0 and q == 0:
                    aft.append(ksr)
                if c8 >= 1:
                    aft.append(sbf_of[(c8, grp)])
                b_SP.readers.append(ksu)
                s.dv.scalar_tensor_tensor(SP[:, hs], SP[:, hs], g128[h],
                                          PS[4 + pz][:, q * 128:(q + 1) * 128],
                                          ALU.mult, ALU.add).after(*aft).inc(
                    "dv", key=ksu)
                if h == 15:
                    b_SP.writer = ksu
                    b_SP.readers = []
        if c8 + 2 < NCH:
            ld_chunk(c8 + 2, parts="tv")
        if c8 < 4:
            gmm_round(c8)
        if stop_after == "a2c0":
            break

    kccw = K("ccw")
    s.dma("sy", "ccw", cc_in.ap().rearrange("(h d) e -> d h e", d=128),
          SP[:, :].rearrange("p (h e) -> p h e", h=16), key=kccw,
          after=b_SP.rd(kccw))
    tap_ready["cc_in"] = [kccw]
    kcc = K("cc")
    s.gp.collective_compute(
        "AllGather", ALU.bypass, replica_groups=[list(range(NCORES))],
        ins=[cc_in.ap().opt()], outs=[cc_out.ap().opt()]).after(kccw).inc(
        "cc", key=kcc)

    b_gt = [Buf() for _ in range(4)]
    b_gt[0].readers += b_xs[0].readers + \
        b_wpqg[2].readers + b_wpqg[3].readers + \
        [b_wpqg[2].writer, b_wpqg[3].writer]
    b_gt[1].readers += b_xs[1].readers
    b_gt[2].readers += b_cs[0].readers
    b_gt[3].readers += b_cs[1].readers
    gld_keys = {}
    pcm = None
    cc_out_r = cc_out.ap().rearrange("(b h d) e -> b d h e", d=128, h=16)
    for c8 in range(NCH):
        par = c8 % 2
        kq = K("qch"); kk = K("kch"); kv = K("vch2")
        aftq = b_qch[par].wr(kq) + dqk_done
        if c8 == 0:
            aftq += ghat_readers
        s.dma("sy", f"qch{par}", qch[par].rearrange("p (h t) -> p h t", h=16),
              dq_col(d_q, c8), key=kq, after=aftq)
        s.dma("ac", f"kch{par}", kch[par].rearrange("p (h t) -> p h t", h=16),
              dq_col(d_k, c8), key=kk,
              after=b_kch[par].wr(kk) + dqk_done)
        s.dma("ac", f"vch{par}", vch[par],
              d_v.ap()[:, c8 * 128:(c8 + 1) * 128], transpose=True,
              key=kv, after=b_vch[par].wr(kv))
        kqh = K("a2qh")
        aft = [kq, k_qdecb] + b_Qhat.wr(kqh) + b_qch[par].rd(kqh)
        s.dv.tensor_tensor(Qhat, qch[par], qdecb_s, ALU.mult).after(*aft).inc(
            "dv", key=kqh)
        for grp in range(4):
            pz = (c8 * 4 + grp) % 2
            gs4 = slice(grp * 512, (grp + 1) * 512)
            kz = K("a2z")
            for q in range(4):
                h = grp * 4 + q
                hs = slice(h * 128, (h + 1) * 128)
                aft = [kk, kq] + b_kch[par].rd(kz) + b_qch[par].rd(kz)
                if q == 0:
                    aft += psb[pz].wr(kz)
                r = s.pe.matmul(PS[pz][:, q * 128:(q + 1) * 128],
                                kch[par][:, hs], qch[par][:, hs],
                                start=True, stop=True).after(*aft)
                if q == 3:
                    r.inc("pe", key=kz)
            kzm = K("a2zm")
            aft = [kz, k_maskT] + b_Zm[pz].wr(kzm) + psb[pz].rd(kzm)
            if c8 == 0 and grp < 2:
                aft += b_ktch[0].readers + [b_ktch[0].writer]
            s.dv.tensor_tensor(ZmS[pz], PS[pz][:, 0:512], maskT_s[:, gs4],
                               ALU.mult).after(*aft).inc("dv", key=kzm)
            kyg = K("a2y2")
            for q in range(4):
                h = grp * 4 + q
                hs = slice(h * 128, (h + 1) * 128)
                qsl = slice(q * 128, (q + 1) * 128)
                aft = [kzm, kv] + b_Zm[pz].rd(kyg) + b_vch[par].rd(kyg)
                if q == 0:
                    aft += psb[2 + pz].wr(kyg)
                    b_yp[c8].readers.append(kyg)
                r = s.pe.matmul(PS[2 + pz][:, qsl], ZmS[pz][:, qsl],
                                vch[par][:, hs], start=True,
                                stop=(c8 == 0)).after(*aft)
                if c8 == 0 and q == 3:
                    r.inc("pe", key=kyg)
                if c8 >= 1:
                    aft2 = [kqh, sbf_of[(c8, grp)]] + b_Qhat.rd(kyg)
                    r = s.pe.matmul(PS[2 + pz][:, qsl], Qhat[:, hs],
                                    ypart(c8)[:, hs], start=False,
                                    stop=True).after(*aft2)
                    if q == 3:
                        r.inc("pe", key=kyg)
            kye = K("a2ye")
            aft = [kyg] + psb[2 + pz].rd(kye)
            if grp == 0:
                aft += b_yp[c8].wr(kye)
            s.ac.activation(ypart(c8)[:, gs4], PS[2 + pz][:, 0:512],
                            AF.Copy).after(*aft).inc("ac", key=kye)
            b_yp[c8].writer = kye
            ye_of[(c8, grp)] = kye
        if c8 == 1:
            for b in range(4):
                kg_ = K("gld"); gld_keys[b] = kg_
                aft = [kcc] + b_gt[b % 4].wr(kg_)
                s.dma("gp", "gld",
                      gtmp[b % 4].rearrange("p (h e) -> p h e", h=16),
                      cc_out_r[b], key=kg_, after=aft)
        if c8 == 6:
            for b in range(NCORES):
                if b >= 4:
                    kg_ = K("gld"); gld_keys[b] = kg_
                    aft = [kcc] + b_gt[b % 4].wr(kg_)
                    s.dma("gp", "gld",
                          gtmp[b % 4].rearrange("p (h e) -> p h e", h=16),
                          cc_out_r[b], key=kg_, after=aft)
                kp = K("pcm")
                for h in range(H):
                    hs = slice(h * 128, (h + 1) * 128)
                    wsl = wtab_sb[:, b * H + h:b * H + h + 1]
                    aft2 = [gld_keys[b], k_wtab]
                    if b == 0 and h == 0:
                        aft2 += [kccw] + b_SP.wr(kp)
                    if b == 0:
                        r = s.dv.tensor_scalar(SP[:, hs], gtmp[0][:, hs],
                                               wsl, None, ALU.mult).after(*aft2)
                    else:
                        r = s.dv.scalar_tensor_tensor(
                            SP[:, hs], gtmp[b % 4][:, hs], wsl, SP[:, hs],
                            ALU.mult, ALU.add).after(*aft2)
                    if h == H - 1:
                        r.inc("dv", key=kp)
                        b_gt[b % 4].rd(kp)
                        b_SP.writer = kp
                        b_SP.readers = []
                pcm = kp
        if stop_after == "a2c0":
            break

    if not ph("p5"):
        return _finish([])

    b_yn = Buf()
    kgnw = K("gnw")
    s.dma("gp", "gnw", gnw_s, gnw_d, key=kgnw,
          after=[pcm] + list(b_lnf.readers))
    kgnb = K("gnb")
    s.dma("gp", "gnb", gnb_s, gnb_d, key=kgnb, after=[pcm])
    b_gnw = Buf(); b_gnw.writer = kgnw
    b_gnb = Buf(); b_gnb.writer = kgnb
    if not ph("post"):
        return _finish([])

    b_gfh = Buf()
    b_xs2 = [Buf(), Buf()]
    b_xs2[0].readers += b_gt[0].readers + [b_gt[0].writer]
    b_xs2[1].readers += b_gt[1].readers + [b_gt[1].writer]
    b_Tb2 = Buf()
    b_Tb2.readers += b_gt[2].readers + [b_gt[2].writer]
    ln2_w = []

    def l2_pre2(tc8):
        hf_, tc_ = divmod(tc8, 4)
        return [x2w_keys[(hf_, c4_, tc_)] for c4_ in range(4)]
    b_PT = Buf()
    b_pdec = Buf()
    b_x0 = [Buf(), Buf()]
    b_x2o = [Buf(), Buf()]
    lnf_r = list(b_lnf.readers)
    for bb in (b_x0[0], b_x0[1], b_x2o[0], b_x2o[1], b_yn):
        bb.readers += lnf_r
    b_stats = Buf()
    x2w_keys = {}

    for hf in range(2):
        for tc in range(4):
            c8g = hf * 4 + tc
            par = tc % 2
            kq = K("qch2")
            s.dma("sy", f"qch{par}", qch[par].rearrange("p (h t) -> p h t", h=16),
                  dq_col(d_q, c8g), key=kq, after=b_qch[par].wr(kq))
            kp = K("pdec")
            aft = b_pdec.wr(kp)
            if hf == 0 and tc == 0:
                aft += b_kch[0].readers + b_kch[1].readers + \
                    [b_kch[0].writer, b_kch[1].writer]
            s.dma("sy", "pdec", pdec_s, pdec_d[c8g], key=kp, after=aft)
            kqh = K("qh2")
            aft = [kq, k_qdecb] + b_Qhat.wr(kqh) + b_qch[par].rd(kqh)
            s.dv.tensor_tensor(Qhat, qch[par], qdecb_s, ALU.mult).after(
                *aft).inc("dv", key=kqh)
            kpt = K("ptl")
            aft = [kp, pcm] + b_PT.wr(kpt) + b_pdec.rd(kpt) + b_SP.rd(kpt)
            s.dv.tensor_tensor(PT, SP[:, :], pdec_s, ALU.mult).after(*aft).inc(
                "dv", key=kpt)
            for h in range(16):
                hs = slice(h * 128, (h + 1) * 128)
                kmm = K("pth")
                aft = [kqh, kpt] + psb[6 + h % 2].wr(kmm) + \
                    b_Qhat.rd(kmm) + b_PT.rd(kmm)
                s.pe.matmul(PS[6 + h % 2][:, 0:128], Qhat[:, hs], PT[:, hs],
                            start=True, stop=True).after(*aft).inc("pe", key=kmm)
                ky = K("ysl")
                aft = [kmm, ye_of[(c8g, h // 4)]] + psb[6 + h % 2].rd(ky)
                b_yp[c8g].readers.append(ky)
                s.dv.tensor_tensor(ypart(c8g)[:, hs], ypart(c8g)[:, hs],
                                   PS[6 + h % 2][:, 0:128], ALU.add).after(
                    *aft).inc("dv", key=ky)
                b_yp[c8g].writer = ky
                j = tc * 16 + h
                krd = K()
                s.dv.reduce_sum(st_s1[:, j:j + 1], ypart(c8g)[:, hs],
                                AX.X).after(ky).inc("dv", key=krd)
                kst = K("yst")
                r = s.dv.scalar_tensor_tensor(
                    Tb[:, 0:128], ypart(c8g)[:, hs], 1.0, ypart(c8g)[:, hs],
                    ALU.mult, ALU.mult, accum_out=st_s2[:, j:j + 1])
                r.after(*(b_Tb.wr(kst) + [ky])).inc("dv", key=kst)
                last_sweep = (krd, kst)
        kmu_b = K()
        s.dv.tensor_scalar(st_mu[:, :], st_s1[:, :], 1.0 / D, None,
                           ALU.mult).after(*last_sweep).inc("dv", key=kmu_b)
        kms_b = K()
        s.dv.scalar_tensor_tensor(st_msq[:, :], st_mu[:, :], 1.0, st_mu[:, :],
                                  ALU.mult, ALU.mult).after(kmu_b).inc("dv", key=kms_b)
        kv = K("gnv")
        s.dv.scalar_tensor_tensor(st_var[:, :], st_s2[:, :], 1.0 / D,
                                  st_msq[:, :], ALU.mult, ALU.subtract).after(
            kms_b).inc("dv", key=kv)
        kv2 = K()
        s.dv.tensor_scalar(st_var[:, :], st_var[:, :], EPS, None,
                           ALU.add).after(kv).inc("dv", key=kv2)
        kq_ = K("gnq")
        s.ac.activation(st_sq[:, :], st_var[:, :], AF.Sqrt).after(
            kv2).inc("ac", key=kq_)
        kr = K("gnr")
        s.dv.reciprocal(st_rstd[:, :], st_sq[:, :]).after(kq_).inc("dv", key=kr)
        grow_keys = []
        for tc in range(4):
            c8g = hf * 4 + tc
            for h in range(16):
                hs = slice(h * 128, (h + 1) * 128)
                j = tc * 16 + h
                kn = K()
                aft = [kr, kmu_b] + (b_yn.wr(kn) if h == 0 else [])
                b_yp[c8g].readers.append(kn)
                s.dv.tensor_scalar(yn_s[:, hs], ypart(c8g)[:, hs],
                                   st_mu[:, j:j + 1], st_rstd[:, j:j + 1],
                                   ALU.subtract, ALU.mult).after(*aft).inc("dv", key=kn)
                kn_last = kn
            ka = K()
            s.dv.tensor_tensor(yn_s, yn_s, gnw_s, ALU.mult).after(
                kgnw, kn_last).inc("dv", key=ka)
            b_gnw.readers.append(ka)
            kb = K()
            s.dv.tensor_tensor(yn_s, yn_s, gnb_s, ALU.add).after(
                kgnb, ka).inc("dv", key=kb)
            b_gnb.readers.append(kb)
            kgr = K("grow")
            aft = [gse8[(3, c8g)], kb] + b_Tb.wr(kgr) + b_sgh8[c8g].rd(kgr)
            s.dv.tensor_tensor(Tb[:, :], yn_s, sgh8(c8g), ALU.mult).after(
                *aft).inc("dv", key=kgr)
            b_yn.readers.append(kgr)
            kw = K("dgw")
            s.dma("sy", "dgw", dg_t[c8g], Tb[:, :], key=kw,
                  after=b_Tb.rd(kw) + [kgr])
            grow_keys.append(kw)
        kgf = []
        for fc in range(16):
            k = K("gfh"); kgf.append(k)
            aft = [grow_keys[3]]
            if fc == 0:
                aft += b_gfh.wr(k)
                if hf == 0:
                    aft += b_ktch[0].readers + b_ktch[1].readers + \
                        b_vch[0].readers + b_vch[1].readers + \
                        b_Zm[0].readers + b_Zm[1].readers + \
                        [b_ktch[0].writer, b_ktch[1].writer,
                         b_vch[0].writer, b_vch[1].writer,
                         b_Zm[0].writer, b_Zm[1].writer]
            s.dma("ac", "gfh", gfh(fc),
                  d_g.ap()[hf * 512:(hf + 1) * 512, fc * 128:(fc + 1) * 128],
                  transpose=True, key=k, after=aft)
        b_gfh.writer = kgf[-1]
        gfh_all = kgf
        for c4 in range(4):
            for qb in range(2):
                k = K("wp"); wgq_keys[("p", hf, c4, qb)] = k
                s.dma("gp", f"wpc{qb % 2}", wpq[qb % 2],
                      wp_img[c4][:, qb * 2048:(qb + 1) * 2048],
                      key=k, after=b_wpq[qb % 2].wr(k), max_dma_last_dim=8192)
            kpj = {}
            for tc in range(2):
                sl = tc % 2
                kx = K("x0")
                s.dma("sy", f"x0{sl}", x0s[sl],
                      bass.AP(x.tensor, (hf * 512 + tc * 128) * C + c4 * 512,
                              [[C, 128], [1, 512]]),
                      key=kx, after=b_x0[sl].wr(kx))
                wgq_keys[("x0", hf, c4, tc)] = kx
            for cc in range(16):
                qb = cc // 4
                for tc in range(4):
                    if cc == 0:
                        kpj[tc] = K("pjmm")
                    aft = [wgq_keys[("p", hf, c4, qb)]]
                    if c4 == 0 and cc == 0:
                        aft += gfh_all
                    if cc == 0:
                        aft += psb[tc].wr(kpj[tc])
                    r = s.pe.matmul(PS[tc][:, :],
                                    gfh(cc)[:, tc * 128:(tc + 1) * 128],
                                    wpq[qb % 2][:, (cc % 4) * 512:(cc % 4 + 1) * 512],
                                    start=(cc == 0), stop=(cc == 15)).after(*aft)
                    b_gfh.readers.append(kpj[tc])
                    if cc % 4 == 3 and tc == 3:
                        if cc == 15:
                            b_wpq[qb % 2].rd(kpj[tc])
                        else:
                            kqr = K("pqr")
                            r.inc("pe", key=kqr)
                            b_wpq[qb % 2].rd(kqr)
                            nqb = qb + 2
                            if nqb < 4:
                                k2 = K("wp"); wgq_keys[("p", hf, c4, nqb)] = k2
                                s.dma("gp", f"wpc{nqb % 2}", wpq[nqb % 2],
                                      wp_img[c4][:, nqb * 2048:(nqb + 1) * 2048],
                                      key=k2, after=b_wpq[nqb % 2].wr(k2),
                                      max_dma_last_dim=8192)
                    if cc == 15:
                        r.inc("pe", key=kpj[tc])
            for tc in range(4):
                sl = tc % 2
                ke = K("x2e")
                aft = [kpj[tc], wgq_keys[("x0", hf, c4, tc)]] + \
                    b_x2o[sl].wr(ke) + psb[tc].rd(ke) + b_x0[sl].rd(ke)
                s.dv.tensor_tensor(x2o[sl], PS[tc][:, :], x0s[sl],
                                   ALU.add).after(*aft).inc("dv", key=ke)
                kw = K("x2w")
                s.dma("sy", f"x2w{sl}",
                      bass.AP(d_x2.ap().tensor, (hf * 512 + tc * 128) * C + c4 * 512,
                              [[C, 128], [1, 512]]),
                      x2o[sl], key=kw, after=b_x2o[sl].rd(kw) + [ke])
                x2w_keys[(hf, c4, tc)] = kw
                if tc + 2 < 4:
                    kx = K("x0")
                    s.dma("sy", f"x0{sl}", x0s[sl],
                          bass.AP(x.tensor,
                                  (hf * 512 + (tc + 2) * 128) * C + c4 * 512,
                                  [[C, 128], [1, 512]]),
                          key=kx, after=b_x0[sl].wr(kx))
                    wgq_keys[("x0", hf, c4, tc + 2)] = kx
        ln2_w += ln_pass("L2", dx2_t, dln2_t, l2_pre2,
                         [pcm] if hf == 0 else [],
                         tiles=range(hf * 4, hf * 4 + 4),
                         bufs=(xs2, Tb2, b_xs2, b_Tb2))

    tap_ready["d_g"] = [x2w_keys[(1, 3, 3)]]
    tap_ready["d_x2"] = [x2w_keys[(hf_, c4_, tc_)]
                        for hf_ in range(2) for c4_ in range(4) for tc_ in range(4)]
    if not ph("post") or not ph("ffn"):
        return _finish([])

    l2_pre = l2_pre2
    tap_ready["d_ln2"] = [ln2_w[7]]

    b_ln2f = Buf()
    kl2 = []
    for fc in range(16):
        k = K("ln2ld"); kl2.append(k)
        aft = [ln2_w[7]]
        if fc == 0:
            for bb in (b_gnw, b_gnb, b_yn, b_x0[0], b_x0[1],
                       b_x2o[0], b_x2o[1]):
                aft += bb.readers + ([bb.writer] if bb.writer else [])
            aft += b_ln2f.wr(k)
        s.dma("ac", "ln2ld", ln2f(fc), d_ln2.ap()[:, fc * 128:(fc + 1) * 128],
              transpose=True, key=k, after=aft)
    b_ln2f.writer = kl2[-1]
    ln2_all = kl2

    ksd = K("sdl")
    s.dma("sy", "sdl", sdown_s, sdown_d, key=ksd,
          after=b_SP.readers + [b_SP.writer])
    b_sdown = Buf(); b_sdown.writer = ksd

    b_wga = [Buf(), Buf()]
    b_wua = [Buf(), Buf()]
    b_wdn = [Buf(), Buf()]
    b_fuse = [Buf() for _ in range(16)]
    b_gs = [Buf(), Buf()]
    b_acc = [Buf() for _ in range(8)]
    for oc in range(2):
        for t in range(4):
            b_wga[oc].readers += b_yp[t].readers + [b_yp[t].writer]
            b_wua[oc].readers += b_yp[t].readers + [b_yp[t].writer]
    for t in range(4, 8):
        b_wdn[0].readers += b_yp[t].readers + [b_yp[t].writer]
    b_wdn[1].readers += b_Qhat.readers + [b_Qhat.writer] + \
        b_wpq[0].readers + b_wpq[1].readers + \
        [b_wpq[0].writer, b_wpq[1].writer]
    for tt in range(2):
        b_gs[tt].readers += b_Tb.readers + [b_Tb.writer]
    fuse_prime = b_qch[0].readers + b_qch[1].readers + \
        [b_qch[0].writer, b_qch[1].writer] + b_pdec.readers + \
        [b_pdec.writer] + b_gfh.readers + [b_gfh.writer]
    for oc in range(16):
        b_fuse[oc].readers += fuse_prime
    acc_prime = b_xs[0].readers + b_xs[1].readers + \
        b_xs2[0].readers + b_xs2[1].readers + b_Tb2.readers + [b_Tb2.writer] + \
        b_cs[0].readers + b_cs[1].readers
    for bb in b_gt + b_sgh8:
        acc_prime += bb.readers + ([bb.writer] if bb.writer else [])
    for t8 in range(8):
        b_acc[t8].readers += acc_prime

    fgs_keys = {}
    ffu_keys = {}
    fdm_keys = {}
    fde_keys = {}
    for og in range(4):
        for oc in range(16):
            ocg = og * 16 + oc
            kwg = K("wga")
            s.dma("gp", f"wga{oc % 2}", wga[oc % 2], wgate_img[ocg], key=kwg,
                  after=b_wga[oc % 2].wr(kwg), max_dma_last_dim=2048)
            kwu = K("wua")
            s.dma("gp", f"wua{oc % 2}", wua[oc % 2], wup_img[ocg], key=kwu,
                  after=b_wua[oc % 2].wr(kwu), max_dma_last_dim=2048)
            kgm = {0: K("fgm"), 1: K("fgm")}
            kum = {0: K("fum"), 1: K("fum")}
            for cc in range(16):
                for tt in range(2):
                    aft = [kwg]
                    if ocg == 0 and tt == 0 and cc == 0:
                        aft += ln2_all
                    if cc == 0:
                        aft += psb[tt].wr(kgm[tt])
                    r = s.pe.matmul(PS[tt][:, :],
                                    wga[oc % 2][:, cc * 128:(cc + 1) * 128],
                                    ln2f(cc)[:, tt * 512:(tt + 1) * 512],
                                    start=(cc == 0), stop=(cc == 15)).after(*aft)
                    b_ln2f.readers.append(kgm[tt])
                    if cc == 15:
                        b_wga[oc % 2].rd(kgm[tt])
                        r.inc("pe", key=kgm[tt])
            for cc in range(16):
                for tt in range(2):
                    aft = [kwu]
                    if cc == 0:
                        aft += psb[2 + tt].wr(kum[tt])
                    r = s.pe.matmul(PS[2 + tt][:, :],
                                    wua[oc % 2][:, cc * 128:(cc + 1) * 128],
                                    ln2f(cc)[:, tt * 512:(tt + 1) * 512],
                                    start=(cc == 0), stop=(cc == 15)).after(*aft)
                    b_ln2f.readers.append(kum[tt])
                    if cc == 15:
                        b_wua[oc % 2].rd(kum[tt])
                        r.inc("pe", key=kum[tt])
            for tt in range(2):
                kgs = K("fgs")
                aft = [kgm[tt], k_sgc] + b_gs[tt].wr(kgs) + psb[tt].rd(kgs)
                s.ac.activation(gss[tt], PS[tt][:, :], AF.Silu,
                                scale=sgc_s[:, ocg:ocg + 1]).after(*aft).inc(
                    "ac", key=kgs)
                fgs_keys[(ocg, tt)] = kgs
                kfu = K("ffu")
                aft = [kum[tt], kgs, k_suc]
                if tt == 0:
                    aft += b_fuse[oc].wr(kfu)
                aft += psb[2 + tt].rd(kfu) + b_gs[tt].rd(kfu)
                s.dv.scalar_tensor_tensor(
                    fuse(oc)[:, tt * 512:(tt + 1) * 512], PS[2 + tt][:, :],
                    suc_s[:, ocg:ocg + 1], gss[tt], ALU.mult, ALU.mult).after(
                    *aft).inc("dv", key=kfu)
                ffu_keys[(ocg, tt)] = kfu
        if og < 3:
            dseq = [(ht, hf2, ht % 2, hf2 == 0)
                    for ht in range(4) for hf2 in range(2)]
        else:
            dseq = [(ht, hf2, (hf2 * 4 + ht) % 2, True)
                    for hf2 in range(2) for ht in range(4)]
        for (ht, hf2, dpar, fresh) in dseq:
            if fresh:
                kwd = K("wdn")
                s.dma("gp", f"wdn{dpar}", wdns[dpar], wdown_img[og * 4 + ht],
                      key=kwd, after=b_wdn[dpar].wr(kwd), max_dma_last_dim=2048)
            if True:
                kdm = {}
                for tc in range(4):
                    kdm[tc] = K("fdm")
                for oc in range(16):
                    for tc in range(4):
                        aft = [kwd, ffu_keys[(og * 16 + oc, hf2)]]
                        if oc == 0:
                            aft += psb[4 + tc].wr(kdm[tc])
                        r = s.pe.matmul(
                            PS[4 + tc][:, :],
                            fuse(oc)[:, hf2 * 512 + tc * 128:hf2 * 512 + (tc + 1) * 128],
                            wdns[dpar][:, oc * 512:(oc + 1) * 512],
                            start=(oc == 0), stop=(oc == 15)).after(*aft)
                        b_fuse[oc].readers.append(kdm[tc])
                        if oc == 15:
                            b_wdn[dpar].rd(kdm[tc])
                            r.inc("pe", key=kdm[tc])
                for tc in range(4):
                    t8 = hf2 * 4 + tc
                    accsl = acc(t8)[:, ht * 512:(ht + 1) * 512]
                    ke = K("fde")
                    aft = [kdm[tc]] + psb[4 + tc].rd(ke)
                    if og == 0:
                        aft += b_acc[t8].wr(ke)
                        r = s.dv.tensor_copy(accsl, PS[4 + tc][:, :]).after(*aft)
                    else:
                        b_acc[t8].readers.append(ke)
                        r = s.dv.tensor_tensor(accsl, accsl, PS[4 + tc][:, :],
                                               ALU.add).after(*aft)
                        b_acc[t8].writer = ke
                        b_acc[t8].readers = []
                    r.inc("dv", key=ke)
                    fde_keys[(og, ht, hf2, tc)] = ke
    b_x2f = [Buf(), Buf()]
    b_x2f[0].readers += b_PT.readers + [b_PT.writer] + \
        b_Qhat.readers + [b_Qhat.writer]
    b_x2f[1].readers += b_Tb.readers + [b_Tb.writer] + \
        b_gs[0].readers + b_gs[1].readers + \
        b_T1[0].readers + b_T1[1].readers + \
        b_tb1[0].readers + b_tb1[1].readers + \
        b_qp[0].readers + b_qp[1].readers
    ow_keys = []
    for t8 in range(8):
        par = t8 % 2
        kx = K("x2f")
        aft = l2_pre(t8) + b_x2f[par].wr(kx)
        s.dma("ac", f"x2f{par}", x2f[par], dx2_t[t8], key=kx, after=aft)
        km = K("finm")
        aft = [fde_keys[(3, ht, t8 // 4, t8 % 4)] for ht in range(4)] + \
            [ksd] + b_acc[t8].wr(km) + b_sdown.rd(km)
        s.dv.tensor_tensor(acc(t8), acc(t8), sdown_s, ALU.mult).after(
            *aft).inc("dv", key=km)
        kf = K("fin")
        s.dv.tensor_tensor(acc(t8), acc(t8), x2f[par], ALU.add).after(
            kx, km, *(b_acc[t8].wr(kf) + b_x2f[par].rd(kf))).inc("dv", key=kf)
        ko = K("ow")
        s.dma("sy" if par == 0 else "ac", f"ow{par}", out_t[t8], acc(t8),
              key=ko, after=b_acc[t8].rd(ko) + [kf])
        ow_keys.append(ko)

    return _finish(ow_keys)


def kernel(L0_attn_x, L0_attn_w_qkv, L0_attn_w_g, L0_attn_w_proj,
           L0_attn_gn_weight, L0_attn_gn_bias, L0_ffn_w_gate, L0_ffn_w_up,
           L0_ffn_w_down, L0_ffn_s_gate, L0_ffn_s_up, L0_ffn_s_down):
    x_full = np.ascontiguousarray(np.asarray(L0_attn_x, np.float32).reshape(T, C))
    sh = _host_shared(L0_attn_w_qkv, L0_attn_w_g, L0_attn_w_proj,
                      L0_ffn_w_gate, L0_ffn_w_up, L0_ffn_w_down,
                      L0_attn_gn_weight, L0_attn_gn_bias,
                      L0_ffn_s_gate, L0_ffn_s_up, L0_ffn_s_down)
    cos, sin = _host_xpos()

    nc = build_nc(taps=tuple(_DEBUG_TAPS))

    in_maps = []
    for core in range(NCORES):
        pc = _host_percore(core, x_full, cos, sin)
        m = dict(x=pc["x"], cs_img=pc["cs_img"], wtab=pc["wtab"],
                 wqkv_img=sh["wqkv_img"], wg_img=sh["wg_img"], wp_img=sh["wp_img"],
                 wgate_img=sh["wgate_img"], wup_img=sh["wup_img"],
                 wdown_img=sh["wdown_img"], maskT=sh["maskT"], qdecb=sh["qdecb"],
                 ghat=sh["ghat"], pdec=sh["pdec"], pmat=sh["pmat"],
                 gnw_b=sh["gnw_b"], gnb_b=sh["gnb_b"], sdown_b=sh["sdown_b"],
                 sg_c=sh["sg_c"], su_c=sh["su_c"])
        in_maps.append(m)

    res = run_bass_kernel_spmd(nc, in_maps, core_ids=list(range(NCORES)))
    outs = [res.results[c]["out"] for c in range(NCORES)]
    full = np.concatenate(outs, axis=0).reshape(1, T, C).astype(np.float32)
    if _DEBUG_TAPS:
        kernel._tap_results = [
            {k: v for k, v in res.results[c].items() if k.startswith("tap_")}
            for c in range(NCORES)]
    return full


# version 19
# speedup vs baseline: 1.1233x; 1.0106x over previous
import numpy as np
import concourse.bass as bass
import concourse.mybir as mybir
from concourse.bass_utils import run_bass_kernel_spmd

F32 = mybir.dt.float32
BF16 = mybir.dt.bfloat16
I8 = mybir.dt.int8
AF = mybir.ActivationFunctionType
ALU = mybir.AluOpType
AX = mybir.AxisListType

NCORES = 8
T = 8192
C = 2048
H = 16
D = 128
TPC = T // NCORES
NCH = TPC // 128
INTER = 8192
EPS = 1e-5
RD = float(D) ** -0.5

_DEBUG_TAPS = []

KBf = 256


ENGS = ("pe", "dv", "ac", "sy", "gp")


class _Op:
    __slots__ = ("method", "args", "kwargs", "waits", "incs")

    def __init__(self, method, args, kwargs):
        self.method = method
        self.args = args
        self.kwargs = kwargs
        self.waits = {}
        self.incs = []


class _OpRef:
    def __init__(self, s, op):
        self.s = s
        self.op = op

    def after(self, *keys):
        for k in keys:
            if k is None:
                continue
            if isinstance(k, list):
                self.after(*k)
                continue
            sem, v = self.s.tick[k]
            if self.op.waits.get(sem, -1) < v:
                self.op.waits[sem] = v
        return self

    def inc(self, sem, key=None, dma=False):
        step = 16 if dma else 1
        self.s.cnt[sem] = self.s.cnt.get(sem, 0) + step
        v = self.s.cnt[sem]
        self.op.incs.append((sem, step))
        if key is not None:
            keys = key if isinstance(key, list) else [key]
            for k in keys:
                assert k not in self.s.tick, f"dup key {k}"
                self.s.tick[k] = (sem, v)
        return self


class _EngProxy:
    def __init__(self, s, eng):
        self._s = s
        self._e = eng

    def __getattr__(self, m):
        def call(*a, **kw):
            op = _Op(m, a, kw)
            self._s.q[self._e].append(op)
            return _OpRef(self._s, op)
        return call


class Sched:
    def __init__(self):
        self.q = {e: [] for e in ENGS}
        self.cnt = {}
        self.tick = {}
        self.nkey = 0
        for e in ENGS:
            setattr(self, e, _EngProxy(self, e))

    def key(self, tag=""):
        self.nkey += 1
        return (tag, self.nkey)

    def dma(self, eng, slot, out, in_, key=None, after=(), **kw):
        r = getattr(self, eng).dma_start(out=out, in_=in_, **kw)
        r.after(*after)
        r.inc("D" + slot, key=key, dma=True)
        return r


class Buf:

    def __init__(self):
        self.writer = None
        self.readers = []

    def wr(self, key):
        aft = list(self.readers) if self.readers else (
            [self.writer] if self.writer is not None else [])
        self.writer = key
        self.readers = []
        return aft

    def rd(self, key):
        self.readers.append(key)
        return [self.writer] if self.writer is not None else []


def _gammas():
    return np.array([1.0 - 2.0 ** (-5.0 - h) for h in range(H)], dtype=np.float64)


def _host_shared(wqkv, wg, wp, wgate, wup, wdown, gnw, gnb, sg, su, sd):
    g = _gammas()
    sh = {}
    sh["wqkv_img"] = np.ascontiguousarray(
        np.asarray(wqkv, np.float32).reshape(16, 128, 12, 512)
        .transpose(2, 1, 0, 3).reshape(12, 128, 8192))
    sh["wg_img"] = np.ascontiguousarray(
        np.asarray(wg, np.float32).reshape(16, 128, 4, 512)
        .transpose(2, 1, 0, 3).reshape(4, 128, 8192))
    sh["wp_img"] = np.ascontiguousarray(
        np.asarray(wp, np.float32).reshape(16, 128, 4, 512)
        .transpose(2, 1, 0, 3).reshape(4, 128, 8192))
    sh["wgate_img"] = np.ascontiguousarray(
        np.asarray(wgate).reshape(16, 128, 64, 128)
        .transpose(2, 1, 0, 3).reshape(64, 128, 2048))
    sh["wup_img"] = np.ascontiguousarray(
        np.asarray(wup).reshape(16, 128, 64, 128)
        .transpose(2, 1, 0, 3).reshape(64, 128, 2048))
    sh["wdown_img"] = np.ascontiguousarray(
        np.asarray(wdown).reshape(4, 16, 128, 4, 512)
        .transpose(0, 3, 2, 1, 4).reshape(16, 128, 8192))
    j = np.arange(128, dtype=np.float64)
    maskT = np.zeros((128, H * 128), dtype=np.float64)
    for h in range(H):
        e = j[None, :] - j[:, None]
        maskT[:, h * 128:(h + 1) * 128] = np.where(e >= 0, g[h] ** np.maximum(e, 0), 0.0)
    sh["maskT"] = maskT.astype(np.float32)
    qd = np.concatenate([g[h] ** (j + 1.0) for h in range(H)])
    sh["qdecb"] = np.broadcast_to(qd.astype(np.float32)[None, :], (128, H * 128)).copy()
    gh = np.zeros((128, H * 128), dtype=np.float64)
    for h in range(H):
        gh[:, h * 128:(h + 1) * 128] = (g[h] ** (127.0 - j))[:, None]
    sh["ghat"] = gh.astype(np.float32)
    pdec = np.zeros((NCH, 128, H * 128), dtype=np.float32)
    for c8 in range(NCH):
        row = np.concatenate([np.full(128, g[h] ** (128.0 * c8)) for h in range(H)])
        pdec[c8] = row.astype(np.float32)[None, :]
    sh["pdec"] = pdec
    P = np.zeros((128, 128), dtype=np.float32)
    ii = np.arange(0, 128, 2)
    P[ii + 1, ii] = -1.0
    P[ii, ii + 1] = 1.0
    sh["pmat"] = P
    sh["gnw_b"] = np.broadcast_to(np.asarray(gnw, np.float32).reshape(1, C), (128, C)).copy()
    sh["gnb_b"] = np.broadcast_to(np.asarray(gnb, np.float32).reshape(1, C), (128, C)).copy()
    sh["sdown_b"] = np.broadcast_to(np.asarray(sd, np.float32).reshape(1, C), (128, C)).copy()
    sh["sg_c"] = np.ascontiguousarray(np.asarray(sg, np.float32).reshape(64, 128).T)
    sh["su_c"] = np.ascontiguousarray(np.asarray(su, np.float32).reshape(64, 128).T)
    return sh


def _host_xpos():
    i = np.arange(C // 2, dtype=np.float64)
    zeta = (2.0 * i + 0.4 * C) / (1.4 * C)
    invf = 1.0 / (10000.0 ** (2.0 * i / C))
    t = np.arange(T, dtype=np.float64)
    scale = zeta[None, :] ** (t[:, None] / 512.0)
    ang = t[:, None] * invf[None, :]
    cos = np.repeat(np.cos(ang) * scale, 2, axis=1).astype(np.float32)
    sin = np.repeat(np.sin(ang) * scale, 2, axis=1).astype(np.float32)
    return cos, sin


def _host_percore(core, x_full, cos, sin):
    g = _gammas()
    pc = {}
    pc["x"] = np.ascontiguousarray(x_full[core * TPC:(core + 1) * TPC])
    cs = np.empty((16, 128, 2048), dtype=np.float32)
    csl = cos[core * TPC:(core + 1) * TPC]
    ssl = sin[core * TPC:(core + 1) * TPC]
    for fc in range(16):
        cs[fc, :, 0:1024] = csl[:, fc * 128:(fc + 1) * 128].T
        cs[fc, :, 1024:2048] = ssl[:, fc * 128:(fc + 1) * 128].T
    pc["cs_img"] = cs
    wt = np.zeros((128, NCORES * H), dtype=np.float32)
    for b in range(core):
        for h in range(H):
            wt[:, b * H + h] = g[h] ** (1024.0 * (core - 1 - b))
    pc["wtab"] = wt
    return pc


def build_nc(taps=(), stop_after="ffn"):
    PH = ["a1", "a2p0", "a2p1", "a2p2", "a2p3", "a2p", "a2c0", "a2", "p5", "post", "ffn"]
    lim = PH.index(stop_after)

    def ph(name):
        return PH.index(name) <= lim

    nc = bass.Bass("TRN2", target_bir_lowering=False)
    g64 = _gammas()
    g128 = [float(np.float32(g64[h] ** 128.0)) for h in range(H)]

    x = nc.dram_tensor("x", [TPC, C], F32, kind="ExternalInput").ap()
    out = nc.dram_tensor("out", [TPC, C], F32, kind="ExternalOutput").ap()
    wqkv_img = nc.dram_tensor("wqkv_img", [12, 128, 8192], F32, kind="ExternalInput").ap()
    wg_img = nc.dram_tensor("wg_img", [4, 128, 8192], F32, kind="ExternalInput").ap()
    wp_img = nc.dram_tensor("wp_img", [4, 128, 8192], F32, kind="ExternalInput").ap()
    wgate_img = nc.dram_tensor("wgate_img", [64, 128, 2048], I8, kind="ExternalInput").ap()
    wup_img = nc.dram_tensor("wup_img", [64, 128, 2048], I8, kind="ExternalInput").ap()
    wdown_img = nc.dram_tensor("wdown_img", [16, 128, 8192], I8, kind="ExternalInput").ap()
    maskT_d = nc.dram_tensor("maskT", [128, 2048], F32, kind="ExternalInput").ap()
    qdecb_d = nc.dram_tensor("qdecb", [128, 2048], F32, kind="ExternalInput").ap()
    ghat_d = nc.dram_tensor("ghat", [128, 2048], F32, kind="ExternalInput").ap()
    pdec_d = nc.dram_tensor("pdec", [NCH, 128, 2048], F32, kind="ExternalInput").ap()
    pmat_d = nc.dram_tensor("pmat", [128, 128], F32, kind="ExternalInput").ap()
    gnw_d = nc.dram_tensor("gnw_b", [128, 2048], F32, kind="ExternalInput").ap()
    gnb_d = nc.dram_tensor("gnb_b", [128, 2048], F32, kind="ExternalInput").ap()
    sdown_d = nc.dram_tensor("sdown_b", [128, 2048], F32, kind="ExternalInput").ap()
    sg_d = nc.dram_tensor("sg_c", [128, 64], F32, kind="ExternalInput").ap()
    su_d = nc.dram_tensor("su_c", [128, 64], F32, kind="ExternalInput").ap()
    cs_d = nc.dram_tensor("cs_img", [16, 128, 2048], F32, kind="ExternalInput").ap()
    wtab_d = nc.dram_tensor("wtab", [128, NCORES * 16], F32, kind="ExternalInput").ap()

    d_ln = nc.dram_tensor("d_ln", [TPC, C], BF16)
    d_q = nc.dram_tensor("d_q", [C, TPC], BF16)
    d_k = nc.dram_tensor("d_k", [C, TPC], BF16)
    d_v = nc.dram_tensor("d_v", [C, TPC], BF16)
    d_g = nc.dram_tensor("d_g", [TPC, C], BF16)
    d_x2 = nc.dram_tensor("d_x2", [TPC, C], F32)
    d_ln2 = nc.dram_tensor("d_ln2", [TPC, C], BF16)
    cc_in = nc.dram_tensor("cc_in", [H * 128, 128], F32)
    cc_out = nc.dram_tensor("cc_out", [NCORES * H * 128, 128], F32)

    tap_outs = {}
    for tname in taps:
        src = {"d_ln": d_ln, "d_q": d_q, "d_k": d_k, "d_v": d_v, "d_g": d_g,
               "d_x2": d_x2, "d_ln2": d_ln2, "cc_in": cc_in,
               "cc_out": cc_out}.get(tname)
        if src is not None:
            tap_outs[tname] = (
                nc.dram_tensor("tap_" + tname, list(src.shape), src.dtype,
                               kind="ExternalOutput").ap(), src.ap())

    ARENA_KB = 204
    ar = nc.alloc_sbuf_tensor("arena", [128, ARENA_KB * KBf], F32)
    arb = ar.bitcast(BF16)

    def f32(kb, n):
        o = int(kb * KBf)
        return ar[:, o:o + n]

    def bf(kb, n):
        o = int(kb * KBf) * 2
        return arb[:, o:o + n]

    lnf = lambda cc: bf(0, 16 * 1024)[:, cc * 1024:(cc + 1) * 1024]
    ghat_s = bf(32, 2048)
    ZmS = [bf(48, 512), bf(49, 512)]
    ln2f = lambda cc: bf(0, 16 * 1024)[:, cc * 1024:(cc + 1) * 1024]
    gnw_s = f32(0, 2048)
    gnb_s = f32(8, 2048)
    yn_s = f32(16, 2048)
    x0s = [f32(24, 512), f32(26, 512)]
    x2o = [f32(28, 512), f32(30, 512)]
    qch = [bf(32, 2048), bf(36, 2048)]
    kch = [bf(40, 2048), bf(44, 2048)]
    ktch = [bf(48, 2048), bf(52, 2048)]
    vch = [bf(56, 2048), bf(60, 2048)]
    pdec_s = f32(40, 2048)
    gfh = lambda fc: bf(48, 16 * 512)[:, fc * 512:(fc + 1) * 512]
    fuse = lambda oc: bf(32, 16 * 1024)[:, oc * 1024:(oc + 1) * 1024]
    xs = [f32(64, 2048), f32(72, 2048)]
    cs_s = [f32(80, 2048), f32(88, 2048)]
    gtmp = [f32(64, 2048), f32(72, 2048), f32(80, 2048), f32(88, 2048)]
    sgh8 = lambda t8: bf(96, 16 * 1024)[:, t8 * 2048:(t8 + 1) * 2048]
    xs2 = [f32(64, 2048), f32(72, 2048)]
    Tb2 = bf(80, 2048)
    acc = lambda t8: f32(64, 8 * 2048)[:, t8 * 2048:(t8 + 1) * 2048]
    wqb = [bf(128, 8192), bf(144, 8192)]
    ypart = lambda c8: bf(128, 8 * 2048)[:, c8 * 2048:(c8 + 1) * 2048]
    wga = [bf(128, 2048), bf(132, 2048)]
    wua = [bf(136, 2048), bf(140, 2048)]
    wdns = [bf(144, 8192), bf(160, 8192)]
    x2f = [f32(184, 2048), f32(192, 2048)]
    maskT_s = bf(160, 2048)
    qdecb_s = bf(164, 2048)
    wpq = [bf(168, 2048), bf(172, 2048)]
    SP = f32(176, 2048)
    sdown_s = f32(176, 2048)
    PT = bf(184, 2048)
    Qhat = bf(188, 2048)
    Tb = bf(192, 2048)
    TbA1 = [bf(192, 512), bf(193, 512)]
    Qp = [bf(194, 512), bf(195, 512)]
    gss = [bf(192, 512), bf(193, 512)]
    T1 = f32(196, 1024)
    st_s1 = f32(200, 64)
    st_s2 = f32(200.25, 64)
    st_mu = f32(200.5, 64)
    st_msq = f32(200.75, 64)
    st_var = f32(201.0, 64)
    st_sq = f32(201.25, 64)
    st_rstd = f32(201.5, 64)
    sgc_s = f32(201.75, 64)
    suc_s = f32(202.0, 64)
    pmat_s = bf(202.25, 128)
    wtab_sb = f32(202.5, 128)

    PS = [nc.alloc_psum_tensor(f"ps{i}", [128, 512], F32) for i in range(8)]

    s = Sched()
    K = s.key

    psb = [Buf() for _ in range(8)]

    x_t = x.rearrange("(n p) c -> n p c", p=128)
    out_t = out.rearrange("(n p) c -> n p c", p=128)
    dln_t = d_ln.ap().rearrange("(n p) c -> n p c", p=128)
    dg_t = d_g.ap().rearrange("(n p) c -> n p c", p=128)
    dx2_t = d_x2.ap().rearrange("(n p) c -> n p c", p=128)
    dln2_t = d_ln2.ap().rearrange("(n p) c -> n p c", p=128)
    dq_rows = d_q.ap().rearrange("(n p) t -> n p t", p=128)
    dk_rows = d_k.ap().rearrange("(n p) t -> n p t", p=128)
    dv_rows = d_v.ap().rearrange("(n p) t -> n p t", p=128)

    def dq_col(dram, c8):
        return dram.ap().rearrange("(h d) t -> d h t", d=128)[:, :, c8 * 128:(c8 + 1) * 128]

    k_maskT = K("maskT"); s.dma("gp", "tb", maskT_s, maskT_d, key=k_maskT, max_dma_last_dim=8192)
    k_qdecb = K("qdecb"); s.dma("gp", "tb2", qdecb_s, qdecb_d, key=k_qdecb, max_dma_last_dim=8192)
    k_pmat = K("pmat"); s.dma("gp", "tb3", pmat_s, pmat_d, key=k_pmat, max_dma_last_dim=8192)
    k_sgc = K("sgc"); s.dma("sy", "tb4", sgc_s, sg_d, key=k_sgc)
    k_suc = K("suc"); s.dma("sy", "tb5", suc_s, su_d, key=k_suc)
    k_wtab = K("wtab"); s.dma("sy", "tb6", wtab_sb, wtab_d, key=k_wtab)

    wq_keys = {}
    for half in range(2):
        k = K("wq"); wq_keys[(0, half)] = k
        s.dma("gp", f"wq{0}{half}", wqb[0][:, half * 4096:(half + 1) * 4096],
              wqkv_img[0][:, half * 4096:(half + 1) * 4096], key=k,
              max_dma_last_dim=8192)
    b_wqb = [Buf(), Buf()]
    b_wqb[0].writer = wq_keys[(0, 1)]

    cs_keys = {}
    b_cs = [Buf(), Buf()]
    for fcg in range(2):
        k = K("cs"); cs_keys[fcg] = k
        s.dma("sy", f"cs{fcg}", cs_s[fcg], cs_d[fcg], key=k)
        b_cs[fcg].writer = k

    b_xs = [Buf(), Buf()]
    b_Tb = Buf()

    def ln_pass(tag, src_t, dst_t, src_pre, first_extra, tiles=range(8),
                bufs=None, ldq=("sy", "sy")):
        if bufs is None:
            xs_l, Tb_l, b_xs_l, b_Tb_l = xs, Tb, b_xs, b_Tb
        else:
            xs_l, Tb_l, b_xs_l, b_Tb_l = bufs
        wkeys = []
        for tc in tiles:
            sl = tc % 2
            kx = K(tag + "x")
            aft = b_xs_l[sl].wr(kx) + (src_pre(tc) or []) + \
                (first_extra if tc < min(tiles) + 2 else [])
            s.dma(ldq[sl], f"{tag}x{sl}", xs_l[sl], src_t[tc], key=kx, after=aft)
            k1 = K(); s.dv.reduce_sum(st_s1[:, tc:tc + 1], xs_l[sl], AX.X).after(
                *b_xs_l[sl].rd(k1)).inc("dv", key=k1)
            k2 = K(); r = s.dv.scalar_tensor_tensor(
                Tb_l[:, :], xs_l[sl], 1.0, xs_l[sl], ALU.mult, ALU.mult,
                accum_out=st_s2[:, tc:tc + 1])
            r.after(*(b_xs_l[sl].rd(k2) + b_Tb_l.wr(k2))).inc("dv", key=k2)
            kmu = K(); s.dv.tensor_scalar(st_mu[:, tc:tc + 1], st_s1[:, tc:tc + 1],
                                          1.0 / C, None, ALU.mult).after(k1).inc(
                "dv", key=kmu)
            kms = K(); s.dv.scalar_tensor_tensor(
                st_msq[:, tc:tc + 1], st_mu[:, tc:tc + 1], 1.0,
                st_mu[:, tc:tc + 1], ALU.mult, ALU.mult).after(kmu).inc("dv", key=kms)
            kv = K(); s.dv.scalar_tensor_tensor(
                st_var[:, tc:tc + 1], st_s2[:, tc:tc + 1], 1.0 / C,
                st_msq[:, tc:tc + 1], ALU.mult, ALU.subtract).after(
                k2, kms).inc("dv", key=kv)
            kv2 = K(); s.dv.tensor_scalar(st_var[:, tc:tc + 1], st_var[:, tc:tc + 1],
                                          EPS, None, ALU.add).after(kv).inc("dv", key=kv2)
            kq = K(); s.ac.activation(st_sq[:, tc:tc + 1], st_var[:, tc:tc + 1],
                                      AF.Sqrt).after(kv2).inc("ac", key=kq)
            krc = K(); s.dv.reciprocal(st_rstd[:, tc:tc + 1],
                                       st_sq[:, tc:tc + 1]).after(kq).inc("dv", key=krc)
            kn = K(tag + "N")
            r = s.dv.tensor_scalar(Tb_l[:, :], xs_l[sl], st_mu[:, tc:tc + 1],
                                   st_rstd[:, tc:tc + 1], ALU.subtract, ALU.mult)
            r.after(*(b_xs_l[sl].rd(kn) + b_Tb_l.wr(kn) + [krc, kmu])).inc("dv", key=kn)
            kw = K(tag + "w")
            s.dma("sy", tag + "w", dst_t[tc], Tb_l[:, :], key=kw,
                  after=b_Tb_l.rd(kw) + [kn])
            wkeys.append(kw)
        return wkeys

    ln1_w = ln_pass("L1", x_t, dln_t, lambda tc: [], [], ldq=("sy", "ac"))

    b_lnf = Buf()
    klnf = []
    for th in range(2):
        for fc in range(16):
            k = K("lnfld"); klnf.append(k)
            s.dma("ac", "lnfld", lnf(fc)[:, th * 512:(th + 1) * 512],
                  d_ln.ap()[th * 512:(th + 1) * 512, fc * 128:(fc + 1) * 128],
                  transpose=True, key=k, after=[ln1_w[th * 4 + 3]])
    b_lnf.writer = klnf[-1]
    lnf_all = klnf
    lnf_half = [klnf[0:16], klnf[16:32]]

    b_tb1 = [Buf(), Buf()]
    b_qp = [Buf(), Buf()]
    b_T1 = [Buf(), Buf()]
    for _b in b_tb1 + b_qp:
        _b.readers += b_Tb.readers + [b_Tb.writer]
    last_dqw = {0: None, 1: None}
    last_dvw = {0: None, 1: None}

    for f4 in range(12):
        wpar = f4 % 2
        if f4 + 1 < 12:
            npar = (f4 + 1) % 2
            for half in range(2):
                k = K("wq"); wq_keys[(f4 + 1, half)] = k
                aft = b_wqb[npar].wr(k) if half == 0 else []
                s.dma("gp", f"wq{npar}{half}",
                      wqb[npar][:, half * 4096:(half + 1) * 4096],
                      wqkv_img[f4 + 1][:, half * 4096:(half + 1) * 4096],
                      key=k, after=aft, max_dma_last_dim=8192)
        for fc in range(4):
            fcg = f4 * 4 + fc
            h = fcg % 16
            kind = "q" if fcg < 16 else ("k" if fcg < 32 else "v")
            kmm = {0: K("a1mm"), 1: K("a1mm")}
            kraw = {}
            pbank = {0: (fc % 2) * 2, 1: (fc % 2) * 2 + 1}
            for tt in range(2):
                for cc in range(16):
                    half = 0 if cc < 8 else 1
                    aft = [wq_keys[(f4, half)]]
                    if f4 == 0 and fc == 0 and cc == 0:
                        aft += lnf_half[tt]
                    if cc == 0:
                        aft += psb[pbank[tt]].wr(kmm[tt])
                    r = s.pe.matmul(PS[pbank[tt]][:, :],
                                    wqb[wpar][:, cc * 512 + fc * 128:cc * 512 + (fc + 1) * 128],
                                    lnf(cc)[:, tt * 512:(tt + 1) * 512],
                                    start=(cc == 0), stop=(cc == 15)).after(*aft)
                    b_lnf.readers.append(kmm[tt])
                    if cc == 15:
                        b_wqb[wpar].rd(kmm[tt])
                        r.inc("pe", key=kmm[tt])
                        last_mm = kmm[tt]
            for tt in range(2):
                kraw[tt] = K("a1raw")
                aft = [kmm[tt]] + b_tb1[tt].wr(kraw[tt]) + psb[pbank[tt]].rd(kraw[tt])
                s.ac.activation(TbA1[tt], PS[pbank[tt]][:, :], AF.Copy,
                                scale=(RD if kind == "q" else 1.0)).after(*aft).inc(
                    "ac", key=kraw[tt])
            for tt in range(2):
                if kind == "v":
                    kd = K("dvw")
                    s.dma("sy", f"dvw{tt}", dv_rows[h][:, tt * 512:(tt + 1) * 512],
                          TbA1[tt], key=kd, after=b_tb1[tt].rd(kd) + [kraw[tt]])
                    last_dvw[tt] = kd
                else:
                    krot = K("a1rot")
                    aft = [kraw[tt], k_pmat] + psb[4 + tt].wr(krot) + b_tb1[tt].rd(krot)
                    s.pe.matmul(PS[4 + tt][:, :], pmat_s, TbA1[tt],
                                start=True, stop=True).after(*aft).inc("pe", key=krot)
                    cpar = fcg % 2
                    kt1 = K("a1t1")
                    aft = [kraw[tt], cs_keys[fcg]] + b_cs[cpar].rd(kt1) + \
                        b_tb1[tt].rd(kt1) + b_T1[0].wr(kt1)
                    s.dv.tensor_tensor(T1[:, 0:512], TbA1[tt],
                                       cs_s[cpar][:, tt * 512:(tt + 1) * 512],
                                       ALU.mult).after(*aft).inc("dv", key=kt1)
                    kt2 = K("a1t2")
                    aft = [krot, cs_keys[fcg]] + b_cs[cpar].rd(kt2) + \
                        psb[4 + tt].rd(kt2) + b_T1[1].wr(kt2)
                    s.dv.tensor_tensor(T1[:, 512:1024], PS[4 + tt][:, :],
                                       cs_s[cpar][:, 1024 + tt * 512:1024 + (tt + 1) * 512],
                                       ALU.mult).after(*aft).inc("dv", key=kt2)
                    kqp = K("a1qp")
                    s.dv.tensor_tensor(Qp[tt], T1[:, 0:512], T1[:, 512:1024],
                                       ALU.add).after(kt1, kt2, *(b_qp[tt].wr(kqp) +
                                       b_T1[0].rd(kqp) + b_T1[1].rd(kqp))).inc("dv", key=kqp)
                    dst = dq_rows[h] if kind == "q" else dk_rows[h]
                    kd = K("dqw")
                    s.dma("sy", f"dqw{tt}", dst[:, tt * 512:(tt + 1) * 512],
                          Qp[tt], key=kd, after=b_qp[tt].rd(kd) + [kqp])
                    last_dqw[tt] = kd
            nxt = fcg + 2
            if 2 <= nxt < 32:
                k = K("cs"); cs_keys[nxt] = k
                s.dma("sy", f"cs{nxt % 2}", cs_s[nxt % 2], cs_d[nxt % 16], key=k,
                      after=b_cs[nxt % 2].wr(k))

    tap_ready = {"d_ln": [ln1_w[7]],
                 "d_q": [last_dqw[0], last_dqw[1]],
                 "d_k": [last_dqw[0], last_dqw[1]],
                 "d_v": [last_dvw[0], last_dvw[1]]}
    def _finish(ow_keys):
        fin_keys = list(ow_keys)
        for tname, (tdst, tsrc) in tap_outs.items():
            rdy = tap_ready.get(tname)
            if rdy is None:
                continue
            kt = K("tap")
            s.dma("sy", "tap" + tname, tdst, tsrc, key=kt,
                  after=rdy + list(ow_keys))
            fin_keys.append(kt)
        r = s.sy.nop().after(*fin_keys)
        for name, v in s.cnt.items():
            if name.startswith("D"):
                if r.op.waits.get(name, -1) < v:
                    r.op.waits[name] = v
        build_nc._last_sched = s
        sems = {}
        for name in s.cnt:
            sems[name] = nc.alloc_semaphore("s_" + name.replace(":", "_"))
        with nc.Block("main") as blk:
            def mk(engname):
                def f(e):
                    seen = {}
                    for op in s.q[engname]:
                        for (sem, v) in sorted(op.waits.items()):
                            if seen.get(sem, -1) < v:
                                e.wait_ge(sems[sem], v)
                                seen[sem] = v
                        inst = getattr(e, op.method)(*op.args, **op.kwargs)
                        for (sem, st) in op.incs:
                            inst.then_inc(sems[sem], st)
                return f
            blk.tensor(mk("pe"))
            blk.vector(mk("dv"))
            blk.scalar(mk("ac"))
            blk.sync(mk("sy"))
            blk.gpsimd(mk("gp"))
        return nc

    if not ph("a2p"):
        return _finish([])

    kbar = last_mm
    k_ghat = K("ghat")
    s.dma("gp", "ghld", ghat_s, ghat_d, key=k_ghat, max_dma_last_dim=8192)
    ghat_readers = []

    b_wpq = [Buf(), Buf()]
    wpqg = [wpq[0], wpq[1], bf(64, 2048), bf(68, 2048)]
    b_wpqg = [b_wpq[0], b_wpq[1], Buf(), Buf()]
    b_sgh8 = [Buf() for _ in range(8)]
    wgq_keys = {}
    gse8 = {}

    def gmm_round(c4):
        for qb in range(4):
            k = K("wg"); wgq_keys[(c4, qb)] = k
            s.dma("gp", f"wpq{qb}", wpqg[qb],
                  wg_img[c4][:, qb * 2048:(qb + 1) * 2048],
                  key=k, after=b_wpqg[qb].wr(k), max_dma_last_dim=8192)
        for half8 in range(2):
            kg = {}
            for cc in range(16):
                qb = cc // 4
                for tq in range(4):
                    t8 = half8 * 4 + tq
                    if cc == 0:
                        kg[tq] = K("gmm")
                    aft = [wgq_keys[(c4, qb)]]
                    if cc == 0:
                        aft += psb[tq].wr(kg[tq])
                    r = s.pe.matmul(PS[tq][:, :],
                                    lnf(cc)[:, t8 * 128:(t8 + 1) * 128],
                                    wpqg[qb][:, (cc % 4) * 512:(cc % 4 + 1) * 512],
                                    start=(cc == 0), stop=(cc == 15)).after(*aft)
                    b_lnf.readers.append(kg[tq])
                    if cc % 4 == 3 and tq == 3 and cc != 15:
                        kqr = K("gqr")
                        r.inc("pe", key=kqr)
                        b_wpqg[qb].rd(kqr)
                    if cc == 15:
                        if tq == 3:
                            b_wpqg[qb].rd(kg[tq])
                        r.inc("pe", key=kg[tq])
            for tq in range(4):
                t8 = half8 * 4 + tq
                ks = K("gse")
                aft = [kg[tq]] + b_sgh8[t8].wr(ks) + psb[tq].rd(ks)
                s.ac.activation(sgh8(t8)[:, c4 * 512:(c4 + 1) * 512],
                                PS[tq][:, :], AF.Silu).after(*aft).inc(
                    "ac", key=ks)
                gse8[(c4, t8)] = ks

    b_SP = Buf()
    ksr = K("srun0")
    s.dv.memset(SP[:, :], 0.0).inc("dv", key=ksr)
    b_SP.writer = ksr

    b_qch = [Buf(), Buf()]
    b_kch = [Buf(), Buf()]
    b_ktch = [Buf(), Buf()]
    b_vch = [Buf(), Buf()]
    b_Qhat = Buf()
    b_Zm = [Buf(), Buf()]
    b_yp = [Buf() for _ in range(NCH)]

    chunk_keys = {}

    dqk_done = [last_dqw[0], last_dqw[1]]
    dv_done = [last_dvw[0], last_dvw[1]]

    def ld_chunk(c8, parts="qktv"):
        par = c8 % 2
        kq = K("qch"); kk = K("kch"); kt = K("ktch"); kv = K("vch")
        chunk_keys[c8] = dict(q=kq, k=kk, kt=kt, v=kv)
        if "q" in parts:
            s.dma("sy", f"qch{par}", qch[par].rearrange("p (h t) -> p h t", h=16),
                  dq_col(d_q, c8), key=kq,
                  after=b_qch[par].wr(kq) + dqk_done)
        if "k" in parts:
            s.dma("sy", f"kch{par}", kch[par].rearrange("p (h t) -> p h t", h=16),
                  dq_col(d_k, c8), key=kk,
                  after=b_kch[par].wr(kk) + dqk_done)
        if "t" in parts:
            s.dma("ac", f"ktch{par}", ktch[par],
                  d_k.ap()[:, c8 * 128:(c8 + 1) * 128], transpose=True,
                  key=kt, after=b_ktch[par].wr(kt) + dqk_done)
        if "v" in parts:
            s.dma("ac", f"vch{par}", vch[par],
                  d_v.ap()[:, c8 * 128:(c8 + 1) * 128], transpose=True,
                  key=kv, after=b_vch[par].wr(kv) + dv_done)

    if stop_after == "a2p0":
        return _finish([])
    ld_chunk(0, parts="tv")
    if stop_after == "a2p1":
        return _finish([])
    if stop_after == "a2p2":
        ld_chunk(1, parts="qk")
        return _finish([])
    if stop_after == "a2p3":
        ld_chunk(1, parts="tv")
        return _finish([])
    ld_chunk(1, parts="tv")
    if stop_after == "a2p":
        return _finish([])

    sbf_of = {}
    ye_of = {}
    for c8 in range(NCH):
        par = c8 % 2
        ck = chunk_keys[c8]
        kkh = K("a2kh")
        aft = [ck["kt"], k_ghat] + b_ktch[par].rd(kkh)
        b_ktch[par].writer = kkh
        b_ktch[par].readers = []
        ghat_readers.append(kkh)
        s.gp.tensor_tensor(ktch[par], ktch[par], ghat_s, ALU.mult).after(
            *aft).inc("gp", key=kkh)
        if c8 >= 1:
            for grp in range(4):
                gs4 = slice(grp * 512, (grp + 1) * 512)
                ksb = K("a2sb")
                aft = b_SP.rd(ksb)
                if grp == 0:
                    aft += b_yp[c8].wr(ksb)
                s.ac.activation(ypart(c8)[:, gs4], SP[:, gs4], AF.Copy).after(
                    *aft).inc("ac", key=ksb)
                b_yp[c8].writer = ksb
                sbf_of[(c8, grp)] = ksb
        for grp in range(4):
            pz = (c8 * 4 + grp) % 2
            ksc = K("a2sc")
            for q in range(4):
                h = grp * 4 + q
                hs = slice(h * 128, (h + 1) * 128)
                aft = [kkh, ck["v"]] + b_ktch[par].rd(ksc) + b_vch[par].rd(ksc)
                if q == 0:
                    aft += psb[4 + pz].wr(ksc)
                r = s.pe.matmul(PS[4 + pz][:, q * 128:(q + 1) * 128],
                                ktch[par][:, hs], vch[par][:, hs],
                                start=True, stop=True).after(*aft)
                if q == 3:
                    r.inc("pe", key=ksc)
            for q in range(4):
                h = grp * 4 + q
                hs = slice(h * 128, (h + 1) * 128)
                ksu = K("a2su")
                aft = [ksc] + psb[4 + pz].rd(ksu)
                if c8 == 0 and grp == 0 and q == 0:
                    aft.append(ksr)
                if c8 >= 1:
                    aft.append(sbf_of[(c8, grp)])
                b_SP.readers.append(ksu)
                s.dv.scalar_tensor_tensor(SP[:, hs], SP[:, hs], g128[h],
                                          PS[4 + pz][:, q * 128:(q + 1) * 128],
                                          ALU.mult, ALU.add).after(*aft).inc(
                    "dv", key=ksu)
                if h == 15:
                    b_SP.writer = ksu
                    b_SP.readers = []
        if c8 + 2 < NCH:
            ld_chunk(c8 + 2, parts="tv")
        if c8 < 4:
            gmm_round(c8)
        if stop_after == "a2c0":
            break

    kccw = K("ccw")
    s.dma("sy", "ccw", cc_in.ap().rearrange("(h d) e -> d h e", d=128),
          SP[:, :].rearrange("p (h e) -> p h e", h=16), key=kccw,
          after=b_SP.rd(kccw))
    tap_ready["cc_in"] = [kccw]
    kcc = K("cc")
    s.gp.collective_compute(
        "AllGather", ALU.bypass, replica_groups=[list(range(NCORES))],
        ins=[cc_in.ap().opt()], outs=[cc_out.ap().opt()]).after(kccw).inc(
        "cc", key=kcc)

    b_gt = [Buf() for _ in range(4)]
    b_gt[0].readers += b_xs[0].readers + \
        b_wpqg[2].readers + b_wpqg[3].readers + \
        [b_wpqg[2].writer, b_wpqg[3].writer]
    b_gt[1].readers += b_xs[1].readers
    b_gt[2].readers += b_cs[0].readers
    b_gt[3].readers += b_cs[1].readers
    gld_keys = {}
    pcm = None
    cc_out_r = cc_out.ap().rearrange("(b h d) e -> b d h e", d=128, h=16)
    for c8 in range(NCH):
        par = c8 % 2
        kq = K("qch"); kk = K("kch"); kv = K("vch2")
        aftq = b_qch[par].wr(kq) + dqk_done
        if c8 == 0:
            aftq += ghat_readers
        s.dma("sy", f"qch{par}", qch[par].rearrange("p (h t) -> p h t", h=16),
              dq_col(d_q, c8), key=kq, after=aftq)
        s.dma("ac", f"kch{par}", kch[par].rearrange("p (h t) -> p h t", h=16),
              dq_col(d_k, c8), key=kk,
              after=b_kch[par].wr(kk) + dqk_done)
        s.dma("ac", f"vch{par}", vch[par],
              d_v.ap()[:, c8 * 128:(c8 + 1) * 128], transpose=True,
              key=kv, after=b_vch[par].wr(kv))
        kqh = K("a2qh")
        aft = [kq, k_qdecb] + b_Qhat.wr(kqh) + b_qch[par].rd(kqh)
        s.dv.tensor_tensor(Qhat, qch[par], qdecb_s, ALU.mult).after(*aft).inc(
            "dv", key=kqh)
        for grp in range(4):
            pz = (c8 * 4 + grp) % 2
            gs4 = slice(grp * 512, (grp + 1) * 512)
            kz = K("a2z")
            for q in range(4):
                h = grp * 4 + q
                hs = slice(h * 128, (h + 1) * 128)
                aft = [kk, kq] + b_kch[par].rd(kz) + b_qch[par].rd(kz)
                if q == 0:
                    aft += psb[pz].wr(kz)
                r = s.pe.matmul(PS[pz][:, q * 128:(q + 1) * 128],
                                kch[par][:, hs], qch[par][:, hs],
                                start=True, stop=True).after(*aft)
                if q == 3:
                    r.inc("pe", key=kz)
            kzm = K("a2zm")
            aft = [kz, k_maskT] + b_Zm[pz].wr(kzm) + psb[pz].rd(kzm)
            if c8 == 0 and grp < 2:
                aft += b_ktch[0].readers + [b_ktch[0].writer]
            s.dv.tensor_tensor(ZmS[pz], PS[pz][:, 0:512], maskT_s[:, gs4],
                               ALU.mult).after(*aft).inc("dv", key=kzm)
            kyg = K("a2y2")
            for q in range(4):
                h = grp * 4 + q
                hs = slice(h * 128, (h + 1) * 128)
                qsl = slice(q * 128, (q + 1) * 128)
                aft = [kzm, kv] + b_Zm[pz].rd(kyg) + b_vch[par].rd(kyg)
                if q == 0:
                    aft += psb[2 + pz].wr(kyg)
                    b_yp[c8].readers.append(kyg)
                r = s.pe.matmul(PS[2 + pz][:, qsl], ZmS[pz][:, qsl],
                                vch[par][:, hs], start=True,
                                stop=(c8 == 0)).after(*aft)
                if c8 == 0 and q == 3:
                    r.inc("pe", key=kyg)
                if c8 >= 1:
                    aft2 = [kqh, sbf_of[(c8, grp)]] + b_Qhat.rd(kyg)
                    r = s.pe.matmul(PS[2 + pz][:, qsl], Qhat[:, hs],
                                    ypart(c8)[:, hs], start=False,
                                    stop=True).after(*aft2)
                    if q == 3:
                        r.inc("pe", key=kyg)
            kye = K("a2ye")
            aft = [kyg] + psb[2 + pz].rd(kye)
            if grp == 0:
                aft += b_yp[c8].wr(kye)
            s.ac.activation(ypart(c8)[:, gs4], PS[2 + pz][:, 0:512],
                            AF.Copy).after(*aft).inc("ac", key=kye)
            b_yp[c8].writer = kye
            ye_of[(c8, grp)] = kye
        if c8 == 1:
            for b in range(4):
                kg_ = K("gld"); gld_keys[b] = kg_
                aft = [kcc] + b_gt[b % 4].wr(kg_)
                s.dma("gp", "gld",
                      gtmp[b % 4].rearrange("p (h e) -> p h e", h=16),
                      cc_out_r[b], key=kg_, after=aft)
        if c8 == 6:
            for b in range(NCORES):
                if b >= 4:
                    kg_ = K("gld"); gld_keys[b] = kg_
                    aft = [kcc] + b_gt[b % 4].wr(kg_)
                    s.dma("gp", "gld",
                          gtmp[b % 4].rearrange("p (h e) -> p h e", h=16),
                          cc_out_r[b], key=kg_, after=aft)
                kp = K("pcm")
                for h in range(H):
                    hs = slice(h * 128, (h + 1) * 128)
                    wsl = wtab_sb[:, b * H + h:b * H + h + 1]
                    aft2 = [gld_keys[b], k_wtab]
                    if b == 0 and h == 0:
                        aft2 += [kccw] + b_SP.wr(kp)
                    if b == 0:
                        r = s.dv.tensor_scalar(SP[:, hs], gtmp[0][:, hs],
                                               wsl, None, ALU.mult).after(*aft2)
                    else:
                        r = s.dv.scalar_tensor_tensor(
                            SP[:, hs], gtmp[b % 4][:, hs], wsl, SP[:, hs],
                            ALU.mult, ALU.add).after(*aft2)
                    if h == H - 1:
                        r.inc("dv", key=kp)
                        b_gt[b % 4].rd(kp)
                        b_SP.writer = kp
                        b_SP.readers = []
                pcm = kp
        if stop_after == "a2c0":
            break

    if not ph("p5"):
        return _finish([])

    b_yn = Buf()
    kgnw = K("gnw")
    s.dma("gp", "gnw", gnw_s, gnw_d, key=kgnw,
          after=[pcm] + list(b_lnf.readers))
    kgnb = K("gnb")
    s.dma("gp", "gnb", gnb_s, gnb_d, key=kgnb, after=[pcm])
    b_gnw = Buf(); b_gnw.writer = kgnw
    b_gnb = Buf(); b_gnb.writer = kgnb
    if not ph("post"):
        return _finish([])

    b_gfh = Buf()
    b_xs2 = [Buf(), Buf()]
    b_xs2[0].readers += b_gt[0].readers + [b_gt[0].writer]
    b_xs2[1].readers += b_gt[1].readers + [b_gt[1].writer]
    b_Tb2 = Buf()
    b_Tb2.readers += b_gt[2].readers + [b_gt[2].writer]
    ln2_w = []

    def l2_pre2(tc8):
        hf_, tc_ = divmod(tc8, 4)
        return [x2w_keys[(hf_, c4_, tc_)] for c4_ in range(4)]
    b_PT = Buf()
    b_pdec = Buf()
    b_x0 = [Buf(), Buf()]
    b_x2o = [Buf(), Buf()]
    lnf_r = list(b_lnf.readers)
    for bb in (b_x0[0], b_x0[1], b_x2o[0], b_x2o[1], b_yn):
        bb.readers += lnf_r
    b_stats = Buf()
    x2w_keys = {}

    grow_of = {}

    def post_sweeps(hf):
        for tc in range(4):
            c8g = hf * 4 + tc
            par = tc % 2
            kq = K("qch2")
            s.dma("sy", f"qch{par}", qch[par].rearrange("p (h t) -> p h t", h=16),
                  dq_col(d_q, c8g), key=kq, after=b_qch[par].wr(kq))
            kp = K("pdec")
            aft = b_pdec.wr(kp)
            if hf == 0 and tc == 0:
                aft += b_kch[0].readers + b_kch[1].readers + \
                    [b_kch[0].writer, b_kch[1].writer]
            s.dma("sy", "pdec", pdec_s, pdec_d[c8g], key=kp, after=aft)
            kqh = K("qh2")
            aft = [kq, k_qdecb] + b_Qhat.wr(kqh) + b_qch[par].rd(kqh)
            s.dv.tensor_tensor(Qhat, qch[par], qdecb_s, ALU.mult).after(
                *aft).inc("dv", key=kqh)
            kpt = K("ptl")
            aft = [kp, pcm] + b_PT.wr(kpt) + b_pdec.rd(kpt) + b_SP.rd(kpt)
            s.dv.tensor_tensor(PT, SP[:, :], pdec_s, ALU.mult).after(*aft).inc(
                "dv", key=kpt)
            for h in range(16):
                hs = slice(h * 128, (h + 1) * 128)
                kmm = K("pth")
                aft = [kqh, kpt] + psb[6 + h % 2].wr(kmm) + \
                    b_Qhat.rd(kmm) + b_PT.rd(kmm)
                s.pe.matmul(PS[6 + h % 2][:, 0:128], Qhat[:, hs], PT[:, hs],
                            start=True, stop=True).after(*aft).inc("pe", key=kmm)
                ky = K("ysl")
                aft = [kmm, ye_of[(c8g, h // 4)]] + psb[6 + h % 2].rd(ky)
                b_yp[c8g].readers.append(ky)
                s.dv.tensor_tensor(ypart(c8g)[:, hs], ypart(c8g)[:, hs],
                                   PS[6 + h % 2][:, 0:128], ALU.add).after(
                    *aft).inc("dv", key=ky)
                b_yp[c8g].writer = ky
                j = tc * 16 + h
                krd = K()
                s.dv.reduce_sum(st_s1[:, j:j + 1], ypart(c8g)[:, hs],
                                AX.X).after(ky).inc("dv", key=krd)
                kst = K("yst")
                r = s.dv.scalar_tensor_tensor(
                    Tb[:, 0:128], ypart(c8g)[:, hs], 1.0, ypart(c8g)[:, hs],
                    ALU.mult, ALU.mult, accum_out=st_s2[:, j:j + 1])
                r.after(*(b_Tb.wr(kst) + [ky])).inc("dv", key=kst)
                last_sweep = (krd, kst)
        kmu_b = K()
        s.dv.tensor_scalar(st_mu[:, :], st_s1[:, :], 1.0 / D, None,
                           ALU.mult).after(*last_sweep).inc("dv", key=kmu_b)
        kms_b = K()
        s.dv.scalar_tensor_tensor(st_msq[:, :], st_mu[:, :], 1.0, st_mu[:, :],
                                  ALU.mult, ALU.mult).after(kmu_b).inc("dv", key=kms_b)
        kv = K("gnv")
        s.dv.scalar_tensor_tensor(st_var[:, :], st_s2[:, :], 1.0 / D,
                                  st_msq[:, :], ALU.mult, ALU.subtract).after(
            kms_b).inc("dv", key=kv)
        kv2 = K()
        s.dv.tensor_scalar(st_var[:, :], st_var[:, :], EPS, None,
                           ALU.add).after(kv).inc("dv", key=kv2)
        kq_ = K("gnq")
        s.ac.activation(st_sq[:, :], st_var[:, :], AF.Sqrt).after(
            kv2).inc("ac", key=kq_)
        kr = K("gnr")
        s.dv.reciprocal(st_rstd[:, :], st_sq[:, :]).after(kq_).inc("dv", key=kr)
        grow_keys = []
        for tc in range(4):
            c8g = hf * 4 + tc
            for h in range(16):
                hs = slice(h * 128, (h + 1) * 128)
                j = tc * 16 + h
                kn = K()
                aft = [kr, kmu_b] + (b_yn.wr(kn) if h == 0 else [])
                b_yp[c8g].readers.append(kn)
                s.dv.tensor_scalar(yn_s[:, hs], ypart(c8g)[:, hs],
                                   st_mu[:, j:j + 1], st_rstd[:, j:j + 1],
                                   ALU.subtract, ALU.mult).after(*aft).inc("dv", key=kn)
                kn_last = kn
            ka = K()
            s.dv.tensor_tensor(yn_s, yn_s, gnw_s, ALU.mult).after(
                kgnw, kn_last).inc("dv", key=ka)
            b_gnw.readers.append(ka)
            kb = K()
            s.dv.tensor_tensor(yn_s, yn_s, gnb_s, ALU.add).after(
                kgnb, ka).inc("dv", key=kb)
            b_gnb.readers.append(kb)
            kgr = K("grow")
            aft = [gse8[(3, c8g)], kb] + b_Tb.wr(kgr) + b_sgh8[c8g].rd(kgr)
            s.dv.tensor_tensor(Tb[:, :], yn_s, sgh8(c8g), ALU.mult).after(
                *aft).inc("dv", key=kgr)
            b_yn.readers.append(kgr)
            kw = K("dgw")
            s.dma("sy", "dgw", dg_t[c8g], Tb[:, :], key=kw,
                  after=b_Tb.rd(kw) + [kgr])
            grow_keys.append(kw)
        grow_of[hf] = grow_keys

    def post_proj(hf):
        grow_keys = grow_of[hf]
        kgf = []
        for fc in range(16):
            k = K("gfh"); kgf.append(k)
            aft = [grow_keys[3]]
            if fc == 0:
                aft += b_gfh.wr(k)
                if hf == 0:
                    aft += b_ktch[0].readers + b_ktch[1].readers + \
                        b_vch[0].readers + b_vch[1].readers + \
                        b_Zm[0].readers + b_Zm[1].readers + \
                        [b_ktch[0].writer, b_ktch[1].writer,
                         b_vch[0].writer, b_vch[1].writer,
                         b_Zm[0].writer, b_Zm[1].writer]
            s.dma("ac", "gfh", gfh(fc),
                  d_g.ap()[hf * 512:(hf + 1) * 512, fc * 128:(fc + 1) * 128],
                  transpose=True, key=k, after=aft)
        b_gfh.writer = kgf[-1]
        gfh_all = kgf
        for c4 in range(4):
            for qb in range(2):
                k = K("wp"); wgq_keys[("p", hf, c4, qb)] = k
                s.dma("gp", f"wpc{qb % 2}", wpq[qb % 2],
                      wp_img[c4][:, qb * 2048:(qb + 1) * 2048],
                      key=k, after=b_wpq[qb % 2].wr(k), max_dma_last_dim=8192)
            kpj = {}
            for tc in range(2):
                sl = tc % 2
                kx = K("x0")
                s.dma("sy", f"x0{sl}", x0s[sl],
                      bass.AP(x.tensor, (hf * 512 + tc * 128) * C + c4 * 512,
                              [[C, 128], [1, 512]]),
                      key=kx, after=b_x0[sl].wr(kx))
                wgq_keys[("x0", hf, c4, tc)] = kx
            for cc in range(16):
                qb = cc // 4
                for tc in range(4):
                    if cc == 0:
                        kpj[tc] = K("pjmm")
                    aft = [wgq_keys[("p", hf, c4, qb)]]
                    if c4 == 0 and cc == 0:
                        aft += gfh_all
                    if cc == 0:
                        aft += psb[tc].wr(kpj[tc])
                    r = s.pe.matmul(PS[tc][:, :],
                                    gfh(cc)[:, tc * 128:(tc + 1) * 128],
                                    wpq[qb % 2][:, (cc % 4) * 512:(cc % 4 + 1) * 512],
                                    start=(cc == 0), stop=(cc == 15)).after(*aft)
                    b_gfh.readers.append(kpj[tc])
                    if cc % 4 == 3 and tc == 3:
                        if cc == 15:
                            b_wpq[qb % 2].rd(kpj[tc])
                        else:
                            kqr = K("pqr")
                            r.inc("pe", key=kqr)
                            b_wpq[qb % 2].rd(kqr)
                            nqb = qb + 2
                            if nqb < 4:
                                k2 = K("wp"); wgq_keys[("p", hf, c4, nqb)] = k2
                                s.dma("gp", f"wpc{nqb % 2}", wpq[nqb % 2],
                                      wp_img[c4][:, nqb * 2048:(nqb + 1) * 2048],
                                      key=k2, after=b_wpq[nqb % 2].wr(k2),
                                      max_dma_last_dim=8192)
                    if cc == 15:
                        r.inc("pe", key=kpj[tc])
            for tc in range(4):
                sl = tc % 2
                ke = K("x2e")
                aft = [kpj[tc], wgq_keys[("x0", hf, c4, tc)]] + \
                    b_x2o[sl].wr(ke) + psb[tc].rd(ke) + b_x0[sl].rd(ke)
                s.dv.tensor_tensor(x2o[sl], PS[tc][:, :], x0s[sl],
                                   ALU.add).after(*aft).inc("dv", key=ke)
                kw = K("x2w")
                s.dma("sy", f"x2w{sl}",
                      bass.AP(d_x2.ap().tensor, (hf * 512 + tc * 128) * C + c4 * 512,
                              [[C, 128], [1, 512]]),
                      x2o[sl], key=kw, after=b_x2o[sl].rd(kw) + [ke])
                x2w_keys[(hf, c4, tc)] = kw
                if tc + 2 < 4:
                    kx = K("x0")
                    s.dma("sy", f"x0{sl}", x0s[sl],
                          bass.AP(x.tensor,
                                  (hf * 512 + (tc + 2) * 128) * C + c4 * 512,
                                  [[C, 128], [1, 512]]),
                          key=kx, after=b_x0[sl].wr(kx))
                    wgq_keys[("x0", hf, c4, tc + 2)] = kx
    post_sweeps(0)
    post_proj(0)
    ln2_w += ln_pass("L2", dx2_t, dln2_t, l2_pre2, [pcm],
                     tiles=range(0, 4), bufs=(xs2, Tb2, b_xs2, b_Tb2))
    post_sweeps(1)
    post_proj(1)
    ln2_w += ln_pass("L2", dx2_t, dln2_t, l2_pre2, [],
                     tiles=range(4, 8), bufs=(xs2, Tb2, b_xs2, b_Tb2))

    tap_ready["d_g"] = [x2w_keys[(1, 3, 3)]]
    tap_ready["d_x2"] = [x2w_keys[(hf_, c4_, tc_)]
                        for hf_ in range(2) for c4_ in range(4) for tc_ in range(4)]
    if not ph("post") or not ph("ffn"):
        return _finish([])

    l2_pre = l2_pre2
    tap_ready["d_ln2"] = [ln2_w[7]]

    b_ln2f = Buf()
    kl2 = []
    for fc in range(16):
        k = K("ln2ld"); kl2.append(k)
        aft = [ln2_w[7]]
        if fc == 0:
            for bb in (b_gnw, b_gnb, b_yn, b_x0[0], b_x0[1],
                       b_x2o[0], b_x2o[1]):
                aft += bb.readers + ([bb.writer] if bb.writer else [])
            aft += b_ln2f.wr(k)
        s.dma("ac", "ln2ld", ln2f(fc), d_ln2.ap()[:, fc * 128:(fc + 1) * 128],
              transpose=True, key=k, after=aft)
    b_ln2f.writer = kl2[-1]
    ln2_all = kl2

    ksd = K("sdl")
    s.dma("sy", "sdl", sdown_s, sdown_d, key=ksd,
          after=b_SP.readers + [b_SP.writer])
    b_sdown = Buf(); b_sdown.writer = ksd

    b_wga = [Buf(), Buf()]
    b_wua = [Buf(), Buf()]
    b_wdn = [Buf(), Buf()]
    b_fuse = [Buf() for _ in range(16)]
    b_gs = [Buf(), Buf()]
    b_acc = [Buf() for _ in range(8)]
    for oc in range(2):
        for t in range(4):
            b_wga[oc].readers += b_yp[t].readers + [b_yp[t].writer]
            b_wua[oc].readers += b_yp[t].readers + [b_yp[t].writer]
    for t in range(4, 8):
        b_wdn[0].readers += b_yp[t].readers + [b_yp[t].writer]
    b_wdn[1].readers += b_Qhat.readers + [b_Qhat.writer] + \
        b_wpq[0].readers + b_wpq[1].readers + \
        [b_wpq[0].writer, b_wpq[1].writer]
    for tt in range(2):
        b_gs[tt].readers += b_Tb.readers + [b_Tb.writer]
    fuse_prime = b_qch[0].readers + b_qch[1].readers + \
        [b_qch[0].writer, b_qch[1].writer] + b_pdec.readers + \
        [b_pdec.writer] + b_gfh.readers + [b_gfh.writer]
    for oc in range(16):
        b_fuse[oc].readers += fuse_prime
    acc_prime = b_xs[0].readers + b_xs[1].readers + \
        b_xs2[0].readers + b_xs2[1].readers + b_Tb2.readers + [b_Tb2.writer] + \
        b_cs[0].readers + b_cs[1].readers
    for bb in b_gt + b_sgh8:
        acc_prime += bb.readers + ([bb.writer] if bb.writer else [])
    for t8 in range(8):
        b_acc[t8].readers += acc_prime

    b_x2f = [Buf(), Buf()]
    b_x2f[0].readers += b_PT.readers + [b_PT.writer] + \
        b_Qhat.readers + [b_Qhat.writer]
    b_x2f[1].readers += b_Tb.readers + [b_Tb.writer] + \
        b_gs[0].readers + b_gs[1].readers + \
        b_T1[0].readers + b_T1[1].readers + \
        b_tb1[0].readers + b_tb1[1].readers + \
        b_qp[0].readers + b_qp[1].readers
    ow_keys = []

    def emit_final(t8, eng):
        par = t8 % 2
        kx = K("x2f")
        aft = l2_pre(t8) + b_x2f[par].wr(kx)
        if par == 1:
            aft += [ffu_keys[(63, 0)], ffu_keys[(63, 1)]]
        s.dma("ac", f"x2f{par}", x2f[par], dx2_t[t8], key=kx, after=aft)
        km = K("finm")
        aft = [fde_keys[(3, ht_, t8 // 4, t8 % 4)] for ht_ in range(4)] + \
            [ksd] + b_acc[t8].wr(km) + b_sdown.rd(km)
        getattr(s, eng).tensor_tensor(acc(t8), acc(t8), sdown_s,
                                      ALU.mult).after(*aft).inc(eng, key=km)
        kf = K("fin")
        getattr(s, eng).tensor_tensor(acc(t8), acc(t8), x2f[par],
                                      ALU.add).after(
            kx, km, *(b_acc[t8].wr(kf) + b_x2f[par].rd(kf))).inc(eng, key=kf)
        ko = K("ow")
        s.dma("sy" if par == 0 else "ac", f"ow{par}", out_t[t8], acc(t8),
              key=ko, after=b_acc[t8].rd(ko) + [kf])
        ow_keys.append(ko)

    fgs_keys = {}
    ffu_keys = {}
    fdm_keys = {}
    fde_keys = {}
    for og in range(4):
        for oc in range(16):
            ocg = og * 16 + oc
            kwg = K("wga")
            s.dma("gp", f"wga{oc % 2}", wga[oc % 2], wgate_img[ocg], key=kwg,
                  after=b_wga[oc % 2].wr(kwg), max_dma_last_dim=2048)
            kwu = K("wua")
            s.dma("gp", f"wua{oc % 2}", wua[oc % 2], wup_img[ocg], key=kwu,
                  after=b_wua[oc % 2].wr(kwu), max_dma_last_dim=2048)
            kgm = {0: K("fgm"), 1: K("fgm")}
            kum = {0: K("fum"), 1: K("fum")}
            for cc in range(16):
                for tt in range(2):
                    aft = [kwg]
                    if ocg == 0 and tt == 0 and cc == 0:
                        aft += ln2_all
                    if cc == 0:
                        aft += psb[tt].wr(kgm[tt])
                    r = s.pe.matmul(PS[tt][:, :],
                                    wga[oc % 2][:, cc * 128:(cc + 1) * 128],
                                    ln2f(cc)[:, tt * 512:(tt + 1) * 512],
                                    start=(cc == 0), stop=(cc == 15)).after(*aft)
                    b_ln2f.readers.append(kgm[tt])
                    if cc == 15:
                        b_wga[oc % 2].rd(kgm[tt])
                        r.inc("pe", key=kgm[tt])
            for cc in range(16):
                for tt in range(2):
                    aft = [kwu]
                    if cc == 0:
                        aft += psb[2 + tt].wr(kum[tt])
                    r = s.pe.matmul(PS[2 + tt][:, :],
                                    wua[oc % 2][:, cc * 128:(cc + 1) * 128],
                                    ln2f(cc)[:, tt * 512:(tt + 1) * 512],
                                    start=(cc == 0), stop=(cc == 15)).after(*aft)
                    b_ln2f.readers.append(kum[tt])
                    if cc == 15:
                        b_wua[oc % 2].rd(kum[tt])
                        r.inc("pe", key=kum[tt])
            for tt in range(2):
                kgs = K("fgs")
                aft = [kgm[tt], k_sgc] + b_gs[tt].wr(kgs) + psb[tt].rd(kgs)
                s.ac.activation(gss[tt], PS[tt][:, :], AF.Silu,
                                scale=sgc_s[:, ocg:ocg + 1]).after(*aft).inc(
                    "ac", key=kgs)
                fgs_keys[(ocg, tt)] = kgs
                kfu = K("ffu")
                aft = [kum[tt], kgs, k_suc]
                if tt == 0:
                    aft += b_fuse[oc].wr(kfu)
                aft += psb[2 + tt].rd(kfu) + b_gs[tt].rd(kfu)
                s.dv.scalar_tensor_tensor(
                    fuse(oc)[:, tt * 512:(tt + 1) * 512], PS[2 + tt][:, :],
                    suc_s[:, ocg:ocg + 1], gss[tt], ALU.mult, ALU.mult).after(
                    *aft).inc("dv", key=kfu)
                ffu_keys[(ocg, tt)] = kfu
        if og < 3:
            dseq = [(ht, hf2, ht % 2, hf2 == 0)
                    for ht in range(4) for hf2 in range(2)]
        else:
            dseq = [(ht, hf2, (hf2 * 4 + ht) % 2, True)
                    for hf2 in range(2) for ht in range(4)]
        for (ht, hf2, dpar, fresh) in dseq:
            if fresh:
                kwd = K("wdn")
                s.dma("gp", f"wdn{dpar}", wdns[dpar], wdown_img[og * 4 + ht],
                      key=kwd, after=b_wdn[dpar].wr(kwd), max_dma_last_dim=2048)
            if True:
                kdm = {}
                for tc in range(4):
                    kdm[tc] = K("fdm")
                for oc in range(16):
                    for tc in range(4):
                        aft = [kwd, ffu_keys[(og * 16 + oc, hf2)]]
                        if oc == 0:
                            aft += psb[4 + tc].wr(kdm[tc])
                        r = s.pe.matmul(
                            PS[4 + tc][:, :],
                            fuse(oc)[:, hf2 * 512 + tc * 128:hf2 * 512 + (tc + 1) * 128],
                            wdns[dpar][:, oc * 512:(oc + 1) * 512],
                            start=(oc == 0), stop=(oc == 15)).after(*aft)
                        b_fuse[oc].readers.append(kdm[tc])
                        if oc == 15:
                            b_wdn[dpar].rd(kdm[tc])
                            r.inc("pe", key=kdm[tc])
                for tc in range(4):
                    t8 = hf2 * 4 + tc
                    accsl = acc(t8)[:, ht * 512:(ht + 1) * 512]
                    ke = K("fde")
                    aft = [kdm[tc]] + psb[4 + tc].rd(ke)
                    if og == 0:
                        aft += b_acc[t8].wr(ke)
                        r = s.dv.tensor_copy(accsl, PS[4 + tc][:, :]).after(*aft)
                    else:
                        b_acc[t8].readers.append(ke)
                        r = s.dv.tensor_tensor(accsl, accsl, PS[4 + tc][:, :],
                                               ALU.add).after(*aft)
                        b_acc[t8].writer = ke
                        b_acc[t8].readers = []
                    r.inc("dv", key=ke)
                    fde_keys[(og, ht, hf2, tc)] = ke
                if og == 3 and ht == 3:
                    for tq8 in range(hf2 * 4, hf2 * 4 + 4):
                        emit_final(tq8, "dv")

    return _finish(ow_keys)


def kernel(L0_attn_x, L0_attn_w_qkv, L0_attn_w_g, L0_attn_w_proj,
           L0_attn_gn_weight, L0_attn_gn_bias, L0_ffn_w_gate, L0_ffn_w_up,
           L0_ffn_w_down, L0_ffn_s_gate, L0_ffn_s_up, L0_ffn_s_down):
    x_full = np.ascontiguousarray(np.asarray(L0_attn_x, np.float32).reshape(T, C))
    sh = _host_shared(L0_attn_w_qkv, L0_attn_w_g, L0_attn_w_proj,
                      L0_ffn_w_gate, L0_ffn_w_up, L0_ffn_w_down,
                      L0_attn_gn_weight, L0_attn_gn_bias,
                      L0_ffn_s_gate, L0_ffn_s_up, L0_ffn_s_down)
    cos, sin = _host_xpos()

    nc = build_nc(taps=tuple(_DEBUG_TAPS))

    in_maps = []
    for core in range(NCORES):
        pc = _host_percore(core, x_full, cos, sin)
        m = dict(x=pc["x"], cs_img=pc["cs_img"], wtab=pc["wtab"],
                 wqkv_img=sh["wqkv_img"], wg_img=sh["wg_img"], wp_img=sh["wp_img"],
                 wgate_img=sh["wgate_img"], wup_img=sh["wup_img"],
                 wdown_img=sh["wdown_img"], maskT=sh["maskT"], qdecb=sh["qdecb"],
                 ghat=sh["ghat"], pdec=sh["pdec"], pmat=sh["pmat"],
                 gnw_b=sh["gnw_b"], gnb_b=sh["gnb_b"], sdown_b=sh["sdown_b"],
                 sg_c=sh["sg_c"], su_c=sh["su_c"])
        in_maps.append(m)

    res = run_bass_kernel_spmd(nc, in_maps, core_ids=list(range(NCORES)))
    outs = [res.results[c]["out"] for c in range(NCORES)]
    full = np.concatenate(outs, axis=0).reshape(1, T, C).astype(np.float32)
    if _DEBUG_TAPS:
        kernel._tap_results = [
            {k: v for k, v in res.results[c].items() if k.startswith("tap_")}
            for c in range(NCORES)]
    return full
